# version 07
# baseline (speedup 1.0000x reference)
import numpy as np
import concourse.bass as bass
import concourse.mybir as mybir
from concourse.bass_utils import run_bass_kernel_spmd

F32 = mybir.dt.float32
BF16 = mybir.dt.bfloat16
AF = mybir.ActivationFunctionType
ALU = mybir.AluOpType

S = 2048
D = 1024
NCH = D // 128
H = 16
HD = 64
INTER = 4096
NI = INTER // 128
NTK = S // 128
RANK = 16
TQB = 256
NFB = S // TQB

ENGINES = ("tensor", "vector", "scalar", "gpsimd", "sync")


class Sem:
    __slots__ = ("h", "n", "name")

    def __init__(self, h, name):
        self.h = h
        self.n = 0
        self.name = name


class Prog:
    def __init__(self, nc):
        self.nc = nc
        self.recs = {e: [] for e in ENGINES}
        self._pend = {e: [] for e in ENGINES}
        self.sems = []

    def sem(self, name, es):
        s = Sem(es.enter_context(self.nc.semaphore(name)), name)
        self.sems.append(s)
        return s

    def wait(self, eng, sem, val=None):
        v = sem.n if val is None else val
        if v > 0:
            self._pend[eng].append((sem, v))

    def op(self, eng, meth, *args, inc=None, inc16=None, **kw):
        waits = self._pend[eng]
        self._pend[eng] = []
        incs = []
        if inc is not None:
            for s in (inc if isinstance(inc, (list, tuple)) else [inc]):
                s.n += 1
                incs.append((s, 1))
        if inc16 is not None:
            inc16.n += 16
            incs.append((inc16, 16))
        self.recs[eng].append((waits, meth, args, kw, incs))

    def dma(self, eng, out, in_, sem, **kw):
        self.op(eng, "dma_start", inc16=sem, out=out, in_=in_, **kw)

    def mm(self, out, lhsT, rhs, start=True, stop=True, inc=None, tp=None):
        self.op("tensor", "matmul", out, lhsT, rhs, start=start, stop=stop,
                skip_group_check=True, tile_position=tp, inc=inc)

    def barrier(self, sems):
        for e in ENGINES:
            for s in sems:
                self.wait(e, s)

    def simulate(self):
        pos = {e: 0 for e in ENGINES}
        val = {s.name: 0 for s in self.sems}
        total = sum(len(r) for r in self.recs.values())
        done = 0
        while done < total:
            progressed = False
            for e in ENGINES:
                while pos[e] < len(self.recs[e]):
                    waits, meth, args, kw, incs = self.recs[e][pos[e]]
                    if any(val[s.name] < v for s, v in waits):
                        break
                    for s, k in incs:
                        val[s.name] += k
                    pos[e] += 1
                    done += 1
                    progressed = True
            if not progressed:
                lines = []
                for e in ENGINES:
                    if pos[e] < len(self.recs[e]):
                        waits, meth, _, _, _ = self.recs[e][pos[e]]
                        ws = [(s.name, v, val[s.name]) for s, v in waits
                              if val[s.name] < v]
                        lines.append(f"{e}@{pos[e]}: {meth} blocked on {ws}")
                raise RuntimeError("DEADLOCK:\n" + "\n".join(lines))

    def replay(self, block):
        for e in ENGINES:
            recs = self.recs[e]

            def make(engine_recs):
                def emit(eng):
                    for waits, meth, args, kw, incs in engine_recs:
                        ww = {}
                        for s, v in waits:
                            if s.name in ww:
                                ww[s.name] = (s, max(v, ww[s.name][1]))
                            else:
                                ww[s.name] = (s, v)
                        for s, v in ww.values():
                            eng.wait_ge(s.h, v)
                        inst = getattr(eng, meth)(*args, **kw)
                        for s, k in incs:
                            inst = inst.then_inc(s.h, k)
                return emit
            getattr(block, e)(make(recs))


class Arena:
    def __init__(self, nc, nbytes=212608):
        self.t = nc.alloc_sbuf_tensor("arena", [128, nbytes], mybir.dt.uint8)
        self.nbytes = nbytes
        self.off = 0
        self.peak = 0

    def take(self, shape, dtype, name=""):
        esz = 4 if dtype == F32 else 2
        free = int(np.prod(shape[1:]))
        nb = free * esz
        off = (self.off + 31) // 32 * 32
        assert off + nb <= self.nbytes, \
            f"SBUF arena overflow at {name}: need {off + nb}, have {self.nbytes}"
        self.off = off + nb
        self.peak = max(self.peak, self.off)
        v = self.t.ap()[:, off:off + nb].bitcast(dtype)
        if len(shape) > 2:
            dims = " ".join(f"d{i}" for i in range(1, len(shape)))
            v = v.rearrange(f"p ({dims}) -> p {dims}",
                            **{f"d{i}": int(shape[i]) for i in range(1, len(shape))})
        if shape[0] != 128:
            v = v[0:shape[0]]
        return v

    def mark(self):
        return self.off

    def reset(self, mark):
        self.off = mark



def _evens_odds_perm():
    perm = np.zeros(D, dtype=np.int64)
    for h in range(H):
        perm[h * 64:h * 64 + 32] = h * 64 + np.arange(0, 64, 2)
        perm[h * 64 + 32:h * 64 + 64] = h * 64 + np.arange(1, 64, 2)
    return perm


def host_prep(inputs):
    g = {}
    T = lambda a: np.ascontiguousarray(np.asarray(a, dtype=np.float32).T)
    C = lambda a: np.ascontiguousarray(np.asarray(a, dtype=np.float32))
    g["wq"] = T(inputs["L0_attn_Wq"])
    g["wk"] = T(inputs["L0_attn_Wk"])
    g["wv"] = T(inputs["L0_attn_Wv"])
    g["w1"] = C(inputs["L0_ffn_W1"])
    g["w2"] = C(inputs["L0_ffn_W2"])
    g["b1c"] = np.ascontiguousarray(
        np.asarray(inputs["L0_ffn_b1"], np.float32).reshape(NI, 128).T)
    g["b2c"] = np.ascontiguousarray(
        np.asarray(inputs["L0_ffn_b2"], np.float32).reshape(NCH, 128).T)

    wqkv = T(inputs["L1_attn_w_qkv"])
    perm = _evens_odds_perm()
    g["wq1"] = np.ascontiguousarray(wqkv[:, 0:D][:, perm])
    g["wk1"] = np.ascontiguousarray(wqkv[:, D:2 * D][:, perm])
    g["wv1"] = np.ascontiguousarray(wqkv[:, 2 * D:3 * D])
    g["wout1"] = T(inputs["L1_attn_w_out"])
    g["boutc"] = np.ascontiguousarray(
        np.asarray(inputs["L1_attn_b_out"], np.float32).reshape(NCH, 128).T)
    g["wfc"] = T(inputs["L1_ffn_w_fc"])
    g["wproj"] = T(inputs["L1_ffn_w_proj"])
    g["uT"] = T(inputs["L1_ffn_u"])
    g["vmat"] = C(inputs["L1_ffn_v"])
    g["fbT"] = T(inputs["L1_ffn_fast_b"])
    g["gatec"] = np.full((16, 1), float(np.asarray(inputs["L1_ffn_gate"])),
                         dtype=np.float32)

    inv_freq = 1.0 / 10000.0 ** (np.arange(0, HD, 2, dtype=np.float32) / HD)
    t = np.arange(S, dtype=np.float32)
    fr = np.outer(t, inv_freq)
    cos = np.cos(fr).astype(np.float32).T
    sin = np.sin(fr).astype(np.float32).T
    g["t1"] = np.ascontiguousarray(np.vstack([cos, cos, cos, cos]))
    g["t2"] = np.ascontiguousarray(np.vstack([-sin, sin, -sin, sin]))
    sw = np.zeros((128, 128), dtype=np.float32)
    for blk in range(4):
        b = blk * 32
        dst = (b + 32) % 64 + (b // 64) * 64
        for i in range(32):
            sw[dst + i, b + i] = 1.0
    g["pswap"] = sw
    return g



class Ctx:
    pass


def setup_common(P, A, es):
    C = Ctx()
    nc = P.nc
    C.psum = es.enter_context(nc.psum_tensor("psum", [128, 4096], F32))
    C.bank = [C.psum[:, 512 * i:512 * (i + 1)] for i in range(8)]
    C.bank_free = [(None, 0)] * 8
    C.dw = P.sem("dw", es)
    C.dx = P.sem("dx", es)
    C.pe = P.sem("pe", es)
    C.ve = P.sem("ve", es)
    C.se = P.sem("se", es)
    C.ge = P.sem("ge", es)
    C.all_sems = [C.dw, C.dx, C.pe, C.ve, C.se, C.ge]
    C.ones_bf_col = A.take([128, 1], BF16, "ones_col")
    C.ones_f_row = A.take([1, 128], F32, "ones_f")
    C.ones_bf_row = A.take([1, 128], BF16, "ones_bfr")
    C.half_f_row = A.take([1, 128], F32, "half_f")
    C.nhalf_f_row = A.take([1, 128], F32, "nhalf_f")
    C.eps_t = A.take([1, 1], F32, "eps")
    C.ones2_bf = A.take([65, 128], BF16, "ones2")
    P.op("gpsimd", "memset", C.ones_bf_col, 1.0)
    P.op("gpsimd", "memset", C.ones_f_row, 1.0)
    P.op("gpsimd", "memset", C.ones_bf_row, 1.0)
    P.op("gpsimd", "memset", C.half_f_row, 0.5)
    P.op("gpsimd", "memset", C.eps_t, 1e-5)
    P.op("gpsimd", "memset", C.ones2_bf, 1.0)
    P.op("gpsimd", "memset", C.nhalf_f_row, -0.5, inc=C.ge)
    return C


def use_bank(P, C, i):
    s, n = C.bank_free[i]
    if s is not None:
        P.wait("tensor", s, n)


def release_bank(C, i, sem):
    C.bank_free[i] = (sem, sem.n)


def emit_ln(P, A, C, x_sb, xn_out):
    m0 = A.mark()
    xb = A.take([128, NCH, S], BF16, "ln_xb")
    sums = A.take([1, 2, S], F32, "ln_sums")
    mneg = A.take([1, S], F32, "ln_mneg")
    rstd = A.take([1, S], F32, "ln_rstd")
    mneg_b = A.take([128, S], F32, "ln_mnegb")
    rstd_b = A.take([128, S], F32, "ln_rstdb")
    scrf = A.take([128, S], F32, "ln_scr")
    bs, bb = 7, 6

    for c in range(NCH):
        P.op("vector", "tensor_copy", xb[:, c, :], x_sb[:, c, :], inc=C.ve)
    for tb in range(S // 512):
        use_bank(P, C, bs)
        P.wait("tensor", C.ve)
        for c in range(NCH):
            P.mm(C.bank[bs][0:1, :], C.ones_bf_col, xb[:, c, bass.ts(tb, 512)],
                 start=(c == 0), stop=(c == NCH - 1),
                 inc=C.pe if c == NCH - 1 else None)
        P.wait("vector", C.pe)
        P.op("vector", "tensor_copy", sums[0:1, 0, bass.ts(tb, 512)],
             C.bank[bs][0:1, :], inc=C.ve)
        release_bank(C, bs, C.ve)
    for c in range(NCH):
        P.wait("scalar", C.pe)
        P.op("scalar", "activation", xb[:, c, :], xb[:, c, :], AF.Square,
             inc=C.se)
    for tb in range(S // 512):
        use_bank(P, C, bs)
        P.wait("tensor", C.se)
        for c in range(NCH):
            P.mm(C.bank[bs][0:1, :], C.ones_bf_col, xb[:, c, bass.ts(tb, 512)],
                 start=(c == 0), stop=(c == NCH - 1),
                 inc=C.pe if c == NCH - 1 else None)
        P.wait("vector", C.pe)
        P.op("vector", "tensor_copy", sums[0:1, 1, bass.ts(tb, 512)],
             C.bank[bs][0:1, :], inc=C.ve)
        release_bank(C, bs, C.ve)
    P.op("vector", "tensor_scalar", mneg, sums[0:1, 0, :], -1.0 / D, None, ALU.mult)
    P.op("vector", "tensor_scalar", rstd, sums[0:1, 0, :], 1.0 / D, None, ALU.mult)
    P.op("vector", "tensor_mul", rstd, rstd, rstd)
    P.op("vector", "scalar_tensor_tensor", rstd, sums[0:1, 1, :], 1.0 / D, rstd,
         ALU.mult, ALU.subtract, inc=C.ve)
    P.wait("scalar", C.ve)
    P.op("scalar", "activation", rstd, rstd, AF.Sqrt, bias=C.eps_t, inc=C.se)
    P.wait("vector", C.se)
    P.op("vector", "reciprocal", rstd, rstd, inc=C.ve)
    for tb in range(S // 512):
        use_bank(P, C, bb)
        P.wait("tensor", C.ve)
        P.mm(C.bank[bb][:, :], C.ones_f_row, mneg[0:1, bass.ts(tb, 512)], inc=C.pe)
        P.wait("vector", C.pe)
        P.op("vector", "tensor_copy", mneg_b[:, bass.ts(tb, 512)], C.bank[bb][:, :],
             inc=C.ve)
        P.wait("tensor", C.ve)
        P.mm(C.bank[bb][:, :], C.ones_f_row, rstd[0:1, bass.ts(tb, 512)], inc=C.pe)
        P.wait("vector", C.pe)
        P.op("vector", "tensor_copy", rstd_b[:, bass.ts(tb, 512)], C.bank[bb][:, :],
             inc=C.ve)
        release_bank(C, bb, C.ve)
    for c in range(NCH):
        P.op("vector", "tensor_add", scrf, x_sb[:, c, :], mneg_b)
        P.op("vector", "tensor_mul", xn_out[:, c, :], scrf, rstd_b, inc=C.ve)
    A.reset(m0)


def emit_ffn(P, A, C, xn, w1_sb, w2_sb, g_hbm, layer, b1c=None, b2c=None):
    m0 = A.mark()
    rr = A.take([128, NI, TQB], BF16, "rr")
    gtile = A.take([128, NCH, TQB], BF16, "gtile")
    scrf = A.take([128, TQB], F32, "ffn_scr")
    f1b = [0, 1]
    f2b = [2, 3, 4, 5]
    gtile_free = C.dx.n
    for tb in range(NFB):
        for c2 in range(NI):
            bk = f1b[c2 % 2]
            use_bank(P, C, bk)
            for ci in range(NCH):
                P.mm(C.bank[bk][:, 0:TQB], w1_sb[:, ci, bass.ts(c2, 128)],
                     xn[:, ci, bass.ts(tb, TQB)],
                     start=(ci == 0), stop=(ci == NCH - 1),
                     inc=C.pe if ci == NCH - 1 else None)
            if layer == 0:
                P.wait("vector", C.pe)
                P.op("vector", "tensor_scalar", rr[:, c2, :], C.bank[bk][:, 0:TQB],
                     b1c[:, c2:c2 + 1], 0.0, ALU.add, ALU.max, inc=C.ve)
                release_bank(C, bk, C.ve)
            else:
                P.wait("vector", C.pe)
                P.op("vector", "tensor_scalar", scrf, C.bank[bk][:, 0:TQB],
                     0.0, None, ALU.max, inc=C.ve)
                release_bank(C, bk, C.ve)
                P.wait("scalar", C.ve)
                P.op("scalar", "activation", rr[:, c2, :], scrf, AF.Square,
                     inc=C.se)
        for oc in range(NCH):
            bk = f2b[oc % 4]
            use_bank(P, C, bk)
            if layer == 1:
                P.wait("tensor", C.se)
            else:
                P.wait("tensor", C.ve)
            for c2 in range(NI):
                P.mm(C.bank[bk][:, 0:TQB], w2_sb[:, c2, bass.ts(oc, 128)],
                     rr[:, c2, :],
                     start=(c2 == 0), stop=(c2 == NI - 1),
                     inc=C.pe if c2 == NI - 1 else None)
            P.wait("scalar", C.pe)
            P.wait("scalar", C.dx, gtile_free)
            if layer == 0:
                P.op("scalar", "activation", gtile[:, oc, :], C.bank[bk][:, 0:TQB],
                     AF.Silu, bias=b2c[:, oc:oc + 1], inc=C.se)
            else:
                P.op("scalar", "activation", gtile[:, oc, :], C.bank[bk][:, 0:TQB],
                     AF.Silu, inc=C.se)
            release_bank(C, bk, C.se)
        P.wait("sync", C.se)
        P.dma("sync", g_hbm[:, :, bass.ts(tb, TQB)].rearrange("c p t -> p c t"),
              gtile, C.dx)
        gtile_free = C.dx.n
    A.reset(m0)


def emit_qk_proj_l0(P, A, C, xn, w_sb, out_bf):
    bks = [0, 1]
    free_i = 0
    for oc in range(NCH):
        for tb in range(S // 512):
            bk = bks[free_i % 2]
            free_i += 1
            use_bank(P, C, bk)
            for ci in range(NCH):
                P.mm(C.bank[bk][:, :], w_sb[:, ci, bass.ts(oc, 128)],
                     xn[:, ci, bass.ts(tb, 512)],
                     start=(ci == 0), stop=(ci == NCH - 1),
                     inc=C.pe if ci == NCH - 1 else None)
            P.wait("vector", C.pe)
            P.op("vector", "tensor_copy", out_bf[:, oc, bass.ts(tb, 512)],
                 C.bank[bk][:, :], inc=C.ve)
            release_bank(C, bk, C.ve)


def emit_qk_proj_rope(P, A, C, xn, w_sb, out_bf, t1, t2, pswap_bf):
    m0 = A.mark()
    qpre = A.take([128, 512], BF16, "rope_qpre")
    s1 = A.take([128, 512], F32, "rope_s1")
    s2 = A.take([128, 512], F32, "rope_s2")
    pi = 0
    for oc in range(NCH):
        for tb in range(S // 512):
            bk = [0, 1][pi % 2]
            bk2 = [2, 3][pi % 2]
            pi += 1
            use_bank(P, C, bk)
            for ci in range(NCH):
                P.mm(C.bank[bk][:, :], w_sb[:, ci, bass.ts(oc, 128)],
                     xn[:, ci, bass.ts(tb, 512)],
                     start=(ci == 0), stop=(ci == NCH - 1),
                     inc=C.pe if ci == NCH - 1 else None)
            P.wait("vector", C.pe)
            P.op("vector", "tensor_copy", qpre, C.bank[bk][:, :], inc=C.ve)
            release_bank(C, bk, C.ve)
            use_bank(P, C, bk2)
            P.wait("tensor", C.ve)
            P.mm(C.bank[bk2][:, :], pswap_bf, qpre, inc=C.pe)
            P.op("vector", "tensor_mul", s1, qpre, t1[:, bass.ts(tb, 512)])
            P.wait("vector", C.pe)
            P.op("vector", "tensor_mul", s2, C.bank[bk2][:, :],
                 t2[:, bass.ts(tb, 512)], inc=C.ve)
            release_bank(C, bk2, C.ve)
            P.op("vector", "tensor_add", out_bf[:, oc, bass.ts(tb, 512)], s1, s2,
                 inc=C.ve)
    A.reset(m0)


def emit_v_proj(P, A, C, xn, wv_sb, v_aug, col_off=0):
    pi = 0
    for tk in range(NTK):
        for ob in range(2):
            bk = [0, 1][pi % 2]
            pi += 1
            use_bank(P, C, bk)
            for ci in range(NCH):
                P.mm(C.bank[bk][:, :], xn[:, ci, bass.ts(tk, 128)],
                     wv_sb[:, ci, col_off + ob * 512:col_off + (ob + 1) * 512],
                     start=(ci == 0), stop=(ci == NCH - 1),
                     inc=C.pe if ci == NCH - 1 else None)
            P.wait("vector", C.pe)
            P.op("vector", "tensor_copy",
                 v_aug[:, tk, ob * 8:(ob + 1) * 8, 0:64],
                 C.bank[bk][:, :].rearrange("p (h d) -> p h d", h=8), inc=C.ve)
            release_bank(C, bk, C.ve)


def emit_mshift(P, A, C, q_bf, hp, mshift, kmaxs, qsq_scr, msrow, inv_scale):
    bq = 7
    P.wait("scalar", C.pe)
    P.op("scalar", "activation", qsq_scr, q_bf[:, hp, :], AF.Square, inc=C.se)
    for side, base in ((0, 0), (1, 64)):
        h = 2 * hp + side
        use_bank(P, C, bq)
        P.wait("tensor", C.se)
        for tb in range(S // 512):
            P.mm(C.bank[bq][0:1, :], C.ones_bf_col[base:base + 64, :],
                 qsq_scr[base:base + 64, bass.ts(tb, 512)], inc=C.pe)
            P.wait("vector", C.pe)
            P.op("vector", "tensor_copy", msrow[0:1, bass.ts(tb, 512)],
                 C.bank[bq][0:1, :], inc=C.ve)
            P.wait("tensor", C.ve)
        release_bank(C, bq, C.ve)
        P.wait("scalar", C.ve)
        P.op("scalar", "activation", msrow, msrow, AF.Sqrt, inc=C.se)
        P.wait("vector", C.se)
        P.op("vector", "tensor_scalar", msrow, msrow,
             kmaxs[0:1, h:h + 1], -64.0, ALU.mult, ALU.add)
        P.op("vector", "tensor_scalar", msrow, msrow, 0.0, -float(inv_scale),
             ALU.max, ALU.mult)
        P.op("vector", "tensor_copy", mshift[base:base + 1, :], msrow, inc=C.ve)


def emit_kmaxs(P, A, C, k_bf, kmaxs, scr, row, inv_scale):
    m0 = A.mark()
    one = A.take([1, 1], F32, "km_one")
    for hp in range(NCH):
        P.wait("scalar", C.pe)
        P.wait("scalar", C.ve)
        P.op("scalar", "activation", scr, k_bf[:, hp, :], AF.Square, inc=C.se)
        for side, base in ((0, 0), (1, 64)):
            h = 2 * hp + side
            bq = 6 + side
            use_bank(P, C, bq)
            P.wait("tensor", C.se)
            for tb in range(S // 512):
                P.mm(C.bank[bq][0:1, :], C.ones_bf_col[base:base + 64, :],
                     scr[base:base + 64, bass.ts(tb, 512)], inc=C.pe)
                P.wait("vector", C.pe)
                P.op("vector", "tensor_copy", row[0:1, bass.ts(tb, 512)],
                     C.bank[bq][0:1, :], inc=C.ve)
                P.wait("tensor", C.ve)
            release_bank(C, bq, C.ve)
            P.op("vector", "reduce_max", one, row, mybir.AxisListType.X, inc=C.ve)
            P.wait("scalar", C.ve)
            P.op("scalar", "activation", one, one, AF.Sqrt, inc=C.se)
            P.wait("vector", C.se)
            P.op("vector", "tensor_scalar", kmaxs[0:1, h:h + 1], one,
                 1.0 / float(inv_scale), None, ALU.mult, inc=C.ve)
    A.reset(m0)


def emit_attention(P, A, C, q_bf, k_bf, v_aug, scale, consume, shifted=False):
    m0 = A.mark()
    recips = A.take([1, 2, 512], F32, "recips")
    rb = A.take([128, 512], F32, "rb")
    s_blk = A.take([128, 512], F32, "s_blk")
    if shifted:
        inv_scale = 1.0 / scale
        mshift = A.take([65, S], BF16, "mshift")
        kmaxs = A.take([1, H], F32, "kmaxs")
        qsq_scr = A.take([128, S], BF16, "qsq")
        msrow = A.take([1, S], F32, "msrow")
        emit_kmaxs(P, A, C, k_bf, kmaxs, qsq_scr, msrow, inv_scale)
    rtmp = A.take([1, 2, 512], F32, "rtmp")
    ptab = [A.take([128, 1024], BF16, "ptab0"),
            A.take([128, 1024], BF16, "ptab1")]
    pv_pairs = [(4, 5), (6, 7)]

    def emit_scores(hp, tq, tk):
        p = tk % 2
        bA, bB = 2 * p, 2 * p + 1
        use_bank(P, C, bA)
        use_bank(P, C, bB)
        if shifted:
            P.mm(C.bank[bA], C.ones2_bf[0:1, :], mshift[0:1, tq],
                 start=True, stop=False)
            P.mm(C.bank[bB], C.ones2_bf[64:65, :], mshift[64:65, tq],
                 start=True, stop=False)
            P.mm(C.bank[bA], k_bf[0:64, hp, bass.ts(tk, 128)],
                 q_bf[0:64, hp, tq], start=False, stop=True, tp=(0, 0))
            P.mm(C.bank[bB], k_bf[64:128, hp, bass.ts(tk, 128)],
                 q_bf[64:128, hp, tq], start=False, stop=True, tp=(64, 0),
                 inc=C.pe)
        else:
            P.mm(C.bank[bA], k_bf[0:64, hp, bass.ts(tk, 128)],
                 q_bf[0:64, hp, tq], start=True, stop=True, tp=(0, 0))
            P.mm(C.bank[bB], k_bf[64:128, hp, bass.ts(tk, 128)],
                 q_bf[64:128, hp, tq], start=True, stop=True, tp=(64, 0),
                 inc=C.pe)

    def emit_exp(tk):
        p = tk % 2
        P.wait("scalar", C.pe)
        P.op("scalar", "activation", ptab[p],
             C.psum[:, 1024 * p:1024 * (p + 1)], AF.Exp, scale=scale,
             inc=C.se)
        release_bank(C, 2 * p, C.se)
        release_bank(C, 2 * p + 1, C.se)
        return C.se.n

    def emit_pv(hp, tk, se_need, bOA, bOB):
        p = tk % 2
        P.wait("tensor", C.se, se_need)
        P.mm(C.bank[bOA][0:65, :], v_aug[:, tk, 2 * hp, :], ptab[p][:, 0:512],
             start=(tk == 0), stop=(tk == NTK - 1))
        P.mm(C.bank[bOB][0:65, :], v_aug[:, tk, 2 * hp + 1, :],
             ptab[p][:, 512:1024],
             start=(tk == 0), stop=(tk == NTK - 1), inc=C.pe)

    pending = [None]

    def norm_part2(hp, tqb, bOA, bOB):
        use_bank(P, C, bOA)
        use_bank(P, C, bOB)
        P.wait("tensor", C.ve)
        P.mm(C.bank[bOA][0:64, :], C.ones_f_row[0:1, 0:64], recips[0:1, 0, :])
        P.mm(C.bank[bOB][0:64, :], C.ones_f_row[0:1, 0:64], recips[0:1, 1, :],
             inc=C.pe)
        P.wait("vector", C.pe)
        P.op("vector", "tensor_copy", rb[0:64, :], C.bank[bOA][0:64, :])
        P.op("vector", "tensor_copy", rb[64:128, :], C.bank[bOB][0:64, :],
             inc=C.ve)
        release_bank(C, bOA, C.ve)
        release_bank(C, bOB, C.ve)
        P.op("vector", "tensor_mul", s_blk[0:64, :], s_blk[0:64, :],
             rb[0:64, :])
        P.op("vector", "tensor_mul", s_blk[64:128, :], s_blk[64:128, :],
             rb[64:128, :], inc=C.ve)
        consume(hp, tqb, s_blk)

    for hp in range(NCH):
        if shifted:
            emit_mshift(P, A, C, q_bf, hp, mshift, kmaxs, qsq_scr, msrow,
                        inv_scale)
            P.wait("tensor", C.ve)
        for tqb in range(S // 512):
            tq = bass.ts(tqb, 512)
            it = hp * 4 + tqb
            bOA, bOB = pv_pairs[it % 2]
            se_at = {}
            for tk in range(NTK):
                emit_scores(hp, tq, tk)
                if tk == 0:
                    use_bank(P, C, bOA)
                    use_bank(P, C, bOB)
                se_at[tk] = emit_exp(tk)
                if tk > 0:
                    emit_pv(hp, tk - 1, se_at[tk - 1], bOA, bOB)
                if tk == 10 and pending[0] is not None:
                    norm_part2(*pending[0])
                    pending[0] = None
            emit_pv(hp, NTK - 1, se_at[NTK - 1], bOA, bOB)
            P.wait("vector", C.pe)
            P.op("vector", "tensor_scalar", rtmp[0:1, 0, :],
                 C.bank[bOA][64:65, :], 1e-36, None, ALU.add)
            P.op("vector", "tensor_scalar", rtmp[0:1, 1, :],
                 C.bank[bOB][64:65, :], 1e-36, None, ALU.add)
            P.op("vector", "tensor_copy", s_blk[0:64, :], C.bank[bOA][0:64, :])
            P.op("vector", "tensor_copy", s_blk[64:128, :],
                 C.bank[bOB][0:64, :], inc=C.ve)
            release_bank(C, bOA, C.ve)
            release_bank(C, bOB, C.ve)
            P.op("vector", "reciprocal", recips[0:1, 0, :], rtmp[0:1, 0, :])
            P.op("vector", "reciprocal", recips[0:1, 1, :], rtmp[0:1, 1, :],
                 inc=C.ve)
            pending[0] = (hp, tqb, bOA, bOB)
            norm_part2(*pending[0])
            pending[0] = None
    A.reset(m0)


def emit_ternarize(P, A, C, w_hbm, n_in, n_out, weff_sb, tag):
    m0 = A.mark()
    nic = n_in // 128
    nob = n_out // 512
    chunk_f = A.take([128, n_out], F32, f"tn_cf_{tag}")
    chunk_abs = A.take([128, n_out], BF16, f"tn_ca_{tag}")
    srow = A.take([1, n_out], F32, f"tn_sr_{tag}")
    bh = A.take([128, n_out], F32, f"tn_bh_{tag}")
    sbc = A.take([128, n_out], BF16, f"tn_sbc_{tag}")
    mneg = A.take([128, n_out], BF16, f"tn_mn_{tag}")
    w_r = w_hbm.rearrange("(c p) o -> p c o", p=128)
    for c in range(nic):
        P.wait("sync", C.se)
        P.dma("sync", chunk_f, w_r[:, c, :], C.dx)
        P.wait("scalar", C.dx)
        P.wait("scalar", C.pe)
        P.op("scalar", "activation", chunk_abs, chunk_f, AF.Abs, inc=C.se)
        P.wait("tensor", C.se)
        for ob in range(nob):
            if c == 0:
                use_bank(P, C, ob)
            P.mm(C.bank[ob][0:1, :], C.ones_bf_col, chunk_abs[:, bass.ts(ob, 512)],
                 start=(c == 0), stop=(c == nic - 1),
                 inc=C.pe if ob == nob - 1 else None)
    P.wait("vector", C.pe)
    for ob in range(nob):
        P.op("vector", "tensor_scalar", srow[0:1, bass.ts(ob, 512)],
             C.bank[ob][0:1, :], 1.0 / n_in, 1e-5, ALU.mult, ALU.max,
             inc=C.ve if ob == nob - 1 else None)
    for ob in range(nob):
        release_bank(C, ob, C.ve)
    for ob in range(nob):
        bk = ob % 2
        use_bank(P, C, bk)
        P.wait("tensor", C.ve)
        P.mm(C.bank[bk][:, :], C.half_f_row, srow[0:1, bass.ts(ob, 512)],
             inc=C.pe)
        P.wait("vector", C.pe)
        P.op("vector", "tensor_copy", bh[:, bass.ts(ob, 512)], C.bank[bk][:, :],
             inc=C.ve)
        release_bank(C, bk, C.ve)
        bk = (ob + 1) % 2
        use_bank(P, C, bk)
        P.wait("tensor", C.ve)
        P.mm(C.bank[bk][:, :], C.ones_f_row, srow[0:1, bass.ts(ob, 512)],
             inc=C.pe)
        P.wait("vector", C.pe)
        P.op("vector", "tensor_copy", sbc[:, bass.ts(ob, 512)], C.bank[bk][:, :],
             inc=C.ve)
        release_bank(C, bk, C.ve)
    for c in range(nic):
        P.wait("sync", C.ve)
        P.dma("sync", chunk_f, w_r[:, c, :], C.dx)
        P.wait("vector", C.dx)
        wch = weff_sb[:, c, :]
        P.op("vector", "tensor_tensor", wch, chunk_f, bh, ALU.is_ge)
        P.op("vector", "scalar_tensor_tensor", mneg, chunk_f, -1.0, bh,
             ALU.mult, ALU.is_ge)
        P.op("vector", "tensor_sub", wch, wch, mneg)
        P.op("vector", "tensor_mul", wch, wch, sbc, inc=C.ve)
    A.reset(m0)


def emit_lowrank_fold(P, A, C, io, weff_fc):
    m0 = A.mark()
    u_sb = A.take([16, INTER], BF16, "lr_u")
    v_sb = A.take([16, D], BF16, "lr_v")
    fb_sb = A.take([16, RANK], F32, "lr_fb")
    gate_c = A.take([16, 1], F32, "lr_gate")
    gfb = A.take([16, RANK], BF16, "lr_gfb")
    at = A.take([16, D], BF16, "lr_at")
    P.dma("gpsimd", u_sb, io.w["uT"], C.dw)
    P.dma("gpsimd", v_sb, io.w["vmat"], C.dw)
    P.dma("gpsimd", fb_sb, io.w["fbT"], C.dw)
    P.dma("gpsimd", gate_c, io.w["gatec"], C.dw)
    P.wait("vector", C.dw)
    P.op("vector", "tensor_scalar", gfb, fb_sb, gate_c, None, ALU.mult, inc=C.ve)
    use_bank(P, C, 0)
    use_bank(P, C, 1)
    P.wait("tensor", C.ve)
    P.mm(C.bank[0][0:16, :], gfb, v_sb[:, 0:512], start=True, stop=True)
    P.mm(C.bank[1][0:16, :], gfb, v_sb[:, 512:1024], start=True, stop=True,
         inc=C.pe)
    P.wait("vector", C.pe)
    P.op("vector", "tensor_copy", at[:, 0:512], C.bank[0][0:16, :])
    P.op("vector", "tensor_copy", at[:, 512:1024], C.bank[1][0:16, :], inc=C.ve)
    release_bank(C, 0, C.ve)
    release_bank(C, 1, C.ve)
    for ci in range(NCH):
        for ob in range(INTER // 512):
            bk = ob % 2
            use_bank(P, C, bk)
            P.wait("tensor", C.ve)
            P.mm(C.bank[bk][:, :], at[:, bass.ts(ci, 128)],
                 u_sb[:, bass.ts(ob, 512)], inc=C.pe)
            P.wait("vector", C.pe)
            P.op("vector", "tensor_add", weff_fc[:, ci, bass.ts(ob, 512)],
                 weff_fc[:, ci, bass.ts(ob, 512)], C.bank[bk][:, :], inc=C.ve)
            release_bank(C, bk, C.ve)
    A.reset(m0)



def emit_layer0(P, A, C, io, x_src, x_dst):
    m_layer = A.mark()
    xn = A.take([128, NCH, S], BF16, "xn0")
    x_src_r = x_src.rearrange("(c p) t -> p c t", p=128)
    x_dst_r = x_dst.rearrange("(c p) t -> c p t", p=128)
    g_hbm = io.g0

    m0 = A.mark()
    x_sb = A.take([128, NCH, S], F32, "x0")
    P.dma("sync", x_sb, x_src_r, C.dx)
    P.wait("vector", C.dx)
    P.wait("vector", C.ge)
    P.wait("tensor", C.ge)
    emit_ln(P, A, C, x_sb, xn)
    A.reset(m0)
    P.barrier(C.all_sems)

    m1 = A.mark()
    w1_sb = A.take([128, NCH, INTER], BF16, "w1")
    w2_sb = A.take([128, NI, D], BF16, "w2")
    b1c = A.take([128, NI], F32, "b1c")
    b2c = A.take([128, NCH], F32, "b2c")
    P.dma("gpsimd", w1_sb, io.w["w1"].rearrange("(c p) o -> p c o", p=128), C.dw)
    P.dma("gpsimd", w2_sb, io.w["w2"].rearrange("(c p) o -> p c o", p=128), C.dw)
    P.dma("sync", b1c, io.w["b1c"], C.dx)
    P.dma("sync", b2c, io.w["b2c"], C.dx)
    P.wait("tensor", C.dw)
    P.wait("vector", C.dx)
    P.wait("scalar", C.dx)
    emit_ffn(P, A, C, xn, w1_sb, w2_sb, g_hbm, 0, b1c, b2c)
    A.reset(m1)
    P.barrier(C.all_sems)

    q_bf = A.take([128, NCH, S], BF16, "q0")
    k_bf = A.take([128, NCH, S], BF16, "k0")
    v_aug = A.take([128, NTK, H, 65], BF16, "v0")
    m2 = A.mark()
    wslot = [A.take([128, NCH, D], BF16, "ws0"),
             A.take([128, NCH, D], BF16, "ws1")]
    P.op("gpsimd", "memset", v_aug[:, :, :, 64:65], 1.0, inc=C.ge)
    P.dma("gpsimd", wslot[0], io.w["wq"].rearrange("(c p) o -> p c o", p=128), C.dw)
    P.wait("tensor", C.dw)
    emit_qk_proj_l0(P, A, C, xn, wslot[0], q_bf)
    P.dma("gpsimd", wslot[1], io.w["wk"].rearrange("(c p) o -> p c o", p=128), C.dw)
    P.wait("tensor", C.dw)
    emit_qk_proj_l0(P, A, C, xn, wslot[1], k_bf)
    P.wait("gpsimd", C.pe)
    P.dma("gpsimd", wslot[0], io.w["wv"].rearrange("(c p) o -> p c o", p=128), C.dw)
    P.wait("tensor", C.dw)
    P.wait("tensor", C.ge)
    emit_v_proj(P, A, C, xn, wslot[0], v_aug)
    A.reset(m2)
    P.barrier(C.all_sems)
    if io.dbg:
        P.dma("sync", io.d_xn, xn, C.dx)
        P.dma("sync", io.d_q, q_bf, C.dx)
        P.dma("sync", io.d_k, k_bf, C.dx)
        P.dma("sync", io.d_v, v_aug, C.dx)

    xst = A.take([128, 2, 512], F32, "xstage")
    gst = A.take([128, 2, 512], BF16, "gstage")
    stage_free = [(None, 0), (None, 0)]
    g_hbm_r = g_hbm
    x_src_cr = x_src.rearrange("(c p) t -> c p t", p=128)

    def consume0(hp, tqb, s_blk):
        sl = (hp * 4 + tqb) % 2
        if io.dbg:
            P.wait("sync", C.ve)
            P.dma("sync", io.d_s[hp, :, bass.ts(tqb, 512)], s_blk, C.dx)
            P.wait("vector", C.dx)
        P.wait("vector", C.dx, stage_issue[(hp, tqb)])
        P.op("vector", "tensor_mul", s_blk, s_blk, gst[:, sl, :])
        P.op("vector", "tensor_add", xst[:, sl, :], xst[:, sl, :], s_blk,
             inc=C.ve)
        P.wait("sync", C.ve)
        P.dma("sync", x_dst_r[hp, :, bass.ts(tqb, 512)], xst[:, sl, :], C.dx)
        stage_free[sl] = (C.dx, C.dx.n)

    stage_issue = {}

    def consume_wrap(hp, tqb, s_blk):
        consume0(hp, tqb, s_blk)
        nxt = hp * 4 + tqb + 1
        if nxt < 32:
            issue_stage(nxt // 4, nxt % 4)

    def issue_stage(hp, tqb):
        sl = (hp * 4 + tqb) % 2
        s, n = stage_free[sl]
        if s is not None:
            P.wait("sync", s, n)
        P.dma("sync", xst[:, sl, :], x_src_cr[hp, :, bass.ts(tqb, 512)], C.dx)
        P.dma("sync", gst[:, sl, :], g_hbm_r[hp, :, bass.ts(tqb, 512)], C.dx)
        stage_issue[(hp, tqb)] = C.dx.n

    issue_stage(0, 0)
    issue_stage(0, 1)
    emit_attention(P, A, C, q_bf, k_bf, v_aug, 1.0 / 32.0, consume_wrap)
    A.reset(m_layer)
    P.barrier(C.all_sems)


def emit_layer1(P, A, C, io, x_src, x_dst):
    m_layer = A.mark()
    xn = A.take([128, NCH, S], BF16, "xn1")
    x_src_r = x_src.rearrange("(c p) t -> p c t", p=128)
    x_src_cr = x_src.rearrange("(c p) t -> c p t", p=128)
    x_dst_r = x_dst.rearrange("(c p) t -> c p t", p=128)
    g_hbm = io.g1

    m0 = A.mark()
    x_sb = A.take([128, NCH, S], F32, "x1")
    P.dma("sync", x_sb, x_src_r, C.dx)
    P.wait("vector", C.dx)
    emit_ln(P, A, C, x_sb, xn)
    A.reset(m0)
    P.barrier(C.all_sems)

    weff_fc = A.take([128, NCH, INTER], BF16, "weff_fc")
    emit_ternarize(P, A, C, io.w["wfc"], D, INTER, weff_fc, "fc")
    P.barrier(C.all_sems)
    emit_lowrank_fold(P, A, C, io, weff_fc)
    P.barrier(C.all_sems)
    weff_pj = A.take([128, NI, D], BF16, "weff_pj")
    emit_ternarize(P, A, C, io.w["wproj"], INTER, D, weff_pj, "pj")
    P.barrier(C.all_sems)
    emit_ffn(P, A, C, xn, weff_fc, weff_pj, g_hbm, 1)
    A.reset(m_layer)
    xn = A.take([128, NCH, S], BF16, "xn1_alias")
    P.barrier(C.all_sems)

    t1 = A.take([128, S], BF16, "t1")
    t2 = A.take([128, S], BF16, "t2")
    pswap = A.take([128, 128], BF16, "pswap")
    attnout = A.take([128, NCH, S], BF16, "attnout")
    q_bf = A.take([128, NCH, S], BF16, "q1")
    k_bf = A.take([128, NCH, S], BF16, "k1")
    v_aug = A.take([128, NTK, H, 65], BF16, "v1")
    m2 = A.mark()
    wslot = A.take([128, NCH, D], BF16, "w1s0")
    P.dma("gpsimd", t1, io.w["t1"], C.dw)
    P.dma("gpsimd", t2, io.w["t2"], C.dw)
    P.dma("gpsimd", pswap, io.w["pswap"], C.dw)
    P.op("gpsimd", "memset", v_aug[:, :, :, 64:65], 1.0, inc=C.ge)
    P.dma("gpsimd", wslot, io.w["wq1"].rearrange("(c p) o -> p c o", p=128),
          C.dw)
    P.wait("tensor", C.dw)
    P.wait("vector", C.dw)
    emit_qk_proj_rope(P, A, C, xn, wslot, q_bf, t1, t2, pswap)
    P.wait("gpsimd", C.pe)
    P.dma("gpsimd", wslot, io.w["wk1"].rearrange("(c p) o -> p c o", p=128),
          C.dw)
    P.wait("tensor", C.dw)
    emit_qk_proj_rope(P, A, C, xn, wslot, k_bf, t1, t2, pswap)
    P.wait("gpsimd", C.pe)
    P.dma("gpsimd", wslot, io.w["wv1"].rearrange("(c p) o -> p c o", p=128),
          C.dw)
    P.wait("tensor", C.dw)
    P.wait("tensor", C.ge)
    emit_v_proj(P, A, C, xn, wslot, v_aug)
    A.reset(m2)
    P.barrier(C.all_sems)

    def consume1(hp, tqb, s_blk):
        P.op("vector", "tensor_copy", attnout[:, hp, bass.ts(tqb, 512)], s_blk,
             inc=C.ve)

    emit_attention(P, A, C, q_bf, k_bf, v_aug, 1.0 / 8.0, consume1, shifted=True)
    P.barrier(C.all_sems)

    m3 = A.mark()
    wout = A.take([128, NCH, D], BF16, "wout")
    boutc = A.take([128, NCH], F32, "boutc")
    xst = A.take([128, 2, 512], F32, "xst1")
    gst = A.take([128, 2, 512], BF16, "gst1")
    sblk = A.take([128, 512], F32, "sblk1")
    P.dma("gpsimd", wout, io.w["wout1"].rearrange("(c p) o -> p c o", p=128), C.dw)
    P.dma("sync", boutc, io.w["boutc"], C.dx)
    stage_free = [(None, 0), (None, 0)]
    stage_issue = {}

    def issue_stage(i):
        oc, tb = i // 4, i % 4
        sl = i % 2
        s, n = stage_free[sl]
        if s is not None:
            P.wait("sync", s, n)
        P.dma("sync", xst[:, sl, :], x_src_cr[oc, :, bass.ts(tb, 512)], C.dx)
        P.dma("sync", gst[:, sl, :], g_hbm[oc, :, bass.ts(tb, 512)], C.dx)
        stage_issue[i] = C.dx.n

    issue_stage(0)
    issue_stage(1)
    P.wait("tensor", C.dw)
    pi = 0
    for oc in range(NCH):
        for tb in range(S // 512):
            i = oc * 4 + tb
            sl = i % 2
            bk = [0, 1][pi % 2]
            pi += 1
            use_bank(P, C, bk)
            for ci in range(NCH):
                P.mm(C.bank[bk][:, :], wout[:, ci, bass.ts(oc, 128)],
                     attnout[:, ci, bass.ts(tb, 512)],
                     start=(ci == 0), stop=(ci == NCH - 1),
                     inc=C.pe if ci == NCH - 1 else None)
            P.wait("vector", C.pe)
            P.wait("vector", C.dx, stage_issue[i])
            P.op("vector", "tensor_scalar", sblk, C.bank[bk][:, :],
                 boutc[:, oc:oc + 1], None, ALU.add, inc=C.ve)
            release_bank(C, bk, C.ve)
            P.op("vector", "tensor_mul", sblk, sblk, gst[:, sl, :])
            P.op("vector", "tensor_add", xst[:, sl, :], xst[:, sl, :], sblk,
                 inc=C.ve)
            P.wait("sync", C.ve)
            P.dma("sync", x_dst_r[oc, :, bass.ts(tb, 512)], xst[:, sl, :], C.dx)
            stage_free[sl] = (C.dx, C.dx.n)
            if i + 2 < 32:
                issue_stage(i + 2)
    A.reset(m_layer)



W_L0 = ["wq", "wk", "wv", "w1", "w2", "b1c", "b2c"]
W_L1 = ["wq1", "wk1", "wv1", "wout1", "boutc", "wfc", "wproj", "uT", "vmat",
        "fbT", "gatec", "t1", "t2", "pswap"]


def build_program(layers=(0, 1), dbg=False):
    from contextlib import ExitStack
    nc = bass.Bass("TRN2", target_bir_lowering=False)
    io = Ctx()
    io.dbg = dbg
    io.x_in = nc.dram_tensor("x_in", [D, S], F32, kind="ExternalInput").ap()
    io.x_out = nc.dram_tensor("x_out", [D, S], F32, kind="ExternalOutput").ap()
    if layers == (0, 1):
        io.x_mid = nc.dram_tensor("x_mid", [D, S], F32, kind="Internal").ap()
    io.w = {}
    names = (W_L0 if 0 in layers else []) + (W_L1 if 1 in layers else [])
    shapes = dict(wq=(D, D), wk=(D, D), wv=(D, D), w1=(D, INTER), w2=(INTER, D),
                  b1c=(128, NI), b2c=(128, NCH), wq1=(D, D), wk1=(D, D),
                  wv1=(D, D), wout1=(D, D), boutc=(128, NCH), wfc=(D, INTER),
                  wproj=(INTER, D), uT=(RANK, INTER), vmat=(RANK, D),
                  fbT=(RANK, RANK), gatec=(16, 1), t1=(128, S), t2=(128, S),
                  pswap=(128, 128))
    for n in names:
        io.w[n] = nc.dram_tensor(n, list(shapes[n]), F32,
                                 kind="ExternalInput").ap()
    gkind = "ExternalOutput" if dbg else "Internal"
    if 0 in layers:
        io.g0 = nc.dram_tensor("g0", [NCH, 128, S], BF16, kind=gkind).ap()
    if 1 in layers:
        io.g1 = nc.dram_tensor("g1", [NCH, 128, S], BF16, kind=gkind).ap()
    if dbg:
        io.d_xn = nc.dram_tensor("d_xn", [128, NCH, S], BF16,
                                 kind="ExternalOutput").ap()
        io.d_q = nc.dram_tensor("d_q", [128, NCH, S], BF16,
                                kind="ExternalOutput").ap()
        io.d_k = nc.dram_tensor("d_k", [128, NCH, S], BF16,
                                kind="ExternalOutput").ap()
        io.d_v = nc.dram_tensor("d_v", [128, NTK, H, 65], BF16,
                                kind="ExternalOutput").ap()
        io.d_s = nc.dram_tensor("d_s", [NCH, 128, S], F32,
                                kind="ExternalOutput").ap()

    es = ExitStack()
    P = Prog(nc)
    A = Arena(nc)
    with nc.Block() as block:
        C = setup_common(P, A, es)
        if layers == (0, 1):
            emit_layer0(P, A, C, io, io.x_in, io.x_mid)
            emit_layer1(P, A, C, io, io.x_mid, io.x_out)
        elif layers == (0,):
            emit_layer0(P, A, C, io, io.x_in, io.x_out)
        elif layers == (1,):
            emit_layer1(P, A, C, io, io.x_in, io.x_out)
        P.wait("sync", C.dx)
        P.wait("sync", C.dw)
        P.op("sync", "nop")
        P.simulate()
        P.replay(block)
    es.close()
    return nc


def kernel(**inputs):
    g = host_prep(inputs)
    x = np.asarray(inputs["L0_attn_x"], dtype=np.float32)
    nc = build_program((0, 1))
    in_maps = []
    for b in range(8):
        m = dict(g)
        m["x_in"] = np.ascontiguousarray(x[b].T)
        in_maps.append(m)
    res = run_bass_kernel_spmd(nc, in_maps, core_ids=list(range(8)))
    out = np.stack([res.results[b]["x_out"].T for b in range(8)], axis=0)
    return out.astype(np.float32)


# version 08
# speedup vs baseline: 1.1190x; 1.1190x over previous
import numpy as np
import concourse.bass as bass
import concourse.mybir as mybir
from concourse.bass_utils import run_bass_kernel_spmd

F32 = mybir.dt.float32
BF16 = mybir.dt.bfloat16
AF = mybir.ActivationFunctionType
ALU = mybir.AluOpType

S = 2048
D = 1024
NCH = D // 128
H = 16
HD = 64
INTER = 4096
NI = INTER // 128
NTK = S // 128
RANK = 16
TQB = 256
NFB = S // TQB

ENGINES = ("tensor", "vector", "scalar", "gpsimd", "sync")


class Sem:
    __slots__ = ("h", "n", "name")

    def __init__(self, h, name):
        self.h = h
        self.n = 0
        self.name = name


class Prog:
    def __init__(self, nc):
        self.nc = nc
        self.recs = {e: [] for e in ENGINES}
        self._pend = {e: [] for e in ENGINES}
        self.sems = []

    def sem(self, name, es):
        s = Sem(es.enter_context(self.nc.semaphore(name)), name)
        self.sems.append(s)
        return s

    def wait(self, eng, sem, val=None):
        v = sem.n if val is None else val
        if v > 0:
            self._pend[eng].append((sem, v))

    def op(self, eng, meth, *args, inc=None, inc16=None, **kw):
        waits = self._pend[eng]
        self._pend[eng] = []
        incs = []
        if inc is not None:
            for s in (inc if isinstance(inc, (list, tuple)) else [inc]):
                s.n += 1
                incs.append((s, 1))
        if inc16 is not None:
            inc16.n += 16
            incs.append((inc16, 16))
        self.recs[eng].append((waits, meth, args, kw, incs))

    def dma(self, eng, out, in_, sem, **kw):
        self.op(eng, "dma_start", inc16=sem, out=out, in_=in_, **kw)

    def mm(self, out, lhsT, rhs, start=True, stop=True, inc=None, tp=None):
        self.op("tensor", "matmul", out, lhsT, rhs, start=start, stop=stop,
                skip_group_check=True, tile_position=tp, inc=inc)

    def barrier(self, sems):
        for e in ENGINES:
            for s in sems:
                self.wait(e, s)

    def simulate(self):
        pos = {e: 0 for e in ENGINES}
        val = {s.name: 0 for s in self.sems}
        total = sum(len(r) for r in self.recs.values())
        done = 0
        while done < total:
            progressed = False
            for e in ENGINES:
                while pos[e] < len(self.recs[e]):
                    waits, meth, args, kw, incs = self.recs[e][pos[e]]
                    if any(val[s.name] < v for s, v in waits):
                        break
                    for s, k in incs:
                        val[s.name] += k
                    pos[e] += 1
                    done += 1
                    progressed = True
            if not progressed:
                lines = []
                for e in ENGINES:
                    if pos[e] < len(self.recs[e]):
                        waits, meth, _, _, _ = self.recs[e][pos[e]]
                        ws = [(s.name, v, val[s.name]) for s, v in waits
                              if val[s.name] < v]
                        lines.append(f"{e}@{pos[e]}: {meth} blocked on {ws}")
                raise RuntimeError("DEADLOCK:\n" + "\n".join(lines))

    def replay(self, block):
        for e in ENGINES:
            recs = self.recs[e]

            def make(engine_recs):
                def emit(eng):
                    for waits, meth, args, kw, incs in engine_recs:
                        ww = {}
                        for s, v in waits:
                            if s.name in ww:
                                ww[s.name] = (s, max(v, ww[s.name][1]))
                            else:
                                ww[s.name] = (s, v)
                        for s, v in ww.values():
                            eng.wait_ge(s.h, v)
                        inst = getattr(eng, meth)(*args, **kw)
                        for s, k in incs:
                            inst = inst.then_inc(s.h, k)
                return emit
            getattr(block, e)(make(recs))


class Arena:
    def __init__(self, nc, nbytes=212608):
        self.t = nc.alloc_sbuf_tensor("arena", [128, nbytes], mybir.dt.uint8)
        self.nbytes = nbytes
        self.off = 0
        self.peak = 0

    def take(self, shape, dtype, name=""):
        esz = 4 if dtype == F32 else 2
        free = int(np.prod(shape[1:]))
        nb = free * esz
        off = (self.off + 31) // 32 * 32
        assert off + nb <= self.nbytes, \
            f"SBUF arena overflow at {name}: need {off + nb}, have {self.nbytes}"
        self.off = off + nb
        self.peak = max(self.peak, self.off)
        v = self.t.ap()[:, off:off + nb].bitcast(dtype)
        if len(shape) > 2:
            dims = " ".join(f"d{i}" for i in range(1, len(shape)))
            v = v.rearrange(f"p ({dims}) -> p {dims}",
                            **{f"d{i}": int(shape[i]) for i in range(1, len(shape))})
        if shape[0] != 128:
            v = v[0:shape[0]]
        return v

    def mark(self):
        return self.off

    def reset(self, mark):
        self.off = mark



def _evens_odds_perm():
    perm = np.zeros(D, dtype=np.int64)
    for h in range(H):
        perm[h * 64:h * 64 + 32] = h * 64 + np.arange(0, 64, 2)
        perm[h * 64 + 32:h * 64 + 64] = h * 64 + np.arange(1, 64, 2)
    return perm


def host_prep(inputs):
    g = {}
    T = lambda a: np.ascontiguousarray(np.asarray(a, dtype=np.float32).T)
    C = lambda a: np.ascontiguousarray(np.asarray(a, dtype=np.float32))
    g["wq"] = T(inputs["L0_attn_Wq"])
    g["wk"] = T(inputs["L0_attn_Wk"])
    g["wv"] = T(inputs["L0_attn_Wv"])
    g["w1"] = C(inputs["L0_ffn_W1"])
    g["w2"] = C(inputs["L0_ffn_W2"])
    g["b1c"] = np.ascontiguousarray(
        np.asarray(inputs["L0_ffn_b1"], np.float32).reshape(NI, 128).T)
    g["b2c"] = np.ascontiguousarray(
        np.asarray(inputs["L0_ffn_b2"], np.float32).reshape(NCH, 128).T)

    wqkv = T(inputs["L1_attn_w_qkv"])
    perm = _evens_odds_perm()
    g["wq1"] = np.ascontiguousarray(wqkv[:, 0:D][:, perm])
    g["wk1"] = np.ascontiguousarray(wqkv[:, D:2 * D][:, perm])
    g["wv1"] = np.ascontiguousarray(wqkv[:, 2 * D:3 * D])
    g["wout1"] = T(inputs["L1_attn_w_out"])
    g["boutc"] = np.ascontiguousarray(
        np.asarray(inputs["L1_attn_b_out"], np.float32).reshape(NCH, 128).T)
    g["wfc"] = T(inputs["L1_ffn_w_fc"])
    g["wproj"] = T(inputs["L1_ffn_w_proj"])
    g["uT"] = T(inputs["L1_ffn_u"])
    g["vmat"] = C(inputs["L1_ffn_v"])
    g["fbT"] = T(inputs["L1_ffn_fast_b"])
    g["gatec"] = np.full((16, 1), float(np.asarray(inputs["L1_ffn_gate"])),
                         dtype=np.float32)

    inv_freq = 1.0 / 10000.0 ** (np.arange(0, HD, 2, dtype=np.float32) / HD)
    t = np.arange(S, dtype=np.float32)
    fr = np.outer(t, inv_freq)
    cos = np.cos(fr).astype(np.float32).T
    sin = np.sin(fr).astype(np.float32).T
    g["t1"] = np.ascontiguousarray(np.vstack([cos, cos, cos, cos]))
    g["t2"] = np.ascontiguousarray(np.vstack([-sin, sin, -sin, sin]))
    sw = np.zeros((128, 128), dtype=np.float32)
    for blk in range(4):
        b = blk * 32
        dst = (b + 32) % 64 + (b // 64) * 64
        for i in range(32):
            sw[dst + i, b + i] = 1.0
    g["pswap"] = sw
    return g



class Ctx:
    pass


def setup_common(P, A, es):
    C = Ctx()
    nc = P.nc
    C.psum = es.enter_context(nc.psum_tensor("psum", [128, 4096], F32))
    C.bank = [C.psum[:, 512 * i:512 * (i + 1)] for i in range(8)]
    C.bank_free = [(None, 0)] * 8
    C.dw = P.sem("dw", es)
    C.dx = P.sem("dx", es)
    C.pe = P.sem("pe", es)
    C.ve = P.sem("ve", es)
    C.se = P.sem("se", es)
    C.ge = P.sem("ge", es)
    C.all_sems = [C.dw, C.dx, C.pe, C.ve, C.se, C.ge]
    C.ones_bf_col = A.take([128, 1], BF16, "ones_col")
    C.ones_f_row = A.take([1, 128], F32, "ones_f")
    C.ones_bf_row = A.take([1, 128], BF16, "ones_bfr")
    C.half_f_row = A.take([1, 128], F32, "half_f")
    C.nhalf_f_row = A.take([1, 128], F32, "nhalf_f")
    C.eps_t = A.take([1, 1], F32, "eps")
    C.ones2_bf = A.take([65, 128], BF16, "ones2")
    P.op("gpsimd", "memset", C.ones_bf_col, 1.0)
    P.op("gpsimd", "memset", C.ones_f_row, 1.0)
    P.op("gpsimd", "memset", C.ones_bf_row, 1.0)
    P.op("gpsimd", "memset", C.half_f_row, 0.5)
    P.op("gpsimd", "memset", C.eps_t, 1e-5)
    P.op("gpsimd", "memset", C.ones2_bf, 1.0)
    P.op("gpsimd", "memset", C.nhalf_f_row, -0.5, inc=C.ge)
    return C


def use_bank(P, C, i):
    s, n = C.bank_free[i]
    if s is not None:
        P.wait("tensor", s, n)


def release_bank(C, i, sem):
    C.bank_free[i] = (sem, sem.n)


def emit_ln(P, A, C, x_sb, xn_out):
    m0 = A.mark()
    xb = A.take([128, NCH, S], BF16, "ln_xb")
    sums = A.take([1, 2, S], F32, "ln_sums")
    mneg = A.take([1, S], F32, "ln_mneg")
    rstd = A.take([1, S], F32, "ln_rstd")
    mneg_b = A.take([128, S], F32, "ln_mnegb")
    rstd_b = A.take([128, S], F32, "ln_rstdb")
    scrf = A.take([128, S], F32, "ln_scr")
    bs, bb = 7, 6

    for c in range(NCH):
        P.op("vector", "tensor_copy", xb[:, c, :], x_sb[:, c, :], inc=C.ve)
    for tb in range(S // 512):
        use_bank(P, C, bs)
        P.wait("tensor", C.ve)
        for c in range(NCH):
            P.mm(C.bank[bs][0:1, :], C.ones_bf_col, xb[:, c, bass.ts(tb, 512)],
                 start=(c == 0), stop=(c == NCH - 1),
                 inc=C.pe if c == NCH - 1 else None)
        P.wait("vector", C.pe)
        P.op("vector", "tensor_copy", sums[0:1, 0, bass.ts(tb, 512)],
             C.bank[bs][0:1, :], inc=C.ve)
        release_bank(C, bs, C.ve)
    for c in range(NCH):
        P.wait("scalar", C.pe)
        P.op("scalar", "activation", xb[:, c, :], xb[:, c, :], AF.Square,
             inc=C.se)
    for tb in range(S // 512):
        use_bank(P, C, bs)
        P.wait("tensor", C.se)
        for c in range(NCH):
            P.mm(C.bank[bs][0:1, :], C.ones_bf_col, xb[:, c, bass.ts(tb, 512)],
                 start=(c == 0), stop=(c == NCH - 1),
                 inc=C.pe if c == NCH - 1 else None)
        P.wait("vector", C.pe)
        P.op("vector", "tensor_copy", sums[0:1, 1, bass.ts(tb, 512)],
             C.bank[bs][0:1, :], inc=C.ve)
        release_bank(C, bs, C.ve)
    P.op("vector", "tensor_scalar", mneg, sums[0:1, 0, :], -1.0 / D, None, ALU.mult)
    P.op("vector", "tensor_scalar", rstd, sums[0:1, 0, :], 1.0 / D, None, ALU.mult)
    P.op("vector", "tensor_mul", rstd, rstd, rstd)
    P.op("vector", "scalar_tensor_tensor", rstd, sums[0:1, 1, :], 1.0 / D, rstd,
         ALU.mult, ALU.subtract, inc=C.ve)
    P.wait("scalar", C.ve)
    P.op("scalar", "activation", rstd, rstd, AF.Sqrt, bias=C.eps_t, inc=C.se)
    P.wait("vector", C.se)
    P.op("vector", "reciprocal", rstd, rstd, inc=C.ve)
    for tb in range(S // 512):
        use_bank(P, C, bb)
        P.wait("tensor", C.ve)
        P.mm(C.bank[bb][:, :], C.ones_f_row, mneg[0:1, bass.ts(tb, 512)], inc=C.pe)
        P.wait("vector", C.pe)
        P.op("vector", "tensor_copy", mneg_b[:, bass.ts(tb, 512)], C.bank[bb][:, :],
             inc=C.ve)
        P.wait("tensor", C.ve)
        P.mm(C.bank[bb][:, :], C.ones_f_row, rstd[0:1, bass.ts(tb, 512)], inc=C.pe)
        P.wait("vector", C.pe)
        P.op("vector", "tensor_copy", rstd_b[:, bass.ts(tb, 512)], C.bank[bb][:, :],
             inc=C.ve)
        release_bank(C, bb, C.ve)
    for c in range(NCH):
        P.op("vector", "tensor_add", scrf, x_sb[:, c, :], mneg_b)
        P.op("vector", "tensor_mul", xn_out[:, c, :], scrf, rstd_b, inc=C.ve)
    A.reset(m0)


def emit_ffn(P, A, C, xn, w1_sb, w2_sb, g_hbm, layer, b1c=None, b2c=None):
    m0 = A.mark()
    rr = A.take([128, NI, TQB], BF16, "rr")
    gtile = A.take([128, NCH, TQB], BF16, "gtile")
    scrf = A.take([128, TQB], F32, "ffn_scr")
    f1b = [0, 1]
    f2b = [2, 3, 4, 5]
    gtile_free = C.dx.n
    for tb in range(NFB):
        for c2 in range(NI):
            bk = f1b[c2 % 2]
            use_bank(P, C, bk)
            for ci in range(NCH):
                P.mm(C.bank[bk][:, 0:TQB], w1_sb[:, ci, bass.ts(c2, 128)],
                     xn[:, ci, bass.ts(tb, TQB)],
                     start=(ci == 0), stop=(ci == NCH - 1),
                     inc=C.pe if ci == NCH - 1 else None)
            if layer == 0:
                P.wait("vector", C.pe)
                P.op("vector", "tensor_scalar", rr[:, c2, :], C.bank[bk][:, 0:TQB],
                     b1c[:, c2:c2 + 1], 0.0, ALU.add, ALU.max, inc=C.ve)
                release_bank(C, bk, C.ve)
            else:
                P.wait("vector", C.pe)
                P.op("vector", "tensor_scalar", scrf, C.bank[bk][:, 0:TQB],
                     0.0, None, ALU.max, inc=C.ve)
                release_bank(C, bk, C.ve)
                P.wait("scalar", C.ve)
                P.op("scalar", "activation", rr[:, c2, :], scrf, AF.Square,
                     inc=C.se)
        for oc in range(NCH):
            bk = f2b[oc % 4]
            use_bank(P, C, bk)
            if layer == 1:
                P.wait("tensor", C.se)
            else:
                P.wait("tensor", C.ve)
            for c2 in range(NI):
                P.mm(C.bank[bk][:, 0:TQB], w2_sb[:, c2, bass.ts(oc, 128)],
                     rr[:, c2, :],
                     start=(c2 == 0), stop=(c2 == NI - 1),
                     inc=C.pe if c2 == NI - 1 else None)
            P.wait("scalar", C.pe)
            P.wait("scalar", C.dx, gtile_free)
            if layer == 0:
                P.op("scalar", "activation", gtile[:, oc, :], C.bank[bk][:, 0:TQB],
                     AF.Silu, bias=b2c[:, oc:oc + 1], inc=C.se)
            else:
                P.op("scalar", "activation", gtile[:, oc, :], C.bank[bk][:, 0:TQB],
                     AF.Silu, inc=C.se)
            release_bank(C, bk, C.se)
        P.wait("sync", C.se)
        P.dma("sync", g_hbm[:, :, bass.ts(tb, TQB)].rearrange("c p t -> p c t"),
              gtile, C.dx)
        gtile_free = C.dx.n
    A.reset(m0)


def emit_qk_proj_l0(P, A, C, xn, w_sb, out_bf):
    bks = [0, 1]
    free_i = 0
    for oc in range(NCH):
        for tb in range(S // 512):
            bk = bks[free_i % 2]
            free_i += 1
            use_bank(P, C, bk)
            for ci in range(NCH):
                P.mm(C.bank[bk][:, :], w_sb[:, ci, bass.ts(oc, 128)],
                     xn[:, ci, bass.ts(tb, 512)],
                     start=(ci == 0), stop=(ci == NCH - 1),
                     inc=C.pe if ci == NCH - 1 else None)
            P.wait("vector", C.pe)
            P.op("vector", "tensor_copy", out_bf[:, oc, bass.ts(tb, 512)],
                 C.bank[bk][:, :], inc=C.ve)
            release_bank(C, bk, C.ve)


def emit_qk_proj_rope(P, A, C, xn, w_sb, out_bf, t1, t2, pswap_bf):
    m0 = A.mark()
    qpre = A.take([128, 512], BF16, "rope_qpre")
    s1 = A.take([128, 512], F32, "rope_s1")
    s2 = A.take([128, 512], F32, "rope_s2")
    pi = 0
    for oc in range(NCH):
        for tb in range(S // 512):
            bk = [0, 1][pi % 2]
            bk2 = [2, 3][pi % 2]
            pi += 1
            use_bank(P, C, bk)
            for ci in range(NCH):
                P.mm(C.bank[bk][:, :], w_sb[:, ci, bass.ts(oc, 128)],
                     xn[:, ci, bass.ts(tb, 512)],
                     start=(ci == 0), stop=(ci == NCH - 1),
                     inc=C.pe if ci == NCH - 1 else None)
            P.wait("vector", C.pe)
            P.op("vector", "tensor_copy", qpre, C.bank[bk][:, :], inc=C.ve)
            release_bank(C, bk, C.ve)
            use_bank(P, C, bk2)
            P.wait("tensor", C.ve)
            P.mm(C.bank[bk2][:, :], pswap_bf, qpre, inc=C.pe)
            P.op("vector", "tensor_mul", s1, qpre, t1[:, bass.ts(tb, 512)])
            P.wait("vector", C.pe)
            P.op("vector", "tensor_mul", s2, C.bank[bk2][:, :],
                 t2[:, bass.ts(tb, 512)], inc=C.ve)
            release_bank(C, bk2, C.ve)
            P.op("vector", "tensor_add", out_bf[:, oc, bass.ts(tb, 512)], s1, s2,
                 inc=C.ve)
    A.reset(m0)


def emit_v_proj(P, A, C, xn, wv_sb, v_aug, col_off=0):
    pi = 0
    for tk in range(NTK):
        for ob in range(2):
            bk = [0, 1][pi % 2]
            pi += 1
            use_bank(P, C, bk)
            for ci in range(NCH):
                P.mm(C.bank[bk][:, :], xn[:, ci, bass.ts(tk, 128)],
                     wv_sb[:, ci, col_off + ob * 512:col_off + (ob + 1) * 512],
                     start=(ci == 0), stop=(ci == NCH - 1),
                     inc=C.pe if ci == NCH - 1 else None)
            P.wait("vector", C.pe)
            P.op("vector", "tensor_copy",
                 v_aug[:, tk, ob * 8:(ob + 1) * 8, 0:64],
                 C.bank[bk][:, :].rearrange("p (h d) -> p h d", h=8), inc=C.ve)
            release_bank(C, bk, C.ve)


def emit_mshift(P, A, C, q_bf, hp, mshift, kmaxs, qsq_scr, msrow, inv_scale):
    bq = 7
    P.wait("scalar", C.pe)
    P.op("scalar", "activation", qsq_scr, q_bf[:, hp, :], AF.Square, inc=C.se)
    for side, base in ((0, 0), (1, 64)):
        h = 2 * hp + side
        use_bank(P, C, bq)
        P.wait("tensor", C.se)
        for tb in range(S // 512):
            P.mm(C.bank[bq][0:1, :], C.ones_bf_col[base:base + 64, :],
                 qsq_scr[base:base + 64, bass.ts(tb, 512)], inc=C.pe)
            P.wait("vector", C.pe)
            P.op("vector", "tensor_copy", msrow[0:1, bass.ts(tb, 512)],
                 C.bank[bq][0:1, :], inc=C.ve)
            P.wait("tensor", C.ve)
        release_bank(C, bq, C.ve)
        P.wait("scalar", C.ve)
        P.op("scalar", "activation", msrow, msrow, AF.Sqrt, inc=C.se)
        P.wait("vector", C.se)
        P.op("vector", "tensor_scalar", msrow, msrow,
             kmaxs[0:1, h:h + 1], -64.0, ALU.mult, ALU.add)
        P.op("vector", "tensor_scalar", msrow, msrow, 0.0, -float(inv_scale),
             ALU.max, ALU.mult)
        P.op("vector", "tensor_copy", mshift[base:base + 1, :], msrow, inc=C.ve)


def emit_kmaxs(P, A, C, k_bf, kmaxs, scr, row, inv_scale):
    m0 = A.mark()
    one = A.take([1, 1], F32, "km_one")
    for hp in range(NCH):
        P.wait("scalar", C.pe)
        P.wait("scalar", C.ve)
        P.op("scalar", "activation", scr, k_bf[:, hp, :], AF.Square, inc=C.se)
        for side, base in ((0, 0), (1, 64)):
            h = 2 * hp + side
            bq = 6 + side
            use_bank(P, C, bq)
            P.wait("tensor", C.se)
            for tb in range(S // 512):
                P.mm(C.bank[bq][0:1, :], C.ones_bf_col[base:base + 64, :],
                     scr[base:base + 64, bass.ts(tb, 512)], inc=C.pe)
                P.wait("vector", C.pe)
                P.op("vector", "tensor_copy", row[0:1, bass.ts(tb, 512)],
                     C.bank[bq][0:1, :], inc=C.ve)
                P.wait("tensor", C.ve)
            release_bank(C, bq, C.ve)
            P.op("vector", "reduce_max", one, row, mybir.AxisListType.X, inc=C.ve)
            P.wait("scalar", C.ve)
            P.op("scalar", "activation", one, one, AF.Sqrt, inc=C.se)
            P.wait("vector", C.se)
            P.op("vector", "tensor_scalar", kmaxs[0:1, h:h + 1], one,
                 1.0 / float(inv_scale), None, ALU.mult, inc=C.ve)
    A.reset(m0)


def emit_attention(P, A, C, q_bf, k_bf, v_aug, scale, consume, shifted=False):
    m0 = A.mark()
    recips = A.take([1, 2, 512], F32, "recips")
    rb = A.take([128, 512], F32, "rb")
    s_blk = A.take([128, 512], F32, "s_blk")
    if shifted:
        inv_scale = 1.0 / scale
        mshift = A.take([65, S], BF16, "mshift")
        kmaxs = A.take([1, H], F32, "kmaxs")
        qsq_scr = A.take([128, S], BF16, "qsq")
        msrow = A.take([1, S], F32, "msrow")
        emit_kmaxs(P, A, C, k_bf, kmaxs, qsq_scr, msrow, inv_scale)
    rtmp = A.take([1, 2, 512], F32, "rtmp")
    ptab = [A.take([128, 1024], BF16, "ptab0"),
            A.take([128, 1024], BF16, "ptab1")]
    pv_pairs = [(4, 5), (6, 7)]

    def emit_scores(hp, tq, tk):
        p = tk % 2
        bA, bB = 2 * p, 2 * p + 1
        use_bank(P, C, bA)
        use_bank(P, C, bB)
        if shifted:
            P.mm(C.bank[bA], C.ones2_bf[0:1, :], mshift[0:1, tq],
                 start=True, stop=False)
            P.mm(C.bank[bB], C.ones2_bf[64:65, :], mshift[64:65, tq],
                 start=True, stop=False)
            P.mm(C.bank[bA], k_bf[0:64, hp, bass.ts(tk, 128)],
                 q_bf[0:64, hp, tq], start=False, stop=True, tp=(0, 0))
            P.mm(C.bank[bB], k_bf[64:128, hp, bass.ts(tk, 128)],
                 q_bf[64:128, hp, tq], start=False, stop=True, tp=(64, 0),
                 inc=C.pe)
        else:
            P.mm(C.bank[bA], k_bf[0:64, hp, bass.ts(tk, 128)],
                 q_bf[0:64, hp, tq], start=True, stop=True, tp=(0, 0))
            P.mm(C.bank[bB], k_bf[64:128, hp, bass.ts(tk, 128)],
                 q_bf[64:128, hp, tq], start=True, stop=True, tp=(64, 0),
                 inc=C.pe)

    def emit_exp(tk):
        p = tk % 2
        P.wait("scalar", C.pe)
        P.op("scalar", "activation", ptab[p],
             C.psum[:, 1024 * p:1024 * (p + 1)], AF.Exp, scale=scale,
             inc=C.se)
        release_bank(C, 2 * p, C.se)
        release_bank(C, 2 * p + 1, C.se)
        return C.se.n

    def emit_pv(hp, tk, se_need, bOA, bOB):
        p = tk % 2
        P.wait("tensor", C.se, se_need)
        P.mm(C.bank[bOA][0:65, :], v_aug[:, tk, 2 * hp, :], ptab[p][:, 0:512],
             start=(tk == 0), stop=(tk == NTK - 1))
        P.mm(C.bank[bOB][0:65, :], v_aug[:, tk, 2 * hp + 1, :],
             ptab[p][:, 512:1024],
             start=(tk == 0), stop=(tk == NTK - 1), inc=C.pe)

    pending = [None]

    def norm_part2(hp, tqb, bOA, bOB):
        use_bank(P, C, bOA)
        use_bank(P, C, bOB)
        P.wait("tensor", C.ve)
        P.mm(C.bank[bOA][0:64, :], C.ones_f_row[0:1, 0:64], recips[0:1, 0, :])
        P.mm(C.bank[bOB][0:64, :], C.ones_f_row[0:1, 0:64], recips[0:1, 1, :],
             inc=C.pe)
        P.wait("vector", C.pe)
        P.op("vector", "tensor_copy", rb[0:64, :], C.bank[bOA][0:64, :])
        P.op("vector", "tensor_copy", rb[64:128, :], C.bank[bOB][0:64, :],
             inc=C.ve)
        release_bank(C, bOA, C.ve)
        release_bank(C, bOB, C.ve)
        P.op("vector", "tensor_mul", s_blk[0:64, :], s_blk[0:64, :],
             rb[0:64, :])
        P.op("vector", "tensor_mul", s_blk[64:128, :], s_blk[64:128, :],
             rb[64:128, :], inc=C.ve)
        consume(hp, tqb, s_blk)

    for hp in range(NCH):
        if shifted:
            emit_mshift(P, A, C, q_bf, hp, mshift, kmaxs, qsq_scr, msrow,
                        inv_scale)
            P.wait("tensor", C.ve)
        for tqb in range(S // 512):
            tq = bass.ts(tqb, 512)
            it = hp * 4 + tqb
            bOA, bOB = pv_pairs[it % 2]
            se_at = {}
            for tk in range(NTK):
                emit_scores(hp, tq, tk)
                if tk == 0:
                    use_bank(P, C, bOA)
                    use_bank(P, C, bOB)
                se_at[tk] = emit_exp(tk)
                if tk > 0:
                    emit_pv(hp, tk - 1, se_at[tk - 1], bOA, bOB)
                if tk == 10 and pending[0] is not None:
                    norm_part2(*pending[0])
                    pending[0] = None
            emit_pv(hp, NTK - 1, se_at[NTK - 1], bOA, bOB)
            P.wait("vector", C.pe)
            P.op("vector", "tensor_scalar", rtmp[0:1, 0, :],
                 C.bank[bOA][64:65, :], 1e-36, None, ALU.add)
            P.op("vector", "tensor_scalar", rtmp[0:1, 1, :],
                 C.bank[bOB][64:65, :], 1e-36, None, ALU.add)
            P.op("vector", "tensor_copy", s_blk[0:64, :], C.bank[bOA][0:64, :])
            P.op("vector", "tensor_copy", s_blk[64:128, :],
                 C.bank[bOB][0:64, :], inc=C.ve)
            release_bank(C, bOA, C.ve)
            release_bank(C, bOB, C.ve)
            P.op("vector", "reciprocal", recips[0:1, 0, :], rtmp[0:1, 0, :])
            P.op("vector", "reciprocal", recips[0:1, 1, :], rtmp[0:1, 1, :],
                 inc=C.ve)
            pending[0] = (hp, tqb, bOA, bOB)
    norm_part2(*pending[0])
    pending[0] = None
    A.reset(m0)


def emit_ternarize(P, A, C, w_hbm, n_in, n_out, weff_sb, tag):
    m0 = A.mark()
    nic = n_in // 128
    nob = n_out // 512
    chunk_f = A.take([128, n_out], F32, f"tn_cf_{tag}")
    chunk_abs = A.take([128, n_out], BF16, f"tn_ca_{tag}")
    srow = A.take([1, n_out], F32, f"tn_sr_{tag}")
    bh = A.take([128, n_out], F32, f"tn_bh_{tag}")
    sbc = A.take([128, n_out], BF16, f"tn_sbc_{tag}")
    mneg = A.take([128, n_out], BF16, f"tn_mn_{tag}")
    w_r = w_hbm.rearrange("(c p) o -> p c o", p=128)
    for c in range(nic):
        P.wait("sync", C.se)
        P.dma("sync", chunk_f, w_r[:, c, :], C.dx)
        P.wait("scalar", C.dx)
        P.wait("scalar", C.pe)
        P.op("scalar", "activation", chunk_abs, chunk_f, AF.Abs, inc=C.se)
        P.wait("tensor", C.se)
        for ob in range(nob):
            if c == 0:
                use_bank(P, C, ob)
            P.mm(C.bank[ob][0:1, :], C.ones_bf_col, chunk_abs[:, bass.ts(ob, 512)],
                 start=(c == 0), stop=(c == nic - 1),
                 inc=C.pe if ob == nob - 1 else None)
    P.wait("vector", C.pe)
    for ob in range(nob):
        P.op("vector", "tensor_scalar", srow[0:1, bass.ts(ob, 512)],
             C.bank[ob][0:1, :], 1.0 / n_in, 1e-5, ALU.mult, ALU.max,
             inc=C.ve if ob == nob - 1 else None)
    for ob in range(nob):
        release_bank(C, ob, C.ve)
    for ob in range(nob):
        bk = ob % 2
        use_bank(P, C, bk)
        P.wait("tensor", C.ve)
        P.mm(C.bank[bk][:, :], C.half_f_row, srow[0:1, bass.ts(ob, 512)],
             inc=C.pe)
        P.wait("vector", C.pe)
        P.op("vector", "tensor_copy", bh[:, bass.ts(ob, 512)], C.bank[bk][:, :],
             inc=C.ve)
        release_bank(C, bk, C.ve)
        bk = (ob + 1) % 2
        use_bank(P, C, bk)
        P.wait("tensor", C.ve)
        P.mm(C.bank[bk][:, :], C.ones_f_row, srow[0:1, bass.ts(ob, 512)],
             inc=C.pe)
        P.wait("vector", C.pe)
        P.op("vector", "tensor_copy", sbc[:, bass.ts(ob, 512)], C.bank[bk][:, :],
             inc=C.ve)
        release_bank(C, bk, C.ve)
    for c in range(nic):
        P.wait("sync", C.ve)
        P.dma("sync", chunk_f, w_r[:, c, :], C.dx)
        P.wait("vector", C.dx)
        wch = weff_sb[:, c, :]
        P.op("vector", "tensor_tensor", wch, chunk_f, bh, ALU.is_ge)
        P.op("vector", "scalar_tensor_tensor", mneg, chunk_f, -1.0, bh,
             ALU.mult, ALU.is_ge)
        P.op("vector", "tensor_sub", wch, wch, mneg)
        P.op("vector", "tensor_mul", wch, wch, sbc, inc=C.ve)
    A.reset(m0)


def emit_lowrank_fold(P, A, C, io, weff_fc):
    m0 = A.mark()
    u_sb = A.take([16, INTER], BF16, "lr_u")
    v_sb = A.take([16, D], BF16, "lr_v")
    fb_sb = A.take([16, RANK], F32, "lr_fb")
    gate_c = A.take([16, 1], F32, "lr_gate")
    gfb = A.take([16, RANK], BF16, "lr_gfb")
    at = A.take([16, D], BF16, "lr_at")
    P.dma("gpsimd", u_sb, io.w["uT"], C.dw)
    P.dma("gpsimd", v_sb, io.w["vmat"], C.dw)
    P.dma("gpsimd", fb_sb, io.w["fbT"], C.dw)
    P.dma("gpsimd", gate_c, io.w["gatec"], C.dw)
    P.wait("vector", C.dw)
    P.op("vector", "tensor_scalar", gfb, fb_sb, gate_c, None, ALU.mult, inc=C.ve)
    use_bank(P, C, 0)
    use_bank(P, C, 1)
    P.wait("tensor", C.ve)
    P.mm(C.bank[0][0:16, :], gfb, v_sb[:, 0:512], start=True, stop=True)
    P.mm(C.bank[1][0:16, :], gfb, v_sb[:, 512:1024], start=True, stop=True,
         inc=C.pe)
    P.wait("vector", C.pe)
    P.op("vector", "tensor_copy", at[:, 0:512], C.bank[0][0:16, :])
    P.op("vector", "tensor_copy", at[:, 512:1024], C.bank[1][0:16, :], inc=C.ve)
    release_bank(C, 0, C.ve)
    release_bank(C, 1, C.ve)
    for ci in range(NCH):
        for ob in range(INTER // 512):
            bk = ob % 2
            use_bank(P, C, bk)
            P.wait("tensor", C.ve)
            P.mm(C.bank[bk][:, :], at[:, bass.ts(ci, 128)],
                 u_sb[:, bass.ts(ob, 512)], inc=C.pe)
            P.wait("vector", C.pe)
            P.op("vector", "tensor_add", weff_fc[:, ci, bass.ts(ob, 512)],
                 weff_fc[:, ci, bass.ts(ob, 512)], C.bank[bk][:, :], inc=C.ve)
            release_bank(C, bk, C.ve)
    A.reset(m0)



def emit_layer0(P, A, C, io, x_src, x_dst):
    m_layer = A.mark()
    xn = A.take([128, NCH, S], BF16, "xn0")
    x_src_r = x_src.rearrange("(c p) t -> p c t", p=128)
    x_dst_r = x_dst.rearrange("(c p) t -> c p t", p=128)
    g_hbm = io.g0

    m0 = A.mark()
    x_sb = A.take([128, NCH, S], F32, "x0")
    P.dma("sync", x_sb, x_src_r, C.dx)
    P.wait("vector", C.dx)
    P.wait("vector", C.ge)
    P.wait("tensor", C.ge)
    emit_ln(P, A, C, x_sb, xn)
    A.reset(m0)
    P.barrier(C.all_sems)

    m1 = A.mark()
    w1_sb = A.take([128, NCH, INTER], BF16, "w1")
    w2_sb = A.take([128, NI, D], BF16, "w2")
    b1c = A.take([128, NI], F32, "b1c")
    b2c = A.take([128, NCH], F32, "b2c")
    P.dma("gpsimd", w1_sb, io.w["w1"].rearrange("(c p) o -> p c o", p=128), C.dw)
    P.dma("gpsimd", w2_sb, io.w["w2"].rearrange("(c p) o -> p c o", p=128), C.dw)
    P.dma("sync", b1c, io.w["b1c"], C.dx)
    P.dma("sync", b2c, io.w["b2c"], C.dx)
    P.wait("tensor", C.dw)
    P.wait("vector", C.dx)
    P.wait("scalar", C.dx)
    emit_ffn(P, A, C, xn, w1_sb, w2_sb, g_hbm, 0, b1c, b2c)
    A.reset(m1)
    P.barrier(C.all_sems)

    q_bf = A.take([128, NCH, S], BF16, "q0")
    k_bf = A.take([128, NCH, S], BF16, "k0")
    v_aug = A.take([128, NTK, H, 65], BF16, "v0")
    m2 = A.mark()
    wslot = [A.take([128, NCH, D], BF16, "ws0"),
             A.take([128, NCH, D], BF16, "ws1")]
    P.op("gpsimd", "memset", v_aug[:, :, :, 64:65], 1.0, inc=C.ge)
    P.dma("gpsimd", wslot[0], io.w["wq"].rearrange("(c p) o -> p c o", p=128), C.dw)
    P.wait("tensor", C.dw)
    emit_qk_proj_l0(P, A, C, xn, wslot[0], q_bf)
    P.dma("gpsimd", wslot[1], io.w["wk"].rearrange("(c p) o -> p c o", p=128), C.dw)
    P.wait("tensor", C.dw)
    emit_qk_proj_l0(P, A, C, xn, wslot[1], k_bf)
    P.wait("gpsimd", C.pe)
    P.dma("gpsimd", wslot[0], io.w["wv"].rearrange("(c p) o -> p c o", p=128), C.dw)
    P.wait("tensor", C.dw)
    P.wait("tensor", C.ge)
    emit_v_proj(P, A, C, xn, wslot[0], v_aug)
    A.reset(m2)
    P.barrier(C.all_sems)
    if io.dbg:
        P.dma("sync", io.d_xn, xn, C.dx)
        P.dma("sync", io.d_q, q_bf, C.dx)
        P.dma("sync", io.d_k, k_bf, C.dx)
        P.dma("sync", io.d_v, v_aug, C.dx)

    xst = A.take([128, 2, 512], F32, "xstage")
    gst = A.take([128, 2, 512], BF16, "gstage")
    stage_free = [(None, 0), (None, 0)]
    g_hbm_r = g_hbm
    x_src_cr = x_src.rearrange("(c p) t -> c p t", p=128)

    def consume0(hp, tqb, s_blk):
        sl = (hp * 4 + tqb) % 2
        if io.dbg:
            P.wait("sync", C.ve)
            P.dma("sync", io.d_s[hp, :, bass.ts(tqb, 512)], s_blk, C.dx)
            P.wait("vector", C.dx)
        P.wait("vector", C.dx, stage_issue[(hp, tqb)])
        P.op("vector", "tensor_mul", s_blk, s_blk, gst[:, sl, :])
        P.op("vector", "tensor_add", xst[:, sl, :], xst[:, sl, :], s_blk,
             inc=C.ve)
        P.wait("sync", C.ve)
        P.dma("sync", x_dst_r[hp, :, bass.ts(tqb, 512)], xst[:, sl, :], C.dx)
        stage_free[sl] = (C.dx, C.dx.n)

    stage_issue = {}

    def consume_wrap(hp, tqb, s_blk):
        consume0(hp, tqb, s_blk)
        nxt = hp * 4 + tqb + 1
        if nxt < 32:
            issue_stage(nxt // 4, nxt % 4)

    def issue_stage(hp, tqb):
        sl = (hp * 4 + tqb) % 2
        s, n = stage_free[sl]
        if s is not None:
            P.wait("sync", s, n)
        P.dma("sync", xst[:, sl, :], x_src_cr[hp, :, bass.ts(tqb, 512)], C.dx)
        P.dma("sync", gst[:, sl, :], g_hbm_r[hp, :, bass.ts(tqb, 512)], C.dx)
        stage_issue[(hp, tqb)] = C.dx.n

    issue_stage(0, 0)
    issue_stage(0, 1)
    emit_attention(P, A, C, q_bf, k_bf, v_aug, 1.0 / 32.0, consume_wrap)
    A.reset(m_layer)
    P.barrier(C.all_sems)


def emit_layer1(P, A, C, io, x_src, x_dst):
    m_layer = A.mark()
    xn = A.take([128, NCH, S], BF16, "xn1")
    x_src_r = x_src.rearrange("(c p) t -> p c t", p=128)
    x_src_cr = x_src.rearrange("(c p) t -> c p t", p=128)
    x_dst_r = x_dst.rearrange("(c p) t -> c p t", p=128)
    g_hbm = io.g1

    m0 = A.mark()
    x_sb = A.take([128, NCH, S], F32, "x1")
    P.dma("sync", x_sb, x_src_r, C.dx)
    P.wait("vector", C.dx)
    emit_ln(P, A, C, x_sb, xn)
    A.reset(m0)
    P.barrier(C.all_sems)

    weff_fc = A.take([128, NCH, INTER], BF16, "weff_fc")
    emit_ternarize(P, A, C, io.w["wfc"], D, INTER, weff_fc, "fc")
    P.barrier(C.all_sems)
    emit_lowrank_fold(P, A, C, io, weff_fc)
    P.barrier(C.all_sems)
    weff_pj = A.take([128, NI, D], BF16, "weff_pj")
    emit_ternarize(P, A, C, io.w["wproj"], INTER, D, weff_pj, "pj")
    P.barrier(C.all_sems)
    emit_ffn(P, A, C, xn, weff_fc, weff_pj, g_hbm, 1)
    A.reset(m_layer)
    xn = A.take([128, NCH, S], BF16, "xn1_alias")
    P.barrier(C.all_sems)

    t1 = A.take([128, S], BF16, "t1")
    t2 = A.take([128, S], BF16, "t2")
    pswap = A.take([128, 128], BF16, "pswap")
    attnout = A.take([128, NCH, S], BF16, "attnout")
    q_bf = A.take([128, NCH, S], BF16, "q1")
    k_bf = A.take([128, NCH, S], BF16, "k1")
    v_aug = A.take([128, NTK, H, 65], BF16, "v1")
    m2 = A.mark()
    wslot = A.take([128, NCH, D], BF16, "w1s0")
    P.dma("gpsimd", t1, io.w["t1"], C.dw)
    P.dma("gpsimd", t2, io.w["t2"], C.dw)
    P.dma("gpsimd", pswap, io.w["pswap"], C.dw)
    P.op("gpsimd", "memset", v_aug[:, :, :, 64:65], 1.0, inc=C.ge)
    P.dma("gpsimd", wslot, io.w["wq1"].rearrange("(c p) o -> p c o", p=128),
          C.dw)
    P.wait("tensor", C.dw)
    P.wait("vector", C.dw)
    emit_qk_proj_rope(P, A, C, xn, wslot, q_bf, t1, t2, pswap)
    P.wait("gpsimd", C.pe)
    P.dma("gpsimd", wslot, io.w["wk1"].rearrange("(c p) o -> p c o", p=128),
          C.dw)
    P.wait("tensor", C.dw)
    emit_qk_proj_rope(P, A, C, xn, wslot, k_bf, t1, t2, pswap)
    P.wait("gpsimd", C.pe)
    P.dma("gpsimd", wslot, io.w["wv1"].rearrange("(c p) o -> p c o", p=128),
          C.dw)
    P.wait("tensor", C.dw)
    P.wait("tensor", C.ge)
    emit_v_proj(P, A, C, xn, wslot, v_aug)
    A.reset(m2)
    P.barrier(C.all_sems)

    def consume1(hp, tqb, s_blk):
        P.op("vector", "tensor_copy", attnout[:, hp, bass.ts(tqb, 512)], s_blk,
             inc=C.ve)

    emit_attention(P, A, C, q_bf, k_bf, v_aug, 1.0 / 8.0, consume1, shifted=True)
    P.barrier(C.all_sems)

    m3 = A.mark()
    wout = A.take([128, NCH, D], BF16, "wout")
    boutc = A.take([128, NCH], F32, "boutc")
    xst = A.take([128, 2, 512], F32, "xst1")
    gst = A.take([128, 2, 512], BF16, "gst1")
    sblk = A.take([128, 512], F32, "sblk1")
    P.dma("gpsimd", wout, io.w["wout1"].rearrange("(c p) o -> p c o", p=128), C.dw)
    P.dma("sync", boutc, io.w["boutc"], C.dx)
    stage_free = [(None, 0), (None, 0)]
    stage_issue = {}

    def issue_stage(i):
        oc, tb = i // 4, i % 4
        sl = i % 2
        s, n = stage_free[sl]
        if s is not None:
            P.wait("sync", s, n)
        P.dma("sync", xst[:, sl, :], x_src_cr[oc, :, bass.ts(tb, 512)], C.dx)
        P.dma("sync", gst[:, sl, :], g_hbm[oc, :, bass.ts(tb, 512)], C.dx)
        stage_issue[i] = C.dx.n

    issue_stage(0)
    issue_stage(1)
    P.wait("tensor", C.dw)
    pi = 0
    for oc in range(NCH):
        for tb in range(S // 512):
            i = oc * 4 + tb
            sl = i % 2
            bk = [0, 1][pi % 2]
            pi += 1
            use_bank(P, C, bk)
            for ci in range(NCH):
                P.mm(C.bank[bk][:, :], wout[:, ci, bass.ts(oc, 128)],
                     attnout[:, ci, bass.ts(tb, 512)],
                     start=(ci == 0), stop=(ci == NCH - 1),
                     inc=C.pe if ci == NCH - 1 else None)
            P.wait("vector", C.pe)
            P.wait("vector", C.dx, stage_issue[i])
            P.op("vector", "tensor_scalar", sblk, C.bank[bk][:, :],
                 boutc[:, oc:oc + 1], None, ALU.add, inc=C.ve)
            release_bank(C, bk, C.ve)
            P.op("vector", "tensor_mul", sblk, sblk, gst[:, sl, :])
            P.op("vector", "tensor_add", xst[:, sl, :], xst[:, sl, :], sblk,
                 inc=C.ve)
            P.wait("sync", C.ve)
            P.dma("sync", x_dst_r[oc, :, bass.ts(tb, 512)], xst[:, sl, :], C.dx)
            stage_free[sl] = (C.dx, C.dx.n)
            if i + 2 < 32:
                issue_stage(i + 2)
    A.reset(m_layer)



W_L0 = ["wq", "wk", "wv", "w1", "w2", "b1c", "b2c"]
W_L1 = ["wq1", "wk1", "wv1", "wout1", "boutc", "wfc", "wproj", "uT", "vmat",
        "fbT", "gatec", "t1", "t2", "pswap"]


def build_program(layers=(0, 1), dbg=False):
    from contextlib import ExitStack
    nc = bass.Bass("TRN2", target_bir_lowering=False)
    io = Ctx()
    io.dbg = dbg
    io.x_in = nc.dram_tensor("x_in", [D, S], F32, kind="ExternalInput").ap()
    io.x_out = nc.dram_tensor("x_out", [D, S], F32, kind="ExternalOutput").ap()
    if layers == (0, 1):
        io.x_mid = nc.dram_tensor("x_mid", [D, S], F32, kind="Internal").ap()
    io.w = {}
    names = (W_L0 if 0 in layers else []) + (W_L1 if 1 in layers else [])
    shapes = dict(wq=(D, D), wk=(D, D), wv=(D, D), w1=(D, INTER), w2=(INTER, D),
                  b1c=(128, NI), b2c=(128, NCH), wq1=(D, D), wk1=(D, D),
                  wv1=(D, D), wout1=(D, D), boutc=(128, NCH), wfc=(D, INTER),
                  wproj=(INTER, D), uT=(RANK, INTER), vmat=(RANK, D),
                  fbT=(RANK, RANK), gatec=(16, 1), t1=(128, S), t2=(128, S),
                  pswap=(128, 128))
    for n in names:
        io.w[n] = nc.dram_tensor(n, list(shapes[n]), F32,
                                 kind="ExternalInput").ap()
    gkind = "ExternalOutput" if dbg else "Internal"
    if 0 in layers:
        io.g0 = nc.dram_tensor("g0", [NCH, 128, S], BF16, kind=gkind).ap()
    if 1 in layers:
        io.g1 = nc.dram_tensor("g1", [NCH, 128, S], BF16, kind=gkind).ap()
    if dbg:
        io.d_xn = nc.dram_tensor("d_xn", [128, NCH, S], BF16,
                                 kind="ExternalOutput").ap()
        io.d_q = nc.dram_tensor("d_q", [128, NCH, S], BF16,
                                kind="ExternalOutput").ap()
        io.d_k = nc.dram_tensor("d_k", [128, NCH, S], BF16,
                                kind="ExternalOutput").ap()
        io.d_v = nc.dram_tensor("d_v", [128, NTK, H, 65], BF16,
                                kind="ExternalOutput").ap()
        io.d_s = nc.dram_tensor("d_s", [NCH, 128, S], F32,
                                kind="ExternalOutput").ap()

    es = ExitStack()
    P = Prog(nc)
    A = Arena(nc)
    with nc.Block() as block:
        C = setup_common(P, A, es)
        if layers == (0, 1):
            emit_layer0(P, A, C, io, io.x_in, io.x_mid)
            emit_layer1(P, A, C, io, io.x_mid, io.x_out)
        elif layers == (0,):
            emit_layer0(P, A, C, io, io.x_in, io.x_out)
        elif layers == (1,):
            emit_layer1(P, A, C, io, io.x_in, io.x_out)
        P.wait("sync", C.dx)
        P.wait("sync", C.dw)
        P.op("sync", "nop")
        P.simulate()
        P.replay(block)
    es.close()
    return nc


def kernel(**inputs):
    g = host_prep(inputs)
    x = np.asarray(inputs["L0_attn_x"], dtype=np.float32)
    nc = build_program((0, 1))
    in_maps = []
    for b in range(8):
        m = dict(g)
        m["x_in"] = np.ascontiguousarray(x[b].T)
        in_maps.append(m)
    res = run_bass_kernel_spmd(nc, in_maps, core_ids=list(range(8)))
    out = np.stack([res.results[b]["x_out"].T for b in range(8)], axis=0)
    return out.astype(np.float32)


# version 09
# speedup vs baseline: 1.1354x; 1.0146x over previous
import numpy as np
import concourse.bass as bass
import concourse.mybir as mybir
from concourse.bass_utils import run_bass_kernel_spmd

F32 = mybir.dt.float32
BF16 = mybir.dt.bfloat16
AF = mybir.ActivationFunctionType
ALU = mybir.AluOpType

S = 2048
D = 1024
NCH = D // 128
H = 16
HD = 64
INTER = 4096
NI = INTER // 128
NTK = S // 128
RANK = 16
TQB = 512
NFB = S // TQB

ENGINES = ("tensor", "vector", "scalar", "gpsimd", "sync")


class Sem:
    __slots__ = ("h", "n", "name")

    def __init__(self, h, name):
        self.h = h
        self.n = 0
        self.name = name


class Prog:
    def __init__(self, nc):
        self.nc = nc
        self.recs = {e: [] for e in ENGINES}
        self._pend = {e: [] for e in ENGINES}
        self.sems = []

    def sem(self, name, es):
        s = Sem(es.enter_context(self.nc.semaphore(name)), name)
        self.sems.append(s)
        return s

    def wait(self, eng, sem, val=None):
        v = sem.n if val is None else val
        if v > 0:
            self._pend[eng].append((sem, v))

    def op(self, eng, meth, *args, inc=None, inc16=None, **kw):
        waits = self._pend[eng]
        self._pend[eng] = []
        incs = []
        if inc is not None:
            for s in (inc if isinstance(inc, (list, tuple)) else [inc]):
                s.n += 1
                incs.append((s, 1))
        if inc16 is not None:
            inc16.n += 16
            incs.append((inc16, 16))
        self.recs[eng].append((waits, meth, args, kw, incs))

    def dma(self, eng, out, in_, sem, **kw):
        self.op(eng, "dma_start", inc16=sem, out=out, in_=in_, **kw)

    def mm(self, out, lhsT, rhs, start=True, stop=True, inc=None, tp=None):
        self.op("tensor", "matmul", out, lhsT, rhs, start=start, stop=stop,
                skip_group_check=True, tile_position=tp, inc=inc)

    def barrier(self, sems):
        for e in ENGINES:
            for s in sems:
                self.wait(e, s)

    def simulate(self):
        pos = {e: 0 for e in ENGINES}
        val = {s.name: 0 for s in self.sems}
        total = sum(len(r) for r in self.recs.values())
        done = 0
        while done < total:
            progressed = False
            for e in ENGINES:
                while pos[e] < len(self.recs[e]):
                    waits, meth, args, kw, incs = self.recs[e][pos[e]]
                    if any(val[s.name] < v for s, v in waits):
                        break
                    for s, k in incs:
                        val[s.name] += k
                    pos[e] += 1
                    done += 1
                    progressed = True
            if not progressed:
                lines = []
                for e in ENGINES:
                    if pos[e] < len(self.recs[e]):
                        waits, meth, _, _, _ = self.recs[e][pos[e]]
                        ws = [(s.name, v, val[s.name]) for s, v in waits
                              if val[s.name] < v]
                        lines.append(f"{e}@{pos[e]}: {meth} blocked on {ws}")
                raise RuntimeError("DEADLOCK:\n" + "\n".join(lines))

    def replay(self, block):
        for e in ENGINES:
            recs = self.recs[e]

            def make(engine_recs):
                def emit(eng):
                    for waits, meth, args, kw, incs in engine_recs:
                        ww = {}
                        for s, v in waits:
                            if s.name in ww:
                                ww[s.name] = (s, max(v, ww[s.name][1]))
                            else:
                                ww[s.name] = (s, v)
                        for s, v in ww.values():
                            eng.wait_ge(s.h, v)
                        inst = getattr(eng, meth)(*args, **kw)
                        for s, k in incs:
                            inst = inst.then_inc(s.h, k)
                return emit
            getattr(block, e)(make(recs))


class Arena:
    def __init__(self, nc, nbytes=212608):
        self.t = nc.alloc_sbuf_tensor("arena", [128, nbytes], mybir.dt.uint8)
        self.nbytes = nbytes
        self.off = 0
        self.peak = 0

    def take(self, shape, dtype, name=""):
        esz = 4 if dtype == F32 else 2
        free = int(np.prod(shape[1:]))
        nb = free * esz
        off = (self.off + 31) // 32 * 32
        assert off + nb <= self.nbytes, \
            f"SBUF arena overflow at {name}: need {off + nb}, have {self.nbytes}"
        self.off = off + nb
        self.peak = max(self.peak, self.off)
        v = self.t.ap()[:, off:off + nb].bitcast(dtype)
        if len(shape) > 2:
            dims = " ".join(f"d{i}" for i in range(1, len(shape)))
            v = v.rearrange(f"p ({dims}) -> p {dims}",
                            **{f"d{i}": int(shape[i]) for i in range(1, len(shape))})
        if shape[0] != 128:
            v = v[0:shape[0]]
        return v

    def mark(self):
        return self.off

    def reset(self, mark):
        self.off = mark



def _evens_odds_perm():
    perm = np.zeros(D, dtype=np.int64)
    for h in range(H):
        perm[h * 64:h * 64 + 32] = h * 64 + np.arange(0, 64, 2)
        perm[h * 64 + 32:h * 64 + 64] = h * 64 + np.arange(1, 64, 2)
    return perm


def host_prep(inputs):
    g = {}
    T = lambda a: np.ascontiguousarray(np.asarray(a, dtype=np.float32).T)
    C = lambda a: np.ascontiguousarray(np.asarray(a, dtype=np.float32))
    g["wq"] = T(inputs["L0_attn_Wq"])
    g["wk"] = T(inputs["L0_attn_Wk"])
    g["wv"] = T(inputs["L0_attn_Wv"])
    g["w1"] = C(inputs["L0_ffn_W1"])
    g["w2"] = C(inputs["L0_ffn_W2"])
    g["b1c"] = np.ascontiguousarray(
        np.asarray(inputs["L0_ffn_b1"], np.float32).reshape(NI, 128).T)
    g["b2c"] = np.ascontiguousarray(
        np.asarray(inputs["L0_ffn_b2"], np.float32).reshape(NCH, 128).T)

    wqkv = T(inputs["L1_attn_w_qkv"])
    perm = _evens_odds_perm()
    g["wq1"] = np.ascontiguousarray(wqkv[:, 0:D][:, perm])
    g["wk1"] = np.ascontiguousarray(wqkv[:, D:2 * D][:, perm])
    g["wv1"] = np.ascontiguousarray(wqkv[:, 2 * D:3 * D])
    g["wout1"] = T(inputs["L1_attn_w_out"])
    g["boutc"] = np.ascontiguousarray(
        np.asarray(inputs["L1_attn_b_out"], np.float32).reshape(NCH, 128).T)
    g["wfc"] = T(inputs["L1_ffn_w_fc"])
    g["wproj"] = T(inputs["L1_ffn_w_proj"])
    g["uT"] = T(inputs["L1_ffn_u"])
    g["vmat"] = C(inputs["L1_ffn_v"])
    g["fbT"] = T(inputs["L1_ffn_fast_b"])
    g["gatec"] = np.full((16, 1), float(np.asarray(inputs["L1_ffn_gate"])),
                         dtype=np.float32)

    inv_freq = 1.0 / 10000.0 ** (np.arange(0, HD, 2, dtype=np.float32) / HD)
    t = np.arange(S, dtype=np.float32)
    fr = np.outer(t, inv_freq)
    cos = np.cos(fr).astype(np.float32).T
    sin = np.sin(fr).astype(np.float32).T
    g["t1"] = np.ascontiguousarray(np.vstack([cos, cos, cos, cos]))
    g["t2"] = np.ascontiguousarray(np.vstack([-sin, sin, -sin, sin]))
    sw = np.zeros((128, 128), dtype=np.float32)
    for blk in range(4):
        b = blk * 32
        dst = (b + 32) % 64 + (b // 64) * 64
        for i in range(32):
            sw[dst + i, b + i] = 1.0
    g["pswap"] = sw
    return g



class Ctx:
    pass


def setup_common(P, A, es):
    C = Ctx()
    nc = P.nc
    C.psum = es.enter_context(nc.psum_tensor("psum", [128, 4096], F32))
    C.bank = [C.psum[:, 512 * i:512 * (i + 1)] for i in range(8)]
    C.bank_free = [(None, 0)] * 8
    C.dw = P.sem("dw", es)
    C.dx = P.sem("dx", es)
    C.pe = P.sem("pe", es)
    C.ve = P.sem("ve", es)
    C.se = P.sem("se", es)
    C.ge = P.sem("ge", es)
    C.all_sems = [C.dw, C.dx, C.pe, C.ve, C.se, C.ge]
    C.ones_bf_col = A.take([128, 1], BF16, "ones_col")
    C.ones_f_row = A.take([1, 128], F32, "ones_f")
    C.ones_bf_row = A.take([1, 128], BF16, "ones_bfr")
    C.half_f_row = A.take([1, 128], F32, "half_f")
    C.nhalf_f_row = A.take([1, 128], F32, "nhalf_f")
    C.eps_t = A.take([1, 1], F32, "eps")
    C.ones2_bf = A.take([65, 128], BF16, "ones2")
    P.op("gpsimd", "memset", C.ones_bf_col, 1.0)
    P.op("gpsimd", "memset", C.ones_f_row, 1.0)
    P.op("gpsimd", "memset", C.ones_bf_row, 1.0)
    P.op("gpsimd", "memset", C.half_f_row, 0.5)
    P.op("gpsimd", "memset", C.eps_t, 1e-5)
    P.op("gpsimd", "memset", C.ones2_bf, 1.0)
    P.op("gpsimd", "memset", C.nhalf_f_row, -0.5, inc=C.ge)
    return C


def use_bank(P, C, i):
    s, n = C.bank_free[i]
    if s is not None:
        P.wait("tensor", s, n)


def release_bank(C, i, sem):
    C.bank_free[i] = (sem, sem.n)


def emit_ln(P, A, C, x_sb, xn_out):
    m0 = A.mark()
    xb = A.take([128, NCH, S], BF16, "ln_xb")
    sums = A.take([1, 2, S], F32, "ln_sums")
    mneg = A.take([1, S], F32, "ln_mneg")
    rstd = A.take([1, S], F32, "ln_rstd")
    mneg_b = A.take([128, S], F32, "ln_mnegb")
    rstd_b = A.take([128, S], F32, "ln_rstdb")
    scrf = A.take([128, S], F32, "ln_scr")
    bs, bb = 7, 6

    for c in range(NCH):
        P.op("vector", "tensor_copy", xb[:, c, :], x_sb[:, c, :], inc=C.ve)
    for tb in range(S // 512):
        use_bank(P, C, bs)
        P.wait("tensor", C.ve)
        for c in range(NCH):
            P.mm(C.bank[bs][0:1, :], C.ones_bf_col, xb[:, c, bass.ts(tb, 512)],
                 start=(c == 0), stop=(c == NCH - 1),
                 inc=C.pe if c == NCH - 1 else None)
        P.wait("vector", C.pe)
        P.op("vector", "tensor_copy", sums[0:1, 0, bass.ts(tb, 512)],
             C.bank[bs][0:1, :], inc=C.ve)
        release_bank(C, bs, C.ve)
    for c in range(NCH):
        P.wait("scalar", C.pe)
        P.op("scalar", "activation", xb[:, c, :], xb[:, c, :], AF.Square,
             inc=C.se)
    for tb in range(S // 512):
        use_bank(P, C, bs)
        P.wait("tensor", C.se)
        for c in range(NCH):
            P.mm(C.bank[bs][0:1, :], C.ones_bf_col, xb[:, c, bass.ts(tb, 512)],
                 start=(c == 0), stop=(c == NCH - 1),
                 inc=C.pe if c == NCH - 1 else None)
        P.wait("vector", C.pe)
        P.op("vector", "tensor_copy", sums[0:1, 1, bass.ts(tb, 512)],
             C.bank[bs][0:1, :], inc=C.ve)
        release_bank(C, bs, C.ve)
    P.op("vector", "tensor_scalar", mneg, sums[0:1, 0, :], -1.0 / D, None, ALU.mult)
    P.op("vector", "tensor_scalar", rstd, sums[0:1, 0, :], 1.0 / D, None, ALU.mult)
    P.op("vector", "tensor_mul", rstd, rstd, rstd)
    P.op("vector", "scalar_tensor_tensor", rstd, sums[0:1, 1, :], 1.0 / D, rstd,
         ALU.mult, ALU.subtract, inc=C.ve)
    P.wait("scalar", C.ve)
    P.op("scalar", "activation", rstd, rstd, AF.Sqrt, bias=C.eps_t, inc=C.se)
    P.wait("vector", C.se)
    P.op("vector", "reciprocal", rstd, rstd, inc=C.ve)
    for tb in range(S // 512):
        use_bank(P, C, bb)
        P.wait("tensor", C.ve)
        P.mm(C.bank[bb][:, :], C.ones_f_row, mneg[0:1, bass.ts(tb, 512)], inc=C.pe)
        P.wait("vector", C.pe)
        P.op("vector", "tensor_copy", mneg_b[:, bass.ts(tb, 512)], C.bank[bb][:, :],
             inc=C.ve)
        P.wait("tensor", C.ve)
        P.mm(C.bank[bb][:, :], C.ones_f_row, rstd[0:1, bass.ts(tb, 512)], inc=C.pe)
        P.wait("vector", C.pe)
        P.op("vector", "tensor_copy", rstd_b[:, bass.ts(tb, 512)], C.bank[bb][:, :],
             inc=C.ve)
        release_bank(C, bb, C.ve)
    for c in range(NCH):
        P.op("vector", "tensor_add", scrf, x_sb[:, c, :], mneg_b)
        P.op("vector", "tensor_mul", xn_out[:, c, :], scrf, rstd_b, inc=C.ve)
    A.reset(m0)


def emit_ffn(P, A, C, xn, w1_sb, w2_sb, g_hbm, layer, b1c=None, b2c=None):
    m0 = A.mark()
    rr = A.take([128, NI, TQB], BF16, "rr")
    gtile = A.take([128, NCH, TQB], BF16, "gtile")
    scrf = A.take([128, TQB], F32, "ffn_scr")
    f1b = [0, 1]
    f2b = [2, 3, 4, 5]
    gtile_free = C.dx.n
    for tb in range(NFB):
        for c2 in range(NI):
            bk = f1b[c2 % 2]
            use_bank(P, C, bk)
            for ci in range(NCH):
                P.mm(C.bank[bk][:, 0:TQB], w1_sb[:, ci, bass.ts(c2, 128)],
                     xn[:, ci, bass.ts(tb, TQB)],
                     start=(ci == 0), stop=(ci == NCH - 1),
                     inc=C.pe if ci == NCH - 1 else None)
            if layer == 0:
                P.wait("vector", C.pe)
                P.op("vector", "tensor_scalar", rr[:, c2, :], C.bank[bk][:, 0:TQB],
                     b1c[:, c2:c2 + 1], 0.0, ALU.add, ALU.max, inc=C.ve)
                release_bank(C, bk, C.ve)
            else:
                P.wait("vector", C.pe)
                P.op("vector", "tensor_scalar", scrf, C.bank[bk][:, 0:TQB],
                     0.0, None, ALU.max, inc=C.ve)
                release_bank(C, bk, C.ve)
                P.wait("scalar", C.ve)
                P.op("scalar", "activation", rr[:, c2, :], scrf, AF.Square,
                     inc=C.se)
        for oc in range(NCH):
            bk = f2b[oc % 4]
            use_bank(P, C, bk)
            if layer == 1:
                P.wait("tensor", C.se)
            else:
                P.wait("tensor", C.ve)
            for c2 in range(NI):
                P.mm(C.bank[bk][:, 0:TQB], w2_sb[:, c2, bass.ts(oc, 128)],
                     rr[:, c2, :],
                     start=(c2 == 0), stop=(c2 == NI - 1),
                     inc=C.pe if c2 == NI - 1 else None)
            P.wait("scalar", C.pe)
            P.wait("scalar", C.dx, gtile_free)
            if layer == 0:
                P.op("scalar", "activation", gtile[:, oc, :], C.bank[bk][:, 0:TQB],
                     AF.Silu, bias=b2c[:, oc:oc + 1], inc=C.se)
            else:
                P.op("scalar", "activation", gtile[:, oc, :], C.bank[bk][:, 0:TQB],
                     AF.Silu, inc=C.se)
            release_bank(C, bk, C.se)
        P.wait("sync", C.se)
        P.dma("sync", g_hbm[:, :, bass.ts(tb, TQB)].rearrange("c p t -> p c t"),
              gtile, C.dx)
        gtile_free = C.dx.n
    A.reset(m0)


def emit_qk_proj_l0(P, A, C, xn, w_sb, out_bf):
    bks = [0, 1]
    free_i = 0
    for oc in range(NCH):
        for tb in range(S // 512):
            bk = bks[free_i % 2]
            free_i += 1
            use_bank(P, C, bk)
            for ci in range(NCH):
                P.mm(C.bank[bk][:, :], w_sb[:, ci, bass.ts(oc, 128)],
                     xn[:, ci, bass.ts(tb, 512)],
                     start=(ci == 0), stop=(ci == NCH - 1),
                     inc=C.pe if ci == NCH - 1 else None)
            P.wait("vector", C.pe)
            P.op("vector", "tensor_copy", out_bf[:, oc, bass.ts(tb, 512)],
                 C.bank[bk][:, :], inc=C.ve)
            release_bank(C, bk, C.ve)


def emit_qk_proj_rope(P, A, C, xn, w_sb, out_bf, t1, t2, pswap_bf):
    m0 = A.mark()
    qpre = A.take([128, 512], BF16, "rope_qpre")
    s1 = A.take([128, 512], F32, "rope_s1")
    s2 = A.take([128, 512], F32, "rope_s2")
    pi = 0
    for oc in range(NCH):
        for tb in range(S // 512):
            bk = [0, 1][pi % 2]
            bk2 = [2, 3][pi % 2]
            pi += 1
            use_bank(P, C, bk)
            for ci in range(NCH):
                P.mm(C.bank[bk][:, :], w_sb[:, ci, bass.ts(oc, 128)],
                     xn[:, ci, bass.ts(tb, 512)],
                     start=(ci == 0), stop=(ci == NCH - 1),
                     inc=C.pe if ci == NCH - 1 else None)
            P.wait("vector", C.pe)
            P.op("vector", "tensor_copy", qpre, C.bank[bk][:, :], inc=C.ve)
            release_bank(C, bk, C.ve)
            use_bank(P, C, bk2)
            P.wait("tensor", C.ve)
            P.mm(C.bank[bk2][:, :], pswap_bf, qpre, inc=C.pe)
            P.op("vector", "tensor_mul", s1, qpre, t1[:, bass.ts(tb, 512)])
            P.wait("vector", C.pe)
            P.op("vector", "tensor_mul", s2, C.bank[bk2][:, :],
                 t2[:, bass.ts(tb, 512)], inc=C.ve)
            release_bank(C, bk2, C.ve)
            P.op("vector", "tensor_add", out_bf[:, oc, bass.ts(tb, 512)], s1, s2,
                 inc=C.ve)
    A.reset(m0)


def emit_v_proj(P, A, C, xn, wv_sb, v_aug, col_off=0):
    pi = 0
    for tk in range(NTK):
        for ob in range(2):
            bk = [0, 1][pi % 2]
            pi += 1
            use_bank(P, C, bk)
            for ci in range(NCH):
                P.mm(C.bank[bk][:, :], xn[:, ci, bass.ts(tk, 128)],
                     wv_sb[:, ci, col_off + ob * 512:col_off + (ob + 1) * 512],
                     start=(ci == 0), stop=(ci == NCH - 1),
                     inc=C.pe if ci == NCH - 1 else None)
            P.wait("vector", C.pe)
            P.op("vector", "tensor_copy",
                 v_aug[:, tk, ob * 8:(ob + 1) * 8, 0:64],
                 C.bank[bk][:, :].rearrange("p (h d) -> p h d", h=8), inc=C.ve)
            release_bank(C, bk, C.ve)


def emit_mshift(P, A, C, q_bf, hp, mshift, kmaxs, qsq_scr, msrow, inv_scale):
    bq = 7
    P.wait("scalar", C.pe)
    P.op("scalar", "activation", qsq_scr, q_bf[:, hp, :], AF.Square, inc=C.se)
    for side, base in ((0, 0), (1, 64)):
        h = 2 * hp + side
        use_bank(P, C, bq)
        P.wait("tensor", C.se)
        for tb in range(S // 512):
            P.mm(C.bank[bq][0:1, :], C.ones_bf_col[base:base + 64, :],
                 qsq_scr[base:base + 64, bass.ts(tb, 512)], inc=C.pe)
            P.wait("vector", C.pe)
            P.op("vector", "tensor_copy", msrow[0:1, bass.ts(tb, 512)],
                 C.bank[bq][0:1, :], inc=C.ve)
            P.wait("tensor", C.ve)
        release_bank(C, bq, C.ve)
        P.wait("scalar", C.ve)
        P.op("scalar", "activation", msrow, msrow, AF.Sqrt, inc=C.se)
        P.wait("vector", C.se)
        P.op("vector", "tensor_scalar", msrow, msrow,
             kmaxs[0:1, h:h + 1], -64.0, ALU.mult, ALU.add)
        P.op("vector", "tensor_scalar", msrow, msrow, 0.0, -float(inv_scale),
             ALU.max, ALU.mult)
        P.op("vector", "tensor_copy", mshift[base:base + 1, :], msrow, inc=C.ve)


def emit_kmaxs(P, A, C, k_bf, kmaxs, scr, row, inv_scale):
    m0 = A.mark()
    one = A.take([1, 1], F32, "km_one")
    for hp in range(NCH):
        P.wait("scalar", C.pe)
        P.wait("scalar", C.ve)
        P.op("scalar", "activation", scr, k_bf[:, hp, :], AF.Square, inc=C.se)
        for side, base in ((0, 0), (1, 64)):
            h = 2 * hp + side
            bq = 6 + side
            use_bank(P, C, bq)
            P.wait("tensor", C.se)
            for tb in range(S // 512):
                P.mm(C.bank[bq][0:1, :], C.ones_bf_col[base:base + 64, :],
                     scr[base:base + 64, bass.ts(tb, 512)], inc=C.pe)
                P.wait("vector", C.pe)
                P.op("vector", "tensor_copy", row[0:1, bass.ts(tb, 512)],
                     C.bank[bq][0:1, :], inc=C.ve)
                P.wait("tensor", C.ve)
            release_bank(C, bq, C.ve)
            P.op("vector", "reduce_max", one, row, mybir.AxisListType.X, inc=C.ve)
            P.wait("scalar", C.ve)
            P.op("scalar", "activation", one, one, AF.Sqrt, inc=C.se)
            P.wait("vector", C.se)
            P.op("vector", "tensor_scalar", kmaxs[0:1, h:h + 1], one,
                 1.0 / float(inv_scale), None, ALU.mult, inc=C.ve)
    A.reset(m0)


def emit_attention(P, A, C, q_bf, k_bf, v_aug, scale, consume, shifted=False):
    m0 = A.mark()
    recips = A.take([1, 2, 512], F32, "recips")
    rb = A.take([128, 512], F32, "rb")
    s_blk = A.take([128, 512], F32, "s_blk")
    if shifted:
        inv_scale = 1.0 / scale
        mshift = A.take([65, S], BF16, "mshift")
        kmaxs = A.take([1, H], F32, "kmaxs")
        qsq_scr = A.take([128, S], BF16, "qsq")
        msrow = A.take([1, S], F32, "msrow")
        emit_kmaxs(P, A, C, k_bf, kmaxs, qsq_scr, msrow, inv_scale)
    rtmp = A.take([1, 2, 512], F32, "rtmp")
    ptab = [A.take([128, 1024], BF16, "ptab0"),
            A.take([128, 1024], BF16, "ptab1")]
    pv_pairs = [(4, 5), (6, 7)]

    def emit_scores(hp, tq, tk):
        p = tk % 2
        bA, bB = 2 * p, 2 * p + 1
        use_bank(P, C, bA)
        use_bank(P, C, bB)
        if shifted:
            P.mm(C.bank[bA], C.ones2_bf[0:1, :], mshift[0:1, tq],
                 start=True, stop=False)
            P.mm(C.bank[bB], C.ones2_bf[64:65, :], mshift[64:65, tq],
                 start=True, stop=False)
            P.mm(C.bank[bA], k_bf[0:64, hp, bass.ts(tk, 128)],
                 q_bf[0:64, hp, tq], start=False, stop=True, tp=(0, 0))
            P.mm(C.bank[bB], k_bf[64:128, hp, bass.ts(tk, 128)],
                 q_bf[64:128, hp, tq], start=False, stop=True, tp=(64, 0),
                 inc=C.pe)
        else:
            P.mm(C.bank[bA], k_bf[0:64, hp, bass.ts(tk, 128)],
                 q_bf[0:64, hp, tq], start=True, stop=True, tp=(0, 0))
            P.mm(C.bank[bB], k_bf[64:128, hp, bass.ts(tk, 128)],
                 q_bf[64:128, hp, tq], start=True, stop=True, tp=(64, 0),
                 inc=C.pe)

    def emit_exp(tk):
        p = tk % 2
        P.wait("scalar", C.pe)
        P.op("scalar", "activation", ptab[p],
             C.psum[:, 1024 * p:1024 * (p + 1)], AF.Exp, scale=scale,
             inc=C.se)
        release_bank(C, 2 * p, C.se)
        release_bank(C, 2 * p + 1, C.se)
        return C.se.n

    def emit_pv(hp, tk, se_need, bOA, bOB):
        p = tk % 2
        P.wait("tensor", C.se, se_need)
        P.mm(C.bank[bOA][0:65, :], v_aug[:, tk, 2 * hp, :], ptab[p][:, 0:512],
             start=(tk == 0), stop=(tk == NTK - 1))
        P.mm(C.bank[bOB][0:65, :], v_aug[:, tk, 2 * hp + 1, :],
             ptab[p][:, 512:1024],
             start=(tk == 0), stop=(tk == NTK - 1), inc=C.pe)

    pending = [None]

    def norm_part2(hp, tqb, bOA, bOB):
        use_bank(P, C, bOA)
        use_bank(P, C, bOB)
        P.wait("tensor", C.ve)
        P.mm(C.bank[bOA][0:64, :], C.ones_f_row[0:1, 0:64], recips[0:1, 0, :])
        P.mm(C.bank[bOB][0:64, :], C.ones_f_row[0:1, 0:64], recips[0:1, 1, :],
             inc=C.pe)
        P.wait("vector", C.pe)
        P.op("vector", "tensor_copy", rb[0:64, :], C.bank[bOA][0:64, :])
        P.op("vector", "tensor_copy", rb[64:128, :], C.bank[bOB][0:64, :],
             inc=C.ve)
        release_bank(C, bOA, C.ve)
        release_bank(C, bOB, C.ve)
        P.op("vector", "tensor_mul", s_blk[0:64, :], s_blk[0:64, :],
             rb[0:64, :])
        P.op("vector", "tensor_mul", s_blk[64:128, :], s_blk[64:128, :],
             rb[64:128, :], inc=C.ve)
        consume(hp, tqb, s_blk)

    for hp in range(NCH):
        if shifted:
            emit_mshift(P, A, C, q_bf, hp, mshift, kmaxs, qsq_scr, msrow,
                        inv_scale)
            P.wait("tensor", C.ve)
        for tqb in range(S // 512):
            tq = bass.ts(tqb, 512)
            it = hp * 4 + tqb
            bOA, bOB = pv_pairs[it % 2]
            se_at = {}
            for tk in range(NTK):
                emit_scores(hp, tq, tk)
                if tk == 0:
                    use_bank(P, C, bOA)
                    use_bank(P, C, bOB)
                se_at[tk] = emit_exp(tk)
                if tk > 0:
                    emit_pv(hp, tk - 1, se_at[tk - 1], bOA, bOB)
                if tk == 10 and pending[0] is not None:
                    norm_part2(*pending[0])
                    pending[0] = None
            emit_pv(hp, NTK - 1, se_at[NTK - 1], bOA, bOB)
            P.wait("vector", C.pe)
            P.op("vector", "tensor_scalar", rtmp[0:1, 0, :],
                 C.bank[bOA][64:65, :], 1e-36, None, ALU.add)
            P.op("vector", "tensor_scalar", rtmp[0:1, 1, :],
                 C.bank[bOB][64:65, :], 1e-36, None, ALU.add)
            P.op("vector", "tensor_copy", s_blk[0:64, :], C.bank[bOA][0:64, :])
            P.op("vector", "tensor_copy", s_blk[64:128, :],
                 C.bank[bOB][0:64, :], inc=C.ve)
            release_bank(C, bOA, C.ve)
            release_bank(C, bOB, C.ve)
            P.op("vector", "reciprocal", recips[0:1, 0, :], rtmp[0:1, 0, :])
            P.op("vector", "reciprocal", recips[0:1, 1, :], rtmp[0:1, 1, :],
                 inc=C.ve)
            pending[0] = (hp, tqb, bOA, bOB)
    norm_part2(*pending[0])
    pending[0] = None
    A.reset(m0)


def emit_ternarize(P, A, C, w_hbm, n_in, n_out, weff_sb, tag):
    m0 = A.mark()
    nic = n_in // 128
    nob = n_out // 512
    chunk_f = A.take([128, n_out], F32, f"tn_cf_{tag}")
    chunk_abs = A.take([128, n_out], BF16, f"tn_ca_{tag}")
    srow = A.take([1, n_out], F32, f"tn_sr_{tag}")
    bh = A.take([128, n_out], F32, f"tn_bh_{tag}")
    sbc = A.take([128, n_out], BF16, f"tn_sbc_{tag}")
    mneg = A.take([128, n_out], BF16, f"tn_mn_{tag}")
    w_r = w_hbm.rearrange("(c p) o -> p c o", p=128)
    for c in range(nic):
        P.wait("sync", C.se)
        P.dma("sync", chunk_f, w_r[:, c, :], C.dx)
        P.wait("scalar", C.dx)
        P.wait("scalar", C.pe)
        P.op("scalar", "activation", chunk_abs, chunk_f, AF.Abs, inc=C.se)
        P.wait("tensor", C.se)
        for ob in range(nob):
            if c == 0:
                use_bank(P, C, ob)
            P.mm(C.bank[ob][0:1, :], C.ones_bf_col, chunk_abs[:, bass.ts(ob, 512)],
                 start=(c == 0), stop=(c == nic - 1),
                 inc=C.pe if ob == nob - 1 else None)
    P.wait("vector", C.pe)
    for ob in range(nob):
        P.op("vector", "tensor_scalar", srow[0:1, bass.ts(ob, 512)],
             C.bank[ob][0:1, :], 1.0 / n_in, 1e-5, ALU.mult, ALU.max,
             inc=C.ve if ob == nob - 1 else None)
    for ob in range(nob):
        release_bank(C, ob, C.ve)
    for ob in range(nob):
        bk = ob % 2
        use_bank(P, C, bk)
        P.wait("tensor", C.ve)
        P.mm(C.bank[bk][:, :], C.half_f_row, srow[0:1, bass.ts(ob, 512)],
             inc=C.pe)
        P.wait("vector", C.pe)
        P.op("vector", "tensor_copy", bh[:, bass.ts(ob, 512)], C.bank[bk][:, :],
             inc=C.ve)
        release_bank(C, bk, C.ve)
        bk = (ob + 1) % 2
        use_bank(P, C, bk)
        P.wait("tensor", C.ve)
        P.mm(C.bank[bk][:, :], C.ones_f_row, srow[0:1, bass.ts(ob, 512)],
             inc=C.pe)
        P.wait("vector", C.pe)
        P.op("vector", "tensor_copy", sbc[:, bass.ts(ob, 512)], C.bank[bk][:, :],
             inc=C.ve)
        release_bank(C, bk, C.ve)
    for c in range(nic):
        P.wait("sync", C.ve)
        P.dma("sync", chunk_f, w_r[:, c, :], C.dx)
        P.wait("vector", C.dx)
        wch = weff_sb[:, c, :]
        P.op("vector", "tensor_tensor", wch, chunk_f, bh, ALU.is_ge)
        P.op("vector", "scalar_tensor_tensor", mneg, chunk_f, -1.0, bh,
             ALU.mult, ALU.is_ge)
        P.op("vector", "tensor_sub", wch, wch, mneg)
        P.op("vector", "tensor_mul", wch, wch, sbc, inc=C.ve)
    A.reset(m0)


def emit_lowrank_fold(P, A, C, io, weff_fc):
    m0 = A.mark()
    u_sb = A.take([16, INTER], BF16, "lr_u")
    v_sb = A.take([16, D], BF16, "lr_v")
    fb_sb = A.take([16, RANK], F32, "lr_fb")
    gate_c = A.take([16, 1], F32, "lr_gate")
    gfb = A.take([16, RANK], BF16, "lr_gfb")
    at = A.take([16, D], BF16, "lr_at")
    P.dma("gpsimd", u_sb, io.w["uT"], C.dw)
    P.dma("gpsimd", v_sb, io.w["vmat"], C.dw)
    P.dma("gpsimd", fb_sb, io.w["fbT"], C.dw)
    P.dma("gpsimd", gate_c, io.w["gatec"], C.dw)
    P.wait("vector", C.dw)
    P.op("vector", "tensor_scalar", gfb, fb_sb, gate_c, None, ALU.mult, inc=C.ve)
    use_bank(P, C, 0)
    use_bank(P, C, 1)
    P.wait("tensor", C.ve)
    P.mm(C.bank[0][0:16, :], gfb, v_sb[:, 0:512], start=True, stop=True)
    P.mm(C.bank[1][0:16, :], gfb, v_sb[:, 512:1024], start=True, stop=True,
         inc=C.pe)
    P.wait("vector", C.pe)
    P.op("vector", "tensor_copy", at[:, 0:512], C.bank[0][0:16, :])
    P.op("vector", "tensor_copy", at[:, 512:1024], C.bank[1][0:16, :], inc=C.ve)
    release_bank(C, 0, C.ve)
    release_bank(C, 1, C.ve)
    for ci in range(NCH):
        for ob in range(INTER // 512):
            bk = ob % 2
            use_bank(P, C, bk)
            P.wait("tensor", C.ve)
            P.mm(C.bank[bk][:, :], at[:, bass.ts(ci, 128)],
                 u_sb[:, bass.ts(ob, 512)], inc=C.pe)
            P.wait("vector", C.pe)
            P.op("vector", "tensor_add", weff_fc[:, ci, bass.ts(ob, 512)],
                 weff_fc[:, ci, bass.ts(ob, 512)], C.bank[bk][:, :], inc=C.ve)
            release_bank(C, bk, C.ve)
    A.reset(m0)



def emit_layer0(P, A, C, io, x_src, x_dst):
    m_layer = A.mark()
    xn = A.take([128, NCH, S], BF16, "xn0")
    x_src_r = x_src.rearrange("(c p) t -> p c t", p=128)
    x_dst_r = x_dst.rearrange("(c p) t -> c p t", p=128)
    g_hbm = io.g0

    m0 = A.mark()
    x_sb = A.take([128, NCH, S], F32, "x0")
    P.dma("sync", x_sb, x_src_r, C.dx)
    P.wait("vector", C.dx)
    P.wait("vector", C.ge)
    P.wait("tensor", C.ge)
    emit_ln(P, A, C, x_sb, xn)
    A.reset(m0)
    P.barrier(C.all_sems)

    m1 = A.mark()
    w1_sb = A.take([128, NCH, INTER], BF16, "w1")
    w2_sb = A.take([128, NI, D], BF16, "w2")
    b1c = A.take([128, NI], F32, "b1c")
    b2c = A.take([128, NCH], F32, "b2c")
    P.dma("gpsimd", w1_sb, io.w["w1"].rearrange("(c p) o -> p c o", p=128), C.dw)
    P.dma("gpsimd", w2_sb, io.w["w2"].rearrange("(c p) o -> p c o", p=128), C.dw)
    P.dma("sync", b1c, io.w["b1c"], C.dx)
    P.dma("sync", b2c, io.w["b2c"], C.dx)
    P.wait("tensor", C.dw)
    P.wait("vector", C.dx)
    P.wait("scalar", C.dx)
    emit_ffn(P, A, C, xn, w1_sb, w2_sb, g_hbm, 0, b1c, b2c)
    A.reset(m1)
    P.barrier(C.all_sems)

    q_bf = A.take([128, NCH, S], BF16, "q0")
    k_bf = A.take([128, NCH, S], BF16, "k0")
    v_aug = A.take([128, NTK, H, 65], BF16, "v0")
    m2 = A.mark()
    wslot = [A.take([128, NCH, D], BF16, "ws0"),
             A.take([128, NCH, D], BF16, "ws1")]
    P.op("gpsimd", "memset", v_aug[:, :, :, 64:65], 1.0, inc=C.ge)
    P.dma("gpsimd", wslot[0], io.w["wq"].rearrange("(c p) o -> p c o", p=128), C.dw)
    P.wait("tensor", C.dw)
    emit_qk_proj_l0(P, A, C, xn, wslot[0], q_bf)
    P.dma("gpsimd", wslot[1], io.w["wk"].rearrange("(c p) o -> p c o", p=128), C.dw)
    P.wait("tensor", C.dw)
    emit_qk_proj_l0(P, A, C, xn, wslot[1], k_bf)
    P.wait("gpsimd", C.pe)
    P.dma("gpsimd", wslot[0], io.w["wv"].rearrange("(c p) o -> p c o", p=128), C.dw)
    P.wait("tensor", C.dw)
    P.wait("tensor", C.ge)
    emit_v_proj(P, A, C, xn, wslot[0], v_aug)
    A.reset(m2)
    P.barrier(C.all_sems)
    if io.dbg:
        P.dma("sync", io.d_xn, xn, C.dx)
        P.dma("sync", io.d_q, q_bf, C.dx)
        P.dma("sync", io.d_k, k_bf, C.dx)
        P.dma("sync", io.d_v, v_aug, C.dx)

    xst = A.take([128, 2, 512], F32, "xstage")
    gst = A.take([128, 2, 512], BF16, "gstage")
    stage_free = [(None, 0), (None, 0)]
    g_hbm_r = g_hbm
    x_src_cr = x_src.rearrange("(c p) t -> c p t", p=128)

    def consume0(hp, tqb, s_blk):
        sl = (hp * 4 + tqb) % 2
        if io.dbg:
            P.wait("sync", C.ve)
            P.dma("sync", io.d_s[hp, :, bass.ts(tqb, 512)], s_blk, C.dx)
            P.wait("vector", C.dx)
        P.wait("vector", C.dx, stage_issue[(hp, tqb)])
        P.op("vector", "tensor_mul", s_blk, s_blk, gst[:, sl, :])
        P.op("vector", "tensor_add", xst[:, sl, :], xst[:, sl, :], s_blk,
             inc=C.ve)
        P.wait("sync", C.ve)
        P.dma("sync", x_dst_r[hp, :, bass.ts(tqb, 512)], xst[:, sl, :], C.dx)
        stage_free[sl] = (C.dx, C.dx.n)

    stage_issue = {}

    def consume_wrap(hp, tqb, s_blk):
        consume0(hp, tqb, s_blk)
        nxt = hp * 4 + tqb + 1
        if nxt < 32:
            issue_stage(nxt // 4, nxt % 4)

    def issue_stage(hp, tqb):
        sl = (hp * 4 + tqb) % 2
        s, n = stage_free[sl]
        if s is not None:
            P.wait("sync", s, n)
        P.dma("sync", xst[:, sl, :], x_src_cr[hp, :, bass.ts(tqb, 512)], C.dx)
        P.dma("sync", gst[:, sl, :], g_hbm_r[hp, :, bass.ts(tqb, 512)], C.dx)
        stage_issue[(hp, tqb)] = C.dx.n

    issue_stage(0, 0)
    issue_stage(0, 1)
    emit_attention(P, A, C, q_bf, k_bf, v_aug, 1.0 / 32.0, consume_wrap)
    A.reset(m_layer)
    P.barrier(C.all_sems)


def emit_layer1(P, A, C, io, x_src, x_dst):
    m_layer = A.mark()
    xn = A.take([128, NCH, S], BF16, "xn1")
    x_src_r = x_src.rearrange("(c p) t -> p c t", p=128)
    x_src_cr = x_src.rearrange("(c p) t -> c p t", p=128)
    x_dst_r = x_dst.rearrange("(c p) t -> c p t", p=128)
    g_hbm = io.g1

    m0 = A.mark()
    x_sb = A.take([128, NCH, S], F32, "x1")
    P.dma("sync", x_sb, x_src_r, C.dx)
    P.wait("vector", C.dx)
    emit_ln(P, A, C, x_sb, xn)
    A.reset(m0)
    P.barrier(C.all_sems)

    weff_fc = A.take([128, NCH, INTER], BF16, "weff_fc")
    emit_ternarize(P, A, C, io.w["wfc"], D, INTER, weff_fc, "fc")
    P.barrier(C.all_sems)
    emit_lowrank_fold(P, A, C, io, weff_fc)
    P.barrier(C.all_sems)
    weff_pj = A.take([128, NI, D], BF16, "weff_pj")
    emit_ternarize(P, A, C, io.w["wproj"], INTER, D, weff_pj, "pj")
    P.barrier(C.all_sems)
    emit_ffn(P, A, C, xn, weff_fc, weff_pj, g_hbm, 1)
    A.reset(m_layer)
    xn = A.take([128, NCH, S], BF16, "xn1_alias")
    P.barrier(C.all_sems)

    t1 = A.take([128, S], BF16, "t1")
    t2 = A.take([128, S], BF16, "t2")
    pswap = A.take([128, 128], BF16, "pswap")
    attnout = A.take([128, NCH, S], BF16, "attnout")
    q_bf = A.take([128, NCH, S], BF16, "q1")
    k_bf = A.take([128, NCH, S], BF16, "k1")
    v_aug = A.take([128, NTK, H, 65], BF16, "v1")
    m2 = A.mark()
    wslot = A.take([128, NCH, D], BF16, "w1s0")
    P.dma("gpsimd", t1, io.w["t1"], C.dw)
    P.dma("gpsimd", t2, io.w["t2"], C.dw)
    P.dma("gpsimd", pswap, io.w["pswap"], C.dw)
    P.op("gpsimd", "memset", v_aug[:, :, :, 64:65], 1.0, inc=C.ge)
    P.dma("gpsimd", wslot, io.w["wq1"].rearrange("(c p) o -> p c o", p=128),
          C.dw)
    P.wait("tensor", C.dw)
    P.wait("vector", C.dw)
    emit_qk_proj_rope(P, A, C, xn, wslot, q_bf, t1, t2, pswap)
    P.wait("gpsimd", C.pe)
    P.dma("gpsimd", wslot, io.w["wk1"].rearrange("(c p) o -> p c o", p=128),
          C.dw)
    P.wait("tensor", C.dw)
    emit_qk_proj_rope(P, A, C, xn, wslot, k_bf, t1, t2, pswap)
    P.wait("gpsimd", C.pe)
    P.dma("gpsimd", wslot, io.w["wv1"].rearrange("(c p) o -> p c o", p=128),
          C.dw)
    P.wait("tensor", C.dw)
    P.wait("tensor", C.ge)
    emit_v_proj(P, A, C, xn, wslot, v_aug)
    A.reset(m2)
    P.barrier(C.all_sems)

    def consume1(hp, tqb, s_blk):
        P.op("vector", "tensor_copy", attnout[:, hp, bass.ts(tqb, 512)], s_blk,
             inc=C.ve)

    emit_attention(P, A, C, q_bf, k_bf, v_aug, 1.0 / 8.0, consume1, shifted=True)
    P.barrier(C.all_sems)

    m3 = A.mark()
    wout = A.take([128, NCH, D], BF16, "wout")
    boutc = A.take([128, NCH], F32, "boutc")
    xst = A.take([128, 2, 512], F32, "xst1")
    gst = A.take([128, 2, 512], BF16, "gst1")
    sblk = A.take([128, 512], F32, "sblk1")
    P.dma("gpsimd", wout, io.w["wout1"].rearrange("(c p) o -> p c o", p=128), C.dw)
    P.dma("sync", boutc, io.w["boutc"], C.dx)
    stage_free = [(None, 0), (None, 0)]
    stage_issue = {}

    def issue_stage(i):
        oc, tb = i // 4, i % 4
        sl = i % 2
        s, n = stage_free[sl]
        if s is not None:
            P.wait("sync", s, n)
        P.dma("sync", xst[:, sl, :], x_src_cr[oc, :, bass.ts(tb, 512)], C.dx)
        P.dma("sync", gst[:, sl, :], g_hbm[oc, :, bass.ts(tb, 512)], C.dx)
        stage_issue[i] = C.dx.n

    issue_stage(0)
    issue_stage(1)
    P.wait("tensor", C.dw)
    pi = 0
    for oc in range(NCH):
        for tb in range(S // 512):
            i = oc * 4 + tb
            sl = i % 2
            bk = [0, 1][pi % 2]
            pi += 1
            use_bank(P, C, bk)
            for ci in range(NCH):
                P.mm(C.bank[bk][:, :], wout[:, ci, bass.ts(oc, 128)],
                     attnout[:, ci, bass.ts(tb, 512)],
                     start=(ci == 0), stop=(ci == NCH - 1),
                     inc=C.pe if ci == NCH - 1 else None)
            P.wait("vector", C.pe)
            P.wait("vector", C.dx, stage_issue[i])
            P.op("vector", "tensor_scalar", sblk, C.bank[bk][:, :],
                 boutc[:, oc:oc + 1], None, ALU.add, inc=C.ve)
            release_bank(C, bk, C.ve)
            P.op("vector", "tensor_mul", sblk, sblk, gst[:, sl, :])
            P.op("vector", "tensor_add", xst[:, sl, :], xst[:, sl, :], sblk,
                 inc=C.ve)
            P.wait("sync", C.ve)
            P.dma("sync", x_dst_r[oc, :, bass.ts(tb, 512)], xst[:, sl, :], C.dx)
            stage_free[sl] = (C.dx, C.dx.n)
            if i + 2 < 32:
                issue_stage(i + 2)
    A.reset(m_layer)



W_L0 = ["wq", "wk", "wv", "w1", "w2", "b1c", "b2c"]
W_L1 = ["wq1", "wk1", "wv1", "wout1", "boutc", "wfc", "wproj", "uT", "vmat",
        "fbT", "gatec", "t1", "t2", "pswap"]


def build_program(layers=(0, 1), dbg=False):
    from contextlib import ExitStack
    nc = bass.Bass("TRN2", target_bir_lowering=False)
    io = Ctx()
    io.dbg = dbg
    io.x_in = nc.dram_tensor("x_in", [D, S], F32, kind="ExternalInput").ap()
    io.x_out = nc.dram_tensor("x_out", [D, S], F32, kind="ExternalOutput").ap()
    if layers == (0, 1):
        io.x_mid = nc.dram_tensor("x_mid", [D, S], F32, kind="Internal").ap()
    io.w = {}
    names = (W_L0 if 0 in layers else []) + (W_L1 if 1 in layers else [])
    shapes = dict(wq=(D, D), wk=(D, D), wv=(D, D), w1=(D, INTER), w2=(INTER, D),
                  b1c=(128, NI), b2c=(128, NCH), wq1=(D, D), wk1=(D, D),
                  wv1=(D, D), wout1=(D, D), boutc=(128, NCH), wfc=(D, INTER),
                  wproj=(INTER, D), uT=(RANK, INTER), vmat=(RANK, D),
                  fbT=(RANK, RANK), gatec=(16, 1), t1=(128, S), t2=(128, S),
                  pswap=(128, 128))
    for n in names:
        io.w[n] = nc.dram_tensor(n, list(shapes[n]), F32,
                                 kind="ExternalInput").ap()
    gkind = "ExternalOutput" if dbg else "Internal"
    if 0 in layers:
        io.g0 = nc.dram_tensor("g0", [NCH, 128, S], BF16, kind=gkind).ap()
    if 1 in layers:
        io.g1 = nc.dram_tensor("g1", [NCH, 128, S], BF16, kind=gkind).ap()
    if dbg:
        io.d_xn = nc.dram_tensor("d_xn", [128, NCH, S], BF16,
                                 kind="ExternalOutput").ap()
        io.d_q = nc.dram_tensor("d_q", [128, NCH, S], BF16,
                                kind="ExternalOutput").ap()
        io.d_k = nc.dram_tensor("d_k", [128, NCH, S], BF16,
                                kind="ExternalOutput").ap()
        io.d_v = nc.dram_tensor("d_v", [128, NTK, H, 65], BF16,
                                kind="ExternalOutput").ap()
        io.d_s = nc.dram_tensor("d_s", [NCH, 128, S], F32,
                                kind="ExternalOutput").ap()

    es = ExitStack()
    P = Prog(nc)
    A = Arena(nc)
    with nc.Block() as block:
        C = setup_common(P, A, es)
        if layers == (0, 1):
            emit_layer0(P, A, C, io, io.x_in, io.x_mid)
            emit_layer1(P, A, C, io, io.x_mid, io.x_out)
        elif layers == (0,):
            emit_layer0(P, A, C, io, io.x_in, io.x_out)
        elif layers == (1,):
            emit_layer1(P, A, C, io, io.x_in, io.x_out)
        P.wait("sync", C.dx)
        P.wait("sync", C.dw)
        P.op("sync", "nop")
        P.simulate()
        P.replay(block)
    es.close()
    return nc


def kernel(**inputs):
    g = host_prep(inputs)
    x = np.asarray(inputs["L0_attn_x"], dtype=np.float32)
    nc = build_program((0, 1))
    in_maps = []
    for b in range(8):
        m = dict(g)
        m["x_in"] = np.ascontiguousarray(x[b].T)
        in_maps.append(m)
    res = run_bass_kernel_spmd(nc, in_maps, core_ids=list(range(8)))
    out = np.stack([res.results[b]["x_out"].T for b in range(8)], axis=0)
    return out.astype(np.float32)


# version 11
# speedup vs baseline: 1.1720x; 1.0323x over previous
import numpy as np
import concourse.bass as bass
import concourse.mybir as mybir
from concourse.bass_utils import run_bass_kernel_spmd

F32 = mybir.dt.float32
BF16 = mybir.dt.bfloat16
AF = mybir.ActivationFunctionType
ALU = mybir.AluOpType

S = 2048
D = 1024
NCH = D // 128
H = 16
HD = 64
INTER = 4096
NI = INTER // 128
NTK = S // 128
RANK = 16
TQB = 256
NFB = S // TQB

ENGINES = ("tensor", "vector", "scalar", "gpsimd", "sync")


class Sem:
    __slots__ = ("h", "n", "name")

    def __init__(self, h, name):
        self.h = h
        self.n = 0
        self.name = name


class Prog:
    def __init__(self, nc):
        self.nc = nc
        self.recs = {e: [] for e in ENGINES}
        self._pend = {e: [] for e in ENGINES}
        self.sems = []

    def sem(self, name, es):
        s = Sem(es.enter_context(self.nc.semaphore(name)), name)
        self.sems.append(s)
        return s

    def wait(self, eng, sem, val=None):
        v = sem.n if val is None else val
        if v > 0:
            self._pend[eng].append((sem, v))

    def op(self, eng, meth, *args, inc=None, inc16=None, **kw):
        waits = self._pend[eng]
        self._pend[eng] = []
        incs = []
        if inc is not None:
            for s in (inc if isinstance(inc, (list, tuple)) else [inc]):
                s.n += 1
                incs.append((s, 1))
        if inc16 is not None:
            inc16.n += 16
            incs.append((inc16, 16))
        self.recs[eng].append((waits, meth, args, kw, incs))

    def dma(self, eng, out, in_, sem, **kw):
        self.op(eng, "dma_start", inc16=sem, out=out, in_=in_, **kw)

    def mm(self, out, lhsT, rhs, start=True, stop=True, inc=None, tp=None):
        self.op("tensor", "matmul", out, lhsT, rhs, start=start, stop=stop,
                skip_group_check=True, tile_position=tp, inc=inc)

    def barrier(self, sems):
        for e in ENGINES:
            for s in sems:
                self.wait(e, s)

    def simulate(self):
        pos = {e: 0 for e in ENGINES}
        val = {s.name: 0 for s in self.sems}
        total = sum(len(r) for r in self.recs.values())
        done = 0
        while done < total:
            progressed = False
            for e in ENGINES:
                while pos[e] < len(self.recs[e]):
                    waits, meth, args, kw, incs = self.recs[e][pos[e]]
                    if any(val[s.name] < v for s, v in waits):
                        break
                    for s, k in incs:
                        val[s.name] += k
                    pos[e] += 1
                    done += 1
                    progressed = True
            if not progressed:
                lines = []
                for e in ENGINES:
                    if pos[e] < len(self.recs[e]):
                        waits, meth, _, _, _ = self.recs[e][pos[e]]
                        ws = [(s.name, v, val[s.name]) for s, v in waits
                              if val[s.name] < v]
                        lines.append(f"{e}@{pos[e]}: {meth} blocked on {ws}")
                raise RuntimeError("DEADLOCK:\n" + "\n".join(lines))

    def replay(self, block):
        for e in ENGINES:
            recs = self.recs[e]

            def make(engine_recs):
                def emit(eng):
                    for waits, meth, args, kw, incs in engine_recs:
                        ww = {}
                        for s, v in waits:
                            if s.name in ww:
                                ww[s.name] = (s, max(v, ww[s.name][1]))
                            else:
                                ww[s.name] = (s, v)
                        for s, v in ww.values():
                            eng.wait_ge(s.h, v)
                        inst = getattr(eng, meth)(*args, **kw)
                        for s, k in incs:
                            inst = inst.then_inc(s.h, k)
                return emit
            getattr(block, e)(make(recs))


class Arena:
    def __init__(self, nc, nbytes=212608):
        self.t = nc.alloc_sbuf_tensor("arena", [128, nbytes], mybir.dt.uint8)
        self.nbytes = nbytes
        self.off = 0
        self.peak = 0

    def take(self, shape, dtype, name=""):
        esz = 4 if dtype == F32 else 2
        free = int(np.prod(shape[1:]))
        nb = free * esz
        off = (self.off + 31) // 32 * 32
        assert off + nb <= self.nbytes, \
            f"SBUF arena overflow at {name}: need {off + nb}, have {self.nbytes}"
        self.off = off + nb
        self.peak = max(self.peak, self.off)
        v = self.t.ap()[:, off:off + nb].bitcast(dtype)
        if len(shape) > 2:
            dims = " ".join(f"d{i}" for i in range(1, len(shape)))
            v = v.rearrange(f"p ({dims}) -> p {dims}",
                            **{f"d{i}": int(shape[i]) for i in range(1, len(shape))})
        if shape[0] != 128:
            v = v[0:shape[0]]
        return v

    def mark(self):
        return self.off

    def reset(self, mark):
        self.off = mark



def _evens_odds_perm():
    perm = np.zeros(D, dtype=np.int64)
    for h in range(H):
        perm[h * 64:h * 64 + 32] = h * 64 + np.arange(0, 64, 2)
        perm[h * 64 + 32:h * 64 + 64] = h * 64 + np.arange(1, 64, 2)
    return perm


def host_prep(inputs):
    g = {}
    T = lambda a: np.ascontiguousarray(np.asarray(a, dtype=np.float32).T)
    C = lambda a: np.ascontiguousarray(np.asarray(a, dtype=np.float32))
    g["wq"] = T(inputs["L0_attn_Wq"])
    g["wk"] = T(inputs["L0_attn_Wk"])
    g["wv"] = T(inputs["L0_attn_Wv"])
    g["w1"] = C(inputs["L0_ffn_W1"])
    g["w2"] = C(inputs["L0_ffn_W2"])
    g["b1c"] = np.ascontiguousarray(
        np.asarray(inputs["L0_ffn_b1"], np.float32).reshape(NI, 128).T)
    g["b2c"] = np.ascontiguousarray(
        np.asarray(inputs["L0_ffn_b2"], np.float32).reshape(NCH, 128).T)

    wqkv = T(inputs["L1_attn_w_qkv"])
    perm = _evens_odds_perm()
    g["wq1"] = np.ascontiguousarray(wqkv[:, 0:D][:, perm])
    g["wk1"] = np.ascontiguousarray(wqkv[:, D:2 * D][:, perm])
    g["wv1"] = np.ascontiguousarray(wqkv[:, 2 * D:3 * D])
    g["wout1"] = T(inputs["L1_attn_w_out"])
    g["boutc"] = np.ascontiguousarray(
        np.asarray(inputs["L1_attn_b_out"], np.float32).reshape(NCH, 128).T)
    g["wfc"] = T(inputs["L1_ffn_w_fc"])
    g["wproj"] = T(inputs["L1_ffn_w_proj"])
    g["uT"] = T(inputs["L1_ffn_u"])
    g["vmat"] = C(inputs["L1_ffn_v"])
    g["fbT"] = T(inputs["L1_ffn_fast_b"])
    g["gatec"] = np.full((16, 1), float(np.asarray(inputs["L1_ffn_gate"])),
                         dtype=np.float32)

    inv_freq = 1.0 / 10000.0 ** (np.arange(0, HD, 2, dtype=np.float32) / HD)
    t = np.arange(S, dtype=np.float32)
    fr = np.outer(t, inv_freq)
    cos = np.cos(fr).astype(np.float32).T
    sin = np.sin(fr).astype(np.float32).T
    g["t1"] = np.ascontiguousarray(np.vstack([cos, cos, cos, cos]))
    g["t2"] = np.ascontiguousarray(np.vstack([-sin, sin, -sin, sin]))
    sw = np.zeros((128, 128), dtype=np.float32)
    for blk in range(4):
        b = blk * 32
        dst = (b + 32) % 64 + (b // 64) * 64
        for i in range(32):
            sw[dst + i, b + i] = 1.0
    g["pswap"] = sw
    return g



class Ctx:
    pass


def setup_common(P, A, es):
    C = Ctx()
    nc = P.nc
    C.psum = es.enter_context(nc.psum_tensor("psum", [128, 4096], F32))
    C.bank = [C.psum[:, 512 * i:512 * (i + 1)] for i in range(8)]
    C.bank_free = [(None, 0)] * 8
    C.dw = P.sem("dw", es)
    C.dx = P.sem("dx", es)
    C.pe = P.sem("pe", es)
    C.ve = P.sem("ve", es)
    C.se = P.sem("se", es)
    C.ge = P.sem("ge", es)
    C.all_sems = [C.dw, C.dx, C.pe, C.ve, C.se, C.ge]
    C.ones_bf_col = A.take([128, 1], BF16, "ones_col")
    C.ones_f_row = A.take([1, 128], F32, "ones_f")
    C.ones_bf_row = A.take([1, 128], BF16, "ones_bfr")
    C.half_f_row = A.take([1, 128], F32, "half_f")
    C.nhalf_f_row = A.take([1, 128], F32, "nhalf_f")
    C.eps_t = A.take([1, 1], F32, "eps")
    C.ones2_bf = A.take([65, 128], BF16, "ones2")
    P.op("gpsimd", "memset", C.ones_bf_col, 1.0)
    P.op("gpsimd", "memset", C.ones_f_row, 1.0)
    P.op("gpsimd", "memset", C.ones_bf_row, 1.0)
    P.op("gpsimd", "memset", C.half_f_row, 0.5)
    P.op("gpsimd", "memset", C.eps_t, 1e-5)
    P.op("gpsimd", "memset", C.ones2_bf, 1.0)
    P.op("gpsimd", "memset", C.nhalf_f_row, -0.5, inc=C.ge)
    return C


def use_bank(P, C, i):
    s, n = C.bank_free[i]
    if s is not None:
        P.wait("tensor", s, n)


def release_bank(C, i, sem):
    C.bank_free[i] = (sem, sem.n)


def emit_ln(P, A, C, x_sb, xn_out):
    m0 = A.mark()
    xb = A.take([128, NCH, S], BF16, "ln_xb")
    sums = A.take([1, 2, S], F32, "ln_sums")
    mneg = A.take([1, S], F32, "ln_mneg")
    rstd = A.take([1, S], F32, "ln_rstd")
    mneg_b = A.take([128, S], F32, "ln_mnegb")
    rstd_b = A.take([128, S], F32, "ln_rstdb")
    scrf = A.take([128, S], F32, "ln_scr")
    bs, bb = 7, 6

    for c in range(NCH):
        P.op("vector", "tensor_copy", xb[:, c, :], x_sb[:, c, :], inc=C.ve)
    for tb in range(S // 512):
        use_bank(P, C, bs)
        P.wait("tensor", C.ve)
        for c in range(NCH):
            P.mm(C.bank[bs][0:1, :], C.ones_bf_col, xb[:, c, bass.ts(tb, 512)],
                 start=(c == 0), stop=(c == NCH - 1),
                 inc=C.pe if c == NCH - 1 else None)
        P.wait("vector", C.pe)
        P.op("vector", "tensor_copy", sums[0:1, 0, bass.ts(tb, 512)],
             C.bank[bs][0:1, :], inc=C.ve)
        release_bank(C, bs, C.ve)
    for c in range(NCH):
        P.wait("scalar", C.pe)
        P.op("scalar", "activation", xb[:, c, :], xb[:, c, :], AF.Square,
             inc=C.se)
    for tb in range(S // 512):
        use_bank(P, C, bs)
        P.wait("tensor", C.se)
        for c in range(NCH):
            P.mm(C.bank[bs][0:1, :], C.ones_bf_col, xb[:, c, bass.ts(tb, 512)],
                 start=(c == 0), stop=(c == NCH - 1),
                 inc=C.pe if c == NCH - 1 else None)
        P.wait("vector", C.pe)
        P.op("vector", "tensor_copy", sums[0:1, 1, bass.ts(tb, 512)],
             C.bank[bs][0:1, :], inc=C.ve)
        release_bank(C, bs, C.ve)
    P.op("vector", "tensor_scalar", mneg, sums[0:1, 0, :], -1.0 / D, None, ALU.mult)
    P.op("vector", "tensor_scalar", rstd, sums[0:1, 0, :], 1.0 / D, None, ALU.mult)
    P.op("vector", "tensor_mul", rstd, rstd, rstd)
    P.op("vector", "scalar_tensor_tensor", rstd, sums[0:1, 1, :], 1.0 / D, rstd,
         ALU.mult, ALU.subtract, inc=C.ve)
    P.wait("scalar", C.ve)
    P.op("scalar", "activation", rstd, rstd, AF.Sqrt, bias=C.eps_t, inc=C.se)
    P.wait("vector", C.se)
    P.op("vector", "reciprocal", rstd, rstd, inc=C.ve)
    for tb in range(S // 512):
        use_bank(P, C, bb)
        P.wait("tensor", C.ve)
        P.mm(C.bank[bb][:, :], C.ones_f_row, mneg[0:1, bass.ts(tb, 512)], inc=C.pe)
        P.wait("vector", C.pe)
        P.op("vector", "tensor_copy", mneg_b[:, bass.ts(tb, 512)], C.bank[bb][:, :],
             inc=C.ve)
        P.wait("tensor", C.ve)
        P.mm(C.bank[bb][:, :], C.ones_f_row, rstd[0:1, bass.ts(tb, 512)], inc=C.pe)
        P.wait("vector", C.pe)
        P.op("vector", "tensor_copy", rstd_b[:, bass.ts(tb, 512)], C.bank[bb][:, :],
             inc=C.ve)
        release_bank(C, bb, C.ve)
    for c in range(NCH):
        P.op("vector", "tensor_add", scrf, x_sb[:, c, :], mneg_b)
        P.op("vector", "tensor_mul", xn_out[:, c, :], scrf, rstd_b, inc=C.ve)
    A.reset(m0)


def emit_ffn(P, A, C, xn, w1_sb, w2_sb, g_hbm, layer, b1c=None, b2c=None):
    m0 = A.mark()
    rr = A.take([128, NI, TQB], BF16, "rr")
    gtile = A.take([128, NCH, TQB], BF16, "gtile")
    scrf = A.take([128, TQB], F32, "ffn_scr")
    f1b = [0, 1]
    f2b = [2, 3, 4, 5]
    gtile_free = C.dx.n
    for tb in range(NFB):
        for c2 in range(NI):
            bk = f1b[c2 % 2]
            use_bank(P, C, bk)
            for ci in range(NCH):
                P.mm(C.bank[bk][:, 0:TQB], w1_sb[:, ci, bass.ts(c2, 128)],
                     xn[:, ci, bass.ts(tb, TQB)],
                     start=(ci == 0), stop=(ci == NCH - 1),
                     inc=C.pe if ci == NCH - 1 else None)
            if layer == 0:
                P.wait("vector", C.pe)
                P.op("vector", "tensor_scalar", rr[:, c2, :], C.bank[bk][:, 0:TQB],
                     b1c[:, c2:c2 + 1], 0.0, ALU.add, ALU.max, inc=C.ve)
                release_bank(C, bk, C.ve)
            else:
                P.wait("vector", C.pe)
                P.op("vector", "tensor_scalar", scrf, C.bank[bk][:, 0:TQB],
                     0.0, None, ALU.max, inc=C.ve)
                release_bank(C, bk, C.ve)
                P.wait("scalar", C.ve)
                P.op("scalar", "activation", rr[:, c2, :], scrf, AF.Square,
                     inc=C.se)
        for oc in range(NCH):
            bk = f2b[oc % 4]
            use_bank(P, C, bk)
            if layer == 1:
                P.wait("tensor", C.se)
            else:
                P.wait("tensor", C.ve)
            for c2 in range(NI):
                P.mm(C.bank[bk][:, 0:TQB], w2_sb[:, c2, bass.ts(oc, 128)],
                     rr[:, c2, :],
                     start=(c2 == 0), stop=(c2 == NI - 1),
                     inc=C.pe if c2 == NI - 1 else None)
            P.wait("scalar", C.pe)
            P.wait("scalar", C.dx, gtile_free)
            if layer == 0:
                P.op("scalar", "activation", gtile[:, oc, :], C.bank[bk][:, 0:TQB],
                     AF.Silu, bias=b2c[:, oc:oc + 1], inc=C.se)
            else:
                P.op("scalar", "activation", gtile[:, oc, :], C.bank[bk][:, 0:TQB],
                     AF.Silu, inc=C.se)
            release_bank(C, bk, C.se)
        P.wait("sync", C.se)
        P.dma("sync", g_hbm[:, :, bass.ts(tb, TQB)].rearrange("c p t -> p c t"),
              gtile, C.dx)
        gtile_free = C.dx.n
    A.reset(m0)


def emit_qk_proj_l0(P, A, C, xn, w_sb, out_bf):
    bks = [0, 1]
    free_i = 0
    for oc in range(NCH):
        for tb in range(S // 512):
            bk = bks[free_i % 2]
            free_i += 1
            use_bank(P, C, bk)
            for ci in range(NCH):
                P.mm(C.bank[bk][:, :], w_sb[:, ci, bass.ts(oc, 128)],
                     xn[:, ci, bass.ts(tb, 512)],
                     start=(ci == 0), stop=(ci == NCH - 1),
                     inc=C.pe if ci == NCH - 1 else None)
            P.wait("vector", C.pe)
            P.op("vector", "tensor_copy", out_bf[:, oc, bass.ts(tb, 512)],
                 C.bank[bk][:, :], inc=C.ve)
            release_bank(C, bk, C.ve)


def emit_qk_proj_rope(P, A, C, xn, w_sb, out_bf, t1, t2, pswap_bf):
    m0 = A.mark()
    qpre = A.take([128, 512], BF16, "rope_qpre")
    s1 = A.take([128, 512], F32, "rope_s1")
    s2 = A.take([128, 512], F32, "rope_s2")
    pi = 0
    for oc in range(NCH):
        for tb in range(S // 512):
            bk = [0, 1][pi % 2]
            bk2 = [2, 3][pi % 2]
            pi += 1
            use_bank(P, C, bk)
            for ci in range(NCH):
                P.mm(C.bank[bk][:, :], w_sb[:, ci, bass.ts(oc, 128)],
                     xn[:, ci, bass.ts(tb, 512)],
                     start=(ci == 0), stop=(ci == NCH - 1),
                     inc=C.pe if ci == NCH - 1 else None)
            P.wait("vector", C.pe)
            P.op("vector", "tensor_copy", qpre, C.bank[bk][:, :], inc=C.ve)
            release_bank(C, bk, C.ve)
            use_bank(P, C, bk2)
            P.wait("tensor", C.ve)
            P.mm(C.bank[bk2][:, :], pswap_bf, qpre, inc=C.pe)
            P.op("vector", "tensor_mul", s1, qpre, t1[:, bass.ts(tb, 512)])
            P.wait("vector", C.pe)
            P.op("vector", "tensor_mul", s2, C.bank[bk2][:, :],
                 t2[:, bass.ts(tb, 512)], inc=C.ve)
            release_bank(C, bk2, C.ve)
            P.op("vector", "tensor_add", out_bf[:, oc, bass.ts(tb, 512)], s1, s2,
                 inc=C.ve)
    A.reset(m0)


def emit_v_proj(P, A, C, xn, wv_sb, v_aug, col_off=0):
    pi = 0
    for tk in range(NTK):
        for ob in range(2):
            bk = [0, 1][pi % 2]
            pi += 1
            use_bank(P, C, bk)
            for ci in range(NCH):
                P.mm(C.bank[bk][:, :], xn[:, ci, bass.ts(tk, 128)],
                     wv_sb[:, ci, col_off + ob * 512:col_off + (ob + 1) * 512],
                     start=(ci == 0), stop=(ci == NCH - 1),
                     inc=C.pe if ci == NCH - 1 else None)
            P.wait("vector", C.pe)
            P.op("vector", "tensor_copy",
                 v_aug[:, tk, ob * 8:(ob + 1) * 8, 0:64],
                 C.bank[bk][:, :].rearrange("p (h d) -> p h d", h=8), inc=C.ve)
            release_bank(C, bk, C.ve)


def emit_mshift(P, A, C, q_bf, hp, mshift, kmaxs, qsq_scr, msrow, inv_scale):
    bq = 7
    P.wait("scalar", C.pe)
    P.op("scalar", "activation", qsq_scr, q_bf[:, hp, :], AF.Square, inc=C.se)
    for side, base in ((0, 0), (1, 64)):
        h = 2 * hp + side
        use_bank(P, C, bq)
        P.wait("tensor", C.se)
        for tb in range(S // 512):
            P.mm(C.bank[bq][0:1, :], C.ones_bf_col[base:base + 64, :],
                 qsq_scr[base:base + 64, bass.ts(tb, 512)], inc=C.pe)
            P.wait("vector", C.pe)
            P.op("vector", "tensor_copy", msrow[0:1, bass.ts(tb, 512)],
                 C.bank[bq][0:1, :], inc=C.ve)
            P.wait("tensor", C.ve)
        release_bank(C, bq, C.ve)
        P.wait("scalar", C.ve)
        P.op("scalar", "activation", msrow, msrow, AF.Sqrt, inc=C.se)
        P.wait("vector", C.se)
        P.op("vector", "tensor_scalar", msrow, msrow,
             kmaxs[0:1, h:h + 1], -64.0, ALU.mult, ALU.add)
        P.op("vector", "tensor_scalar", msrow, msrow, 0.0, -float(inv_scale),
             ALU.max, ALU.mult)
        P.op("vector", "tensor_copy", mshift[base:base + 1, :], msrow, inc=C.ve)


def emit_kmaxs(P, A, C, k_bf, kmaxs, scr, row, inv_scale):
    m0 = A.mark()
    one = A.take([1, 1], F32, "km_one")
    for hp in range(NCH):
        P.wait("scalar", C.pe)
        P.wait("scalar", C.ve)
        P.op("scalar", "activation", scr, k_bf[:, hp, :], AF.Square, inc=C.se)
        for side, base in ((0, 0), (1, 64)):
            h = 2 * hp + side
            bq = 6 + side
            use_bank(P, C, bq)
            P.wait("tensor", C.se)
            for tb in range(S // 512):
                P.mm(C.bank[bq][0:1, :], C.ones_bf_col[base:base + 64, :],
                     scr[base:base + 64, bass.ts(tb, 512)], inc=C.pe)
                P.wait("vector", C.pe)
                P.op("vector", "tensor_copy", row[0:1, bass.ts(tb, 512)],
                     C.bank[bq][0:1, :], inc=C.ve)
                P.wait("tensor", C.ve)
            release_bank(C, bq, C.ve)
            P.op("vector", "reduce_max", one, row, mybir.AxisListType.X, inc=C.ve)
            P.wait("scalar", C.ve)
            P.op("scalar", "activation", one, one, AF.Sqrt, inc=C.se)
            P.wait("vector", C.se)
            P.op("vector", "tensor_scalar", kmaxs[0:1, h:h + 1], one,
                 1.0 / float(inv_scale), None, ALU.mult, inc=C.ve)
    A.reset(m0)


def emit_attention(P, A, C, q_bf, k_bf, v_aug, scale, consume, shifted=False):
    m0 = A.mark()
    recips = A.take([1, 2, 512], F32, "recips")
    rb = A.take([128, 512], F32, "rb")
    s_blk = A.take([128, 512], F32, "s_blk")
    if shifted:
        inv_scale = 1.0 / scale
        mshift = A.take([65, S], BF16, "mshift")
        kmaxs = A.take([1, H], F32, "kmaxs")
        qsq_scr = A.take([128, S], BF16, "qsq")
        msrow = A.take([1, S], F32, "msrow")
        emit_kmaxs(P, A, C, k_bf, kmaxs, qsq_scr, msrow, inv_scale)
    rtmp = A.take([1, 2, 512], F32, "rtmp")
    ptab = [A.take([128, 1024], BF16, "ptab0"),
            A.take([128, 1024], BF16, "ptab1")]
    pv_pairs = [(4, 5), (6, 7)]

    def emit_scores(hp, tq, tk):
        p = tk % 2
        bA, bB = 2 * p, 2 * p + 1
        use_bank(P, C, bA)
        use_bank(P, C, bB)
        if shifted:
            P.mm(C.bank[bA], C.ones2_bf[0:1, :], mshift[0:1, tq],
                 start=True, stop=False)
            P.mm(C.bank[bB], C.ones2_bf[64:65, :], mshift[64:65, tq],
                 start=True, stop=False)
            P.mm(C.bank[bA], k_bf[0:64, hp, bass.ts(tk, 128)],
                 q_bf[0:64, hp, tq], start=False, stop=True, tp=(0, 0))
            P.mm(C.bank[bB], k_bf[64:128, hp, bass.ts(tk, 128)],
                 q_bf[64:128, hp, tq], start=False, stop=True, tp=(64, 0),
                 inc=C.pe)
        else:
            P.mm(C.bank[bA], k_bf[0:64, hp, bass.ts(tk, 128)],
                 q_bf[0:64, hp, tq], start=True, stop=True, tp=(0, 0))
            P.mm(C.bank[bB], k_bf[64:128, hp, bass.ts(tk, 128)],
                 q_bf[64:128, hp, tq], start=True, stop=True, tp=(64, 0),
                 inc=C.pe)

    def emit_exp(tk):
        p = tk % 2
        P.wait("scalar", C.pe)
        P.op("scalar", "activation", ptab[p],
             C.psum[:, 1024 * p:1024 * (p + 1)], AF.Exp, scale=scale,
             inc=C.se)
        release_bank(C, 2 * p, C.se)
        release_bank(C, 2 * p + 1, C.se)
        return C.se.n

    def emit_pv(hp, tk, se_need, bOA, bOB):
        p = tk % 2
        P.wait("tensor", C.se, se_need)
        P.mm(C.bank[bOA][0:65, :], v_aug[:, tk, 2 * hp, :], ptab[p][:, 0:512],
             start=(tk == 0), stop=(tk == NTK - 1))
        P.mm(C.bank[bOB][0:65, :], v_aug[:, tk, 2 * hp + 1, :],
             ptab[p][:, 512:1024],
             start=(tk == 0), stop=(tk == NTK - 1), inc=C.pe)

    pending = [None]

    def norm_part2(hp, tqb, bOA, bOB):
        use_bank(P, C, bOA)
        use_bank(P, C, bOB)
        P.wait("tensor", C.ve)
        P.mm(C.bank[bOA][0:64, :], C.ones_f_row[0:1, 0:64], recips[0:1, 0, :])
        P.mm(C.bank[bOB][0:64, :], C.ones_f_row[0:1, 0:64], recips[0:1, 1, :],
             inc=C.pe)
        P.wait("vector", C.pe)
        P.op("vector", "tensor_copy", rb[0:64, :], C.bank[bOA][0:64, :])
        P.op("vector", "tensor_copy", rb[64:128, :], C.bank[bOB][0:64, :],
             inc=C.ve)
        release_bank(C, bOA, C.ve)
        release_bank(C, bOB, C.ve)
        P.op("vector", "tensor_mul", s_blk[0:64, :], s_blk[0:64, :],
             rb[0:64, :])
        P.op("vector", "tensor_mul", s_blk[64:128, :], s_blk[64:128, :],
             rb[64:128, :], inc=C.ve)
        consume(hp, tqb, s_blk)

    for hp in range(NCH):
        if shifted:
            emit_mshift(P, A, C, q_bf, hp, mshift, kmaxs, qsq_scr, msrow,
                        inv_scale)
            P.wait("tensor", C.ve)
        for tqb in range(S // 512):
            tq = bass.ts(tqb, 512)
            it = hp * 4 + tqb
            bOA, bOB = pv_pairs[it % 2]
            se_at = {}
            for tk in range(NTK):
                emit_scores(hp, tq, tk)
                if tk == 0:
                    use_bank(P, C, bOA)
                    use_bank(P, C, bOB)
                se_at[tk] = emit_exp(tk)
                if tk > 0:
                    emit_pv(hp, tk - 1, se_at[tk - 1], bOA, bOB)
                if tk == 10 and pending[0] is not None:
                    norm_part2(*pending[0])
                    pending[0] = None
            emit_pv(hp, NTK - 1, se_at[NTK - 1], bOA, bOB)
            P.wait("vector", C.pe)
            P.op("vector", "tensor_scalar", rtmp[0:1, 0, :],
                 C.bank[bOA][64:65, :], 1e-36, None, ALU.add)
            P.op("vector", "tensor_scalar", rtmp[0:1, 1, :],
                 C.bank[bOB][64:65, :], 1e-36, None, ALU.add)
            P.op("vector", "tensor_copy", s_blk[0:64, :], C.bank[bOA][0:64, :])
            P.op("vector", "tensor_copy", s_blk[64:128, :],
                 C.bank[bOB][0:64, :], inc=C.ve)
            release_bank(C, bOA, C.ve)
            release_bank(C, bOB, C.ve)
            P.op("vector", "reciprocal", recips[0:1, 0, :], rtmp[0:1, 0, :])
            P.op("vector", "reciprocal", recips[0:1, 1, :], rtmp[0:1, 1, :],
                 inc=C.ve)
            pending[0] = (hp, tqb, bOA, bOB)
    norm_part2(*pending[0])
    pending[0] = None
    A.reset(m0)


def emit_ternarize(P, A, C, w_hbm, n_in, n_out, weff_sb, tag):
    m0 = A.mark()
    nic = n_in // 128
    nob = n_out // 512
    chunk_f = [A.take([128, n_out], F32, f"tn_cf0_{tag}"),
               A.take([128, n_out], F32, f"tn_cf1_{tag}")]
    chunk_abs = [A.take([128, n_out], BF16, f"tn_ca0_{tag}"),
                 A.take([128, n_out], BF16, f"tn_ca1_{tag}")]
    srow = A.take([1, n_out], F32, f"tn_sr_{tag}")
    bh = A.take([128, n_out], F32, f"tn_bh_{tag}")
    bhn = A.take([128, n_out], F32, f"tn_bhn_{tag}")
    sbc = A.take([128, n_out], BF16, f"tn_sbc_{tag}")
    mneg = chunk_abs[0]
    w_r = w_hbm.rearrange("(c p) o -> p c o", p=128)
    cf_free = [C.se.n, C.se.n]
    ca_free = [C.pe.n, C.pe.n]
    for c in range(nic):
        p = c % 2
        P.wait("sync", C.se, cf_free[p])
        P.dma("sync", chunk_f[p], w_r[:, c, :], C.dx)
        P.wait("scalar", C.dx)
        P.wait("scalar", C.pe, ca_free[p])
        P.op("scalar", "activation", chunk_abs[p], chunk_f[p], AF.Abs, inc=C.se)
        cf_free[p] = C.se.n
        P.wait("tensor", C.se)
        for ob in range(nob):
            if c == 0:
                use_bank(P, C, ob)
            P.mm(C.bank[ob][0:1, :], C.ones_bf_col,
                 chunk_abs[p][:, bass.ts(ob, 512)],
                 start=(c == 0), stop=(c == nic - 1),
                 inc=C.pe if ob == nob - 1 else None)
        ca_free[p] = C.pe.n
    P.wait("vector", C.pe)
    for ob in range(nob):
        P.op("vector", "tensor_scalar", srow[0:1, bass.ts(ob, 512)],
             C.bank[ob][0:1, :], 1.0 / n_in, 1e-5, ALU.mult, ALU.max,
             inc=C.ve if ob == nob - 1 else None)
    for ob in range(nob):
        release_bank(C, ob, C.ve)
    for ob in range(nob):
        for row_const, dst in ((C.half_f_row, bh), (C.nhalf_f_row, bhn),
                               (C.ones_f_row, None)):
            bk = ob % 2
            use_bank(P, C, bk)
            P.wait("tensor", C.ve)
            P.mm(C.bank[bk], row_const, srow[0:1, bass.ts(ob, 512)], inc=C.pe)
            P.wait("vector", C.pe)
            out_ap = (sbc if dst is None else dst)[:, bass.ts(ob, 512)]
            P.op("vector", "tensor_copy", out_ap, C.bank[bk], inc=C.ve)
            release_bank(C, bk, C.ve)
    cf2_free = [(C.ve.n, C.ge.n), (C.ve.n, C.ge.n)]
    P.wait("gpsimd", C.ve)
    for c in range(nic):
        p = c % 2
        ve_n, ge_n = cf2_free[p]
        P.wait("sync", C.ve, ve_n)
        P.wait("sync", C.ge, ge_n)
        P.dma("sync", chunk_f[p], w_r[:, c, :], C.dx)
        wch = weff_sb[:, c, :]
        P.wait("vector", C.dx)
        P.op("vector", "tensor_tensor", wch, chunk_f[p], bh, ALU.is_ge)
        P.op("vector", "tensor_tensor", mneg, chunk_f[p], bhn, ALU.is_le)
        P.op("vector", "tensor_sub", wch, wch, mneg)
        P.op("vector", "tensor_mul", wch, wch, sbc, inc=C.ve)
        cf2_free[p] = (C.ve.n, C.ge.n)
    A.reset(m0)


def emit_lowrank_fold(P, A, C, io, weff_fc):
    m0 = A.mark()
    u_sb = A.take([16, INTER], BF16, "lr_u")
    v_sb = A.take([16, D], BF16, "lr_v")
    fb_sb = A.take([16, RANK], F32, "lr_fb")
    gate_c = A.take([16, 1], F32, "lr_gate")
    gfb = A.take([16, RANK], BF16, "lr_gfb")
    at = A.take([16, D], BF16, "lr_at")
    P.dma("gpsimd", u_sb, io.w["uT"], C.dw)
    P.dma("gpsimd", v_sb, io.w["vmat"], C.dw)
    P.dma("gpsimd", fb_sb, io.w["fbT"], C.dw)
    P.dma("gpsimd", gate_c, io.w["gatec"], C.dw)
    P.wait("vector", C.dw)
    P.op("vector", "tensor_scalar", gfb, fb_sb, gate_c, None, ALU.mult, inc=C.ve)
    use_bank(P, C, 0)
    use_bank(P, C, 1)
    P.wait("tensor", C.ve)
    P.mm(C.bank[0][0:16, :], gfb, v_sb[:, 0:512], start=True, stop=True)
    P.mm(C.bank[1][0:16, :], gfb, v_sb[:, 512:1024], start=True, stop=True,
         inc=C.pe)
    P.wait("vector", C.pe)
    P.op("vector", "tensor_copy", at[:, 0:512], C.bank[0][0:16, :])
    P.op("vector", "tensor_copy", at[:, 512:1024], C.bank[1][0:16, :], inc=C.ve)
    release_bank(C, 0, C.ve)
    release_bank(C, 1, C.ve)
    for ci in range(NCH):
        for ob in range(INTER // 512):
            bk = ob % 2
            use_bank(P, C, bk)
            P.wait("tensor", C.ve)
            P.mm(C.bank[bk][:, :], at[:, bass.ts(ci, 128)],
                 u_sb[:, bass.ts(ob, 512)], inc=C.pe)
            P.wait("vector", C.pe)
            P.op("vector", "tensor_add", weff_fc[:, ci, bass.ts(ob, 512)],
                 weff_fc[:, ci, bass.ts(ob, 512)], C.bank[bk][:, :], inc=C.ve)
            release_bank(C, bk, C.ve)
    A.reset(m0)



def emit_layer0(P, A, C, io, x_src, x_dst):
    m_layer = A.mark()
    xn = A.take([128, NCH, S], BF16, "xn0")
    x_src_r = x_src.rearrange("(c p) t -> p c t", p=128)
    x_dst_r = x_dst.rearrange("(c p) t -> c p t", p=128)
    g_hbm = io.g0

    m0 = A.mark()
    x_sb = A.take([128, NCH, S], F32, "x0")
    P.dma("sync", x_sb, x_src_r, C.dx)
    P.wait("vector", C.dx)
    P.wait("vector", C.ge)
    P.wait("tensor", C.ge)
    emit_ln(P, A, C, x_sb, xn)
    A.reset(m0)
    P.barrier(C.all_sems)

    m1 = A.mark()
    w1_sb = A.take([128, NCH, INTER], BF16, "w1")
    w2_sb = A.take([128, NI, D], BF16, "w2")
    b1c = A.take([128, NI], F32, "b1c")
    b2c = A.take([128, NCH], F32, "b2c")
    P.dma("gpsimd", w1_sb, io.w["w1"].rearrange("(c p) o -> p c o", p=128), C.dw)
    P.dma("gpsimd", w2_sb, io.w["w2"].rearrange("(c p) o -> p c o", p=128), C.dw)
    P.dma("sync", b1c, io.w["b1c"], C.dx)
    P.dma("sync", b2c, io.w["b2c"], C.dx)
    P.wait("tensor", C.dw)
    P.wait("vector", C.dx)
    P.wait("scalar", C.dx)
    emit_ffn(P, A, C, xn, w1_sb, w2_sb, g_hbm, 0, b1c, b2c)
    A.reset(m1)
    P.barrier(C.all_sems)

    q_bf = A.take([128, NCH, S], BF16, "q0")
    k_bf = A.take([128, NCH, S], BF16, "k0")
    v_aug = A.take([128, NTK, H, 65], BF16, "v0")
    m2 = A.mark()
    wslot = [A.take([128, NCH, D], BF16, "ws0"),
             A.take([128, NCH, D], BF16, "ws1")]
    P.op("gpsimd", "memset", v_aug[:, :, :, 64:65], 1.0, inc=C.ge)
    P.dma("gpsimd", wslot[0], io.w["wq"].rearrange("(c p) o -> p c o", p=128), C.dw)
    P.wait("tensor", C.dw)
    emit_qk_proj_l0(P, A, C, xn, wslot[0], q_bf)
    P.dma("gpsimd", wslot[1], io.w["wk"].rearrange("(c p) o -> p c o", p=128), C.dw)
    P.wait("tensor", C.dw)
    emit_qk_proj_l0(P, A, C, xn, wslot[1], k_bf)
    P.wait("gpsimd", C.pe)
    P.dma("gpsimd", wslot[0], io.w["wv"].rearrange("(c p) o -> p c o", p=128), C.dw)
    P.wait("tensor", C.dw)
    P.wait("tensor", C.ge)
    emit_v_proj(P, A, C, xn, wslot[0], v_aug)
    A.reset(m2)
    P.barrier(C.all_sems)
    if io.dbg:
        P.dma("sync", io.d_xn, xn, C.dx)
        P.dma("sync", io.d_q, q_bf, C.dx)
        P.dma("sync", io.d_k, k_bf, C.dx)
        P.dma("sync", io.d_v, v_aug, C.dx)

    xst = A.take([128, 2, 512], F32, "xstage")
    gst = A.take([128, 2, 512], BF16, "gstage")
    stage_free = [(None, 0), (None, 0)]
    g_hbm_r = g_hbm
    x_src_cr = x_src.rearrange("(c p) t -> c p t", p=128)

    def consume0(hp, tqb, s_blk):
        sl = (hp * 4 + tqb) % 2
        if io.dbg:
            P.wait("sync", C.ve)
            P.dma("sync", io.d_s[hp, :, bass.ts(tqb, 512)], s_blk, C.dx)
            P.wait("vector", C.dx)
        P.wait("vector", C.dx, stage_issue[(hp, tqb)])
        P.op("vector", "tensor_mul", s_blk, s_blk, gst[:, sl, :])
        P.op("vector", "tensor_add", xst[:, sl, :], xst[:, sl, :], s_blk,
             inc=C.ve)
        P.wait("sync", C.ve)
        P.dma("sync", x_dst_r[hp, :, bass.ts(tqb, 512)], xst[:, sl, :], C.dx)
        stage_free[sl] = (C.dx, C.dx.n)

    stage_issue = {}

    def consume_wrap(hp, tqb, s_blk):
        consume0(hp, tqb, s_blk)
        nxt = hp * 4 + tqb + 1
        if nxt < 32:
            issue_stage(nxt // 4, nxt % 4)

    def issue_stage(hp, tqb):
        sl = (hp * 4 + tqb) % 2
        s, n = stage_free[sl]
        if s is not None:
            P.wait("sync", s, n)
        P.dma("sync", xst[:, sl, :], x_src_cr[hp, :, bass.ts(tqb, 512)], C.dx)
        P.dma("sync", gst[:, sl, :], g_hbm_r[hp, :, bass.ts(tqb, 512)], C.dx)
        stage_issue[(hp, tqb)] = C.dx.n

    issue_stage(0, 0)
    issue_stage(0, 1)
    emit_attention(P, A, C, q_bf, k_bf, v_aug, 1.0 / 32.0, consume_wrap)
    A.reset(m_layer)
    P.barrier(C.all_sems)


def emit_layer1(P, A, C, io, x_src, x_dst):
    m_layer = A.mark()
    xn = A.take([128, NCH, S], BF16, "xn1")
    x_src_r = x_src.rearrange("(c p) t -> p c t", p=128)
    x_src_cr = x_src.rearrange("(c p) t -> c p t", p=128)
    x_dst_r = x_dst.rearrange("(c p) t -> c p t", p=128)
    g_hbm = io.g1

    m0 = A.mark()
    x_sb = A.take([128, NCH, S], F32, "x1")
    P.dma("sync", x_sb, x_src_r, C.dx)
    P.wait("vector", C.dx)
    emit_ln(P, A, C, x_sb, xn)
    A.reset(m0)
    P.barrier(C.all_sems)

    weff_fc = A.take([128, NCH, INTER], BF16, "weff_fc")
    emit_ternarize(P, A, C, io.w["wfc"], D, INTER, weff_fc, "fc")
    P.barrier(C.all_sems)
    emit_lowrank_fold(P, A, C, io, weff_fc)
    P.barrier(C.all_sems)
    weff_pj = A.take([128, NI, D], BF16, "weff_pj")
    emit_ternarize(P, A, C, io.w["wproj"], INTER, D, weff_pj, "pj")
    P.barrier(C.all_sems)
    emit_ffn(P, A, C, xn, weff_fc, weff_pj, g_hbm, 1)
    A.reset(m_layer)
    xn = A.take([128, NCH, S], BF16, "xn1_alias")
    P.barrier(C.all_sems)

    t1 = A.take([128, S], BF16, "t1")
    t2 = A.take([128, S], BF16, "t2")
    pswap = A.take([128, 128], BF16, "pswap")
    attnout = A.take([128, NCH, S], BF16, "attnout")
    q_bf = A.take([128, NCH, S], BF16, "q1")
    k_bf = A.take([128, NCH, S], BF16, "k1")
    v_aug = A.take([128, NTK, H, 65], BF16, "v1")
    m2 = A.mark()
    wslot = A.take([128, NCH, D], BF16, "w1s0")
    P.dma("gpsimd", t1, io.w["t1"], C.dw)
    P.dma("gpsimd", t2, io.w["t2"], C.dw)
    P.dma("gpsimd", pswap, io.w["pswap"], C.dw)
    P.op("gpsimd", "memset", v_aug[:, :, :, 64:65], 1.0, inc=C.ge)
    P.dma("gpsimd", wslot, io.w["wq1"].rearrange("(c p) o -> p c o", p=128),
          C.dw)
    P.wait("tensor", C.dw)
    P.wait("vector", C.dw)
    emit_qk_proj_rope(P, A, C, xn, wslot, q_bf, t1, t2, pswap)
    P.wait("gpsimd", C.pe)
    P.dma("gpsimd", wslot, io.w["wk1"].rearrange("(c p) o -> p c o", p=128),
          C.dw)
    P.wait("tensor", C.dw)
    emit_qk_proj_rope(P, A, C, xn, wslot, k_bf, t1, t2, pswap)
    P.wait("gpsimd", C.pe)
    P.dma("gpsimd", wslot, io.w["wv1"].rearrange("(c p) o -> p c o", p=128),
          C.dw)
    P.wait("tensor", C.dw)
    P.wait("tensor", C.ge)
    emit_v_proj(P, A, C, xn, wslot, v_aug)
    A.reset(m2)
    P.barrier(C.all_sems)

    def consume1(hp, tqb, s_blk):
        P.op("vector", "tensor_copy", attnout[:, hp, bass.ts(tqb, 512)], s_blk,
             inc=C.ve)

    emit_attention(P, A, C, q_bf, k_bf, v_aug, 1.0 / 8.0, consume1, shifted=True)
    P.barrier(C.all_sems)

    m3 = A.mark()
    wout = A.take([128, NCH, D], BF16, "wout")
    boutc = A.take([128, NCH], F32, "boutc")
    xst = A.take([128, 2, 512], F32, "xst1")
    gst = A.take([128, 2, 512], BF16, "gst1")
    sblk = A.take([128, 512], F32, "sblk1")
    P.dma("gpsimd", wout, io.w["wout1"].rearrange("(c p) o -> p c o", p=128), C.dw)
    P.dma("sync", boutc, io.w["boutc"], C.dx)
    stage_free = [(None, 0), (None, 0)]
    stage_issue = {}

    def issue_stage(i):
        oc, tb = i // 4, i % 4
        sl = i % 2
        s, n = stage_free[sl]
        if s is not None:
            P.wait("sync", s, n)
        P.dma("sync", xst[:, sl, :], x_src_cr[oc, :, bass.ts(tb, 512)], C.dx)
        P.dma("sync", gst[:, sl, :], g_hbm[oc, :, bass.ts(tb, 512)], C.dx)
        stage_issue[i] = C.dx.n

    issue_stage(0)
    issue_stage(1)
    P.wait("tensor", C.dw)
    pi = 0
    for oc in range(NCH):
        for tb in range(S // 512):
            i = oc * 4 + tb
            sl = i % 2
            bk = [0, 1][pi % 2]
            pi += 1
            use_bank(P, C, bk)
            for ci in range(NCH):
                P.mm(C.bank[bk][:, :], wout[:, ci, bass.ts(oc, 128)],
                     attnout[:, ci, bass.ts(tb, 512)],
                     start=(ci == 0), stop=(ci == NCH - 1),
                     inc=C.pe if ci == NCH - 1 else None)
            P.wait("vector", C.pe)
            P.wait("vector", C.dx, stage_issue[i])
            P.op("vector", "tensor_scalar", sblk, C.bank[bk][:, :],
                 boutc[:, oc:oc + 1], None, ALU.add, inc=C.ve)
            release_bank(C, bk, C.ve)
            P.op("vector", "tensor_mul", sblk, sblk, gst[:, sl, :])
            P.op("vector", "tensor_add", xst[:, sl, :], xst[:, sl, :], sblk,
                 inc=C.ve)
            P.wait("sync", C.ve)
            P.dma("sync", x_dst_r[oc, :, bass.ts(tb, 512)], xst[:, sl, :], C.dx)
            stage_free[sl] = (C.dx, C.dx.n)
            if i + 2 < 32:
                issue_stage(i + 2)
    A.reset(m_layer)



W_L0 = ["wq", "wk", "wv", "w1", "w2", "b1c", "b2c"]
W_L1 = ["wq1", "wk1", "wv1", "wout1", "boutc", "wfc", "wproj", "uT", "vmat",
        "fbT", "gatec", "t1", "t2", "pswap"]


def build_program(layers=(0, 1), dbg=False):
    from contextlib import ExitStack
    nc = bass.Bass("TRN2", target_bir_lowering=False)
    io = Ctx()
    io.dbg = dbg
    io.x_in = nc.dram_tensor("x_in", [D, S], F32, kind="ExternalInput").ap()
    io.x_out = nc.dram_tensor("x_out", [D, S], F32, kind="ExternalOutput").ap()
    if layers == (0, 1):
        io.x_mid = nc.dram_tensor("x_mid", [D, S], F32, kind="Internal").ap()
    io.w = {}
    names = (W_L0 if 0 in layers else []) + (W_L1 if 1 in layers else [])
    shapes = dict(wq=(D, D), wk=(D, D), wv=(D, D), w1=(D, INTER), w2=(INTER, D),
                  b1c=(128, NI), b2c=(128, NCH), wq1=(D, D), wk1=(D, D),
                  wv1=(D, D), wout1=(D, D), boutc=(128, NCH), wfc=(D, INTER),
                  wproj=(INTER, D), uT=(RANK, INTER), vmat=(RANK, D),
                  fbT=(RANK, RANK), gatec=(16, 1), t1=(128, S), t2=(128, S),
                  pswap=(128, 128))
    for n in names:
        io.w[n] = nc.dram_tensor(n, list(shapes[n]), F32,
                                 kind="ExternalInput").ap()
    gkind = "ExternalOutput" if dbg else "Internal"
    if 0 in layers:
        io.g0 = nc.dram_tensor("g0", [NCH, 128, S], BF16, kind=gkind).ap()
    if 1 in layers:
        io.g1 = nc.dram_tensor("g1", [NCH, 128, S], BF16, kind=gkind).ap()
    if dbg:
        io.d_xn = nc.dram_tensor("d_xn", [128, NCH, S], BF16,
                                 kind="ExternalOutput").ap()
        io.d_q = nc.dram_tensor("d_q", [128, NCH, S], BF16,
                                kind="ExternalOutput").ap()
        io.d_k = nc.dram_tensor("d_k", [128, NCH, S], BF16,
                                kind="ExternalOutput").ap()
        io.d_v = nc.dram_tensor("d_v", [128, NTK, H, 65], BF16,
                                kind="ExternalOutput").ap()
        io.d_s = nc.dram_tensor("d_s", [NCH, 128, S], F32,
                                kind="ExternalOutput").ap()

    es = ExitStack()
    P = Prog(nc)
    A = Arena(nc)
    with nc.Block() as block:
        C = setup_common(P, A, es)
        if layers == (0, 1):
            emit_layer0(P, A, C, io, io.x_in, io.x_mid)
            emit_layer1(P, A, C, io, io.x_mid, io.x_out)
        elif layers == (0,):
            emit_layer0(P, A, C, io, io.x_in, io.x_out)
        elif layers == (1,):
            emit_layer1(P, A, C, io, io.x_in, io.x_out)
        P.wait("sync", C.dx)
        P.wait("sync", C.dw)
        P.op("sync", "nop")
        P.simulate()
        P.replay(block)
    es.close()
    return nc


def kernel(**inputs):
    g = host_prep(inputs)
    x = np.asarray(inputs["L0_attn_x"], dtype=np.float32)
    nc = build_program((0, 1))
    in_maps = []
    for b in range(8):
        m = dict(g)
        m["x_in"] = np.ascontiguousarray(x[b].T)
        in_maps.append(m)
    res = run_bass_kernel_spmd(nc, in_maps, core_ids=list(range(8)))
    out = np.stack([res.results[b]["x_out"].T for b in range(8)], axis=0)
    return out.astype(np.float32)


# version 12
# speedup vs baseline: 1.1931x; 1.0180x over previous
import numpy as np
import concourse.bass as bass
import concourse.mybir as mybir
from concourse.bass_utils import run_bass_kernel_spmd

F32 = mybir.dt.float32
BF16 = mybir.dt.bfloat16
AF = mybir.ActivationFunctionType
ALU = mybir.AluOpType

S = 2048
D = 1024
NCH = D // 128
H = 16
HD = 64
INTER = 4096
NI = INTER // 128
NTK = S // 128
RANK = 16
TQB = 256
NFB = S // TQB

ENGINES = ("tensor", "vector", "scalar", "gpsimd", "sync")


class Sem:
    __slots__ = ("h", "n", "name")

    def __init__(self, h, name):
        self.h = h
        self.n = 0
        self.name = name


class Prog:
    def __init__(self, nc):
        self.nc = nc
        self.recs = {e: [] for e in ENGINES}
        self._pend = {e: [] for e in ENGINES}
        self.sems = []

    def sem(self, name, es):
        s = Sem(es.enter_context(self.nc.semaphore(name)), name)
        self.sems.append(s)
        return s

    def wait(self, eng, sem, val=None):
        v = sem.n if val is None else val
        if v > 0:
            self._pend[eng].append((sem, v))

    def op(self, eng, meth, *args, inc=None, inc16=None, **kw):
        waits = self._pend[eng]
        self._pend[eng] = []
        incs = []
        if inc is not None:
            for s in (inc if isinstance(inc, (list, tuple)) else [inc]):
                s.n += 1
                incs.append((s, 1))
        if inc16 is not None:
            inc16.n += 16
            incs.append((inc16, 16))
        self.recs[eng].append((waits, meth, args, kw, incs))

    def dma(self, eng, out, in_, sem, **kw):
        self.op(eng, "dma_start", inc16=sem, out=out, in_=in_, **kw)

    def mm(self, out, lhsT, rhs, start=True, stop=True, inc=None, tp=None):
        self.op("tensor", "matmul", out, lhsT, rhs, start=start, stop=stop,
                skip_group_check=True, tile_position=tp, inc=inc)

    def barrier(self, sems):
        for e in ENGINES:
            for s in sems:
                self.wait(e, s)

    def simulate(self):
        pos = {e: 0 for e in ENGINES}
        val = {s.name: 0 for s in self.sems}
        total = sum(len(r) for r in self.recs.values())
        done = 0
        while done < total:
            progressed = False
            for e in ENGINES:
                while pos[e] < len(self.recs[e]):
                    waits, meth, args, kw, incs = self.recs[e][pos[e]]
                    if any(val[s.name] < v for s, v in waits):
                        break
                    for s, k in incs:
                        val[s.name] += k
                    pos[e] += 1
                    done += 1
                    progressed = True
            if not progressed:
                lines = []
                for e in ENGINES:
                    if pos[e] < len(self.recs[e]):
                        waits, meth, _, _, _ = self.recs[e][pos[e]]
                        ws = [(s.name, v, val[s.name]) for s, v in waits
                              if val[s.name] < v]
                        lines.append(f"{e}@{pos[e]}: {meth} blocked on {ws}")
                raise RuntimeError("DEADLOCK:\n" + "\n".join(lines))

    def replay(self, block):
        for e in ENGINES:
            recs = self.recs[e]

            def make(engine_recs):
                def emit(eng):
                    for waits, meth, args, kw, incs in engine_recs:
                        ww = {}
                        for s, v in waits:
                            if s.name in ww:
                                ww[s.name] = (s, max(v, ww[s.name][1]))
                            else:
                                ww[s.name] = (s, v)
                        for s, v in ww.values():
                            eng.wait_ge(s.h, v)
                        inst = getattr(eng, meth)(*args, **kw)
                        for s, k in incs:
                            inst = inst.then_inc(s.h, k)
                return emit
            getattr(block, e)(make(recs))


class Arena:
    def __init__(self, nc, nbytes=212608):
        self.t = nc.alloc_sbuf_tensor("arena", [128, nbytes], mybir.dt.uint8)
        self.nbytes = nbytes
        self.off = 0
        self.peak = 0

    def take(self, shape, dtype, name=""):
        esz = 4 if dtype == F32 else 2
        free = int(np.prod(shape[1:]))
        nb = free * esz
        off = (self.off + 31) // 32 * 32
        assert off + nb <= self.nbytes, \
            f"SBUF arena overflow at {name}: need {off + nb}, have {self.nbytes}"
        self.off = off + nb
        self.peak = max(self.peak, self.off)
        v = self.t.ap()[:, off:off + nb].bitcast(dtype)
        if len(shape) > 2:
            dims = " ".join(f"d{i}" for i in range(1, len(shape)))
            v = v.rearrange(f"p ({dims}) -> p {dims}",
                            **{f"d{i}": int(shape[i]) for i in range(1, len(shape))})
        if shape[0] != 128:
            v = v[0:shape[0]]
        return v

    def mark(self):
        return self.off

    def reset(self, mark):
        self.off = mark



def _evens_odds_perm():
    perm = np.zeros(D, dtype=np.int64)
    for h in range(H):
        perm[h * 64:h * 64 + 32] = h * 64 + np.arange(0, 64, 2)
        perm[h * 64 + 32:h * 64 + 64] = h * 64 + np.arange(1, 64, 2)
    return perm


def host_prep(inputs):
    g = {}
    T = lambda a: np.ascontiguousarray(np.asarray(a, dtype=np.float32).T)
    C = lambda a: np.ascontiguousarray(np.asarray(a, dtype=np.float32))
    g["wq"] = T(inputs["L0_attn_Wq"])
    g["wk"] = T(inputs["L0_attn_Wk"])
    g["wv"] = T(inputs["L0_attn_Wv"])
    g["w1"] = C(inputs["L0_ffn_W1"])
    g["w2"] = C(inputs["L0_ffn_W2"])
    g["b1c"] = np.ascontiguousarray(
        np.asarray(inputs["L0_ffn_b1"], np.float32).reshape(NI, 128).T)
    g["b2c"] = np.ascontiguousarray(
        np.asarray(inputs["L0_ffn_b2"], np.float32).reshape(NCH, 128).T)

    wqkv = T(inputs["L1_attn_w_qkv"])
    perm = _evens_odds_perm()
    g["wq1"] = np.ascontiguousarray(wqkv[:, 0:D][:, perm])
    g["wk1"] = np.ascontiguousarray(wqkv[:, D:2 * D][:, perm])
    g["wv1"] = np.ascontiguousarray(wqkv[:, 2 * D:3 * D])
    g["wout1"] = T(inputs["L1_attn_w_out"])
    g["boutc"] = np.ascontiguousarray(
        np.asarray(inputs["L1_attn_b_out"], np.float32).reshape(NCH, 128).T)
    g["wfc"] = T(inputs["L1_ffn_w_fc"])
    g["wproj"] = T(inputs["L1_ffn_w_proj"])
    g["uT"] = T(inputs["L1_ffn_u"])
    g["vmat"] = C(inputs["L1_ffn_v"])
    g["fbT"] = T(inputs["L1_ffn_fast_b"])
    g["gatec"] = np.full((16, 1), float(np.asarray(inputs["L1_ffn_gate"])),
                         dtype=np.float32)

    inv_freq = 1.0 / 10000.0 ** (np.arange(0, HD, 2, dtype=np.float32) / HD)
    t = np.arange(S, dtype=np.float32)
    fr = np.outer(t, inv_freq)
    cos = np.cos(fr).astype(np.float32).T
    sin = np.sin(fr).astype(np.float32).T
    g["t1"] = np.ascontiguousarray(np.vstack([cos, cos, cos, cos]))
    g["t2"] = np.ascontiguousarray(np.vstack([-sin, sin, -sin, sin]))
    sw = np.zeros((128, 128), dtype=np.float32)
    for blk in range(4):
        b = blk * 32
        dst = (b + 32) % 64 + (b // 64) * 64
        for i in range(32):
            sw[dst + i, b + i] = 1.0
    g["pswap"] = sw
    return g



class Ctx:
    pass


def setup_common(P, A, es):
    C = Ctx()
    nc = P.nc
    C.psum = es.enter_context(nc.psum_tensor("psum", [128, 4096], F32))
    C.bank = [C.psum[:, 512 * i:512 * (i + 1)] for i in range(8)]
    C.bank_free = [(None, 0)] * 8
    C.dw = P.sem("dw", es)
    C.dx = P.sem("dx", es)
    C.pe = P.sem("pe", es)
    C.ve = P.sem("ve", es)
    C.se = P.sem("se", es)
    C.ge = P.sem("ge", es)
    C.all_sems = [C.dw, C.dx, C.pe, C.ve, C.se, C.ge]
    C.ones_bf_col = A.take([128, 1], BF16, "ones_col")
    C.ones_f_row = A.take([1, 128], F32, "ones_f")
    C.ones_bf_row = A.take([1, 128], BF16, "ones_bfr")
    C.half_f_row = A.take([1, 128], F32, "half_f")
    C.nhalf_f_row = A.take([1, 128], F32, "nhalf_f")
    C.eps_t = A.take([1, 1], F32, "eps")
    C.ones2_bf = A.take([65, 128], BF16, "ones2")
    P.op("gpsimd", "memset", C.ones_bf_col, 1.0)
    P.op("gpsimd", "memset", C.ones_f_row, 1.0)
    P.op("gpsimd", "memset", C.ones_bf_row, 1.0)
    P.op("gpsimd", "memset", C.half_f_row, 0.5)
    P.op("gpsimd", "memset", C.eps_t, 1e-5)
    P.op("gpsimd", "memset", C.ones2_bf, 1.0)
    P.op("gpsimd", "memset", C.nhalf_f_row, -0.5, inc=C.ge)
    return C


def use_bank(P, C, i):
    s, n = C.bank_free[i]
    if s is not None:
        P.wait("tensor", s, n)


def release_bank(C, i, sem):
    C.bank_free[i] = (sem, sem.n)


def emit_ln(P, A, C, x_sb, xn_out):
    m0 = A.mark()
    xb = A.take([128, NCH, S], BF16, "ln_xb")
    sums = A.take([1, 2, S], F32, "ln_sums")
    mneg = A.take([1, S], F32, "ln_mneg")
    rstd = A.take([1, S], F32, "ln_rstd")
    mneg_b = A.take([128, S], F32, "ln_mnegb")
    rstd_b = A.take([128, S], F32, "ln_rstdb")
    scrf = A.take([128, S], F32, "ln_scr")
    bs, bb = 7, 6

    for c in range(NCH):
        P.op("vector", "tensor_copy", xb[:, c, :], x_sb[:, c, :], inc=C.ve)
    for tb in range(S // 512):
        use_bank(P, C, bs)
        P.wait("tensor", C.ve)
        for c in range(NCH):
            P.mm(C.bank[bs][0:1, :], C.ones_bf_col, xb[:, c, bass.ts(tb, 512)],
                 start=(c == 0), stop=(c == NCH - 1),
                 inc=C.pe if c == NCH - 1 else None)
        P.wait("vector", C.pe)
        P.op("vector", "tensor_copy", sums[0:1, 0, bass.ts(tb, 512)],
             C.bank[bs][0:1, :], inc=C.ve)
        release_bank(C, bs, C.ve)
    for c in range(NCH):
        P.wait("scalar", C.pe)
        P.op("scalar", "activation", xb[:, c, :], xb[:, c, :], AF.Square,
             inc=C.se)
    for tb in range(S // 512):
        use_bank(P, C, bs)
        P.wait("tensor", C.se)
        for c in range(NCH):
            P.mm(C.bank[bs][0:1, :], C.ones_bf_col, xb[:, c, bass.ts(tb, 512)],
                 start=(c == 0), stop=(c == NCH - 1),
                 inc=C.pe if c == NCH - 1 else None)
        P.wait("vector", C.pe)
        P.op("vector", "tensor_copy", sums[0:1, 1, bass.ts(tb, 512)],
             C.bank[bs][0:1, :], inc=C.ve)
        release_bank(C, bs, C.ve)
    P.op("vector", "tensor_scalar", mneg, sums[0:1, 0, :], -1.0 / D, None, ALU.mult)
    P.op("vector", "tensor_scalar", rstd, sums[0:1, 0, :], 1.0 / D, None, ALU.mult)
    P.op("vector", "tensor_mul", rstd, rstd, rstd)
    P.op("vector", "scalar_tensor_tensor", rstd, sums[0:1, 1, :], 1.0 / D, rstd,
         ALU.mult, ALU.subtract, inc=C.ve)
    P.wait("scalar", C.ve)
    P.op("scalar", "activation", rstd, rstd, AF.Sqrt, bias=C.eps_t, inc=C.se)
    P.wait("vector", C.se)
    P.op("vector", "reciprocal", rstd, rstd, inc=C.ve)
    for tb in range(S // 512):
        use_bank(P, C, bb)
        P.wait("tensor", C.ve)
        P.mm(C.bank[bb][:, :], C.ones_f_row, mneg[0:1, bass.ts(tb, 512)], inc=C.pe)
        P.wait("vector", C.pe)
        P.op("vector", "tensor_copy", mneg_b[:, bass.ts(tb, 512)], C.bank[bb][:, :],
             inc=C.ve)
        P.wait("tensor", C.ve)
        P.mm(C.bank[bb][:, :], C.ones_f_row, rstd[0:1, bass.ts(tb, 512)], inc=C.pe)
        P.wait("vector", C.pe)
        P.op("vector", "tensor_copy", rstd_b[:, bass.ts(tb, 512)], C.bank[bb][:, :],
             inc=C.ve)
        release_bank(C, bb, C.ve)
    for c in range(NCH):
        P.op("vector", "tensor_add", scrf, x_sb[:, c, :], mneg_b)
        P.op("vector", "tensor_mul", xn_out[:, c, :], scrf, rstd_b, inc=C.ve)
    A.reset(m0)


def emit_ffn(P, A, C, xn, w1_sb, w2_sb, g_hbm, layer, b1c=None, b2c=None):
    m0 = A.mark()
    rr = A.take([128, NI, TQB], BF16, "rr")
    gtile = A.take([128, NCH, TQB], BF16, "gtile")
    scrf = A.take([128, TQB], F32, "ffn_scr")
    f1b = [0, 1]
    f2b = [2, 3, 4, 5]
    gtile_free = C.dx.n
    for tb in range(NFB):
        for c2 in range(NI):
            bk = f1b[c2 % 2]
            use_bank(P, C, bk)
            for ci in range(NCH):
                P.mm(C.bank[bk][:, 0:TQB], w1_sb[:, ci, bass.ts(c2, 128)],
                     xn[:, ci, bass.ts(tb, TQB)],
                     start=(ci == 0), stop=(ci == NCH - 1),
                     inc=C.pe if ci == NCH - 1 else None)
            if layer == 0:
                P.wait("vector", C.pe)
                P.op("vector", "tensor_scalar", rr[:, c2, :], C.bank[bk][:, 0:TQB],
                     b1c[:, c2:c2 + 1], 0.0, ALU.add, ALU.max, inc=C.ve)
                release_bank(C, bk, C.ve)
            else:
                P.wait("vector", C.pe)
                P.op("vector", "tensor_scalar", scrf, C.bank[bk][:, 0:TQB],
                     0.0, None, ALU.max, inc=C.ve)
                release_bank(C, bk, C.ve)
                P.wait("scalar", C.ve)
                P.op("scalar", "activation", rr[:, c2, :], scrf, AF.Square,
                     inc=C.se)
        for oc in range(NCH):
            bk = f2b[oc % 4]
            use_bank(P, C, bk)
            if layer == 1:
                P.wait("tensor", C.se)
            else:
                P.wait("tensor", C.ve)
            for c2 in range(NI):
                P.mm(C.bank[bk][:, 0:TQB], w2_sb[:, c2, bass.ts(oc, 128)],
                     rr[:, c2, :],
                     start=(c2 == 0), stop=(c2 == NI - 1),
                     inc=C.pe if c2 == NI - 1 else None)
            P.wait("scalar", C.pe)
            P.wait("scalar", C.dx, gtile_free)
            if layer == 0:
                P.op("scalar", "activation", gtile[:, oc, :], C.bank[bk][:, 0:TQB],
                     AF.Silu, bias=b2c[:, oc:oc + 1], inc=C.se)
            else:
                P.op("scalar", "activation", gtile[:, oc, :], C.bank[bk][:, 0:TQB],
                     AF.Silu, inc=C.se)
            release_bank(C, bk, C.se)
        P.wait("sync", C.se)
        P.dma("sync", g_hbm[:, :, bass.ts(tb, TQB)].rearrange("c p t -> p c t"),
              gtile, C.dx)
        gtile_free = C.dx.n
    A.reset(m0)


def emit_qk_proj_l0(P, A, C, xn, w_sb, out_bf):
    bks = [0, 1]
    free_i = 0
    for oc in range(NCH):
        for tb in range(S // 512):
            bk = bks[free_i % 2]
            free_i += 1
            use_bank(P, C, bk)
            for ci in range(NCH):
                P.mm(C.bank[bk][:, :], w_sb[:, ci, bass.ts(oc, 128)],
                     xn[:, ci, bass.ts(tb, 512)],
                     start=(ci == 0), stop=(ci == NCH - 1),
                     inc=C.pe if ci == NCH - 1 else None)
            P.wait("vector", C.pe)
            P.op("vector", "tensor_copy", out_bf[:, oc, bass.ts(tb, 512)],
                 C.bank[bk][:, :], inc=C.ve)
            release_bank(C, bk, C.ve)


def emit_qk_proj_rope(P, A, C, xn, w_sb, out_bf, t1, t2, pswap_bf):
    m0 = A.mark()
    qpre = A.take([128, 512], BF16, "rope_qpre")
    s1 = A.take([128, 512], F32, "rope_s1")
    s2 = A.take([128, 512], F32, "rope_s2")
    pi = 0
    for oc in range(NCH):
        for tb in range(S // 512):
            bk = [0, 1][pi % 2]
            bk2 = [2, 3][pi % 2]
            pi += 1
            use_bank(P, C, bk)
            for ci in range(NCH):
                P.mm(C.bank[bk][:, :], w_sb[:, ci, bass.ts(oc, 128)],
                     xn[:, ci, bass.ts(tb, 512)],
                     start=(ci == 0), stop=(ci == NCH - 1),
                     inc=C.pe if ci == NCH - 1 else None)
            P.wait("vector", C.pe)
            P.op("vector", "tensor_copy", qpre, C.bank[bk][:, :], inc=C.ve)
            release_bank(C, bk, C.ve)
            use_bank(P, C, bk2)
            P.wait("tensor", C.ve)
            P.mm(C.bank[bk2][:, :], pswap_bf, qpre, inc=C.pe)
            P.op("vector", "tensor_mul", s1, qpre, t1[:, bass.ts(tb, 512)])
            P.wait("vector", C.pe)
            P.op("vector", "tensor_mul", s2, C.bank[bk2][:, :],
                 t2[:, bass.ts(tb, 512)], inc=C.ve)
            release_bank(C, bk2, C.ve)
            P.op("vector", "tensor_add", out_bf[:, oc, bass.ts(tb, 512)], s1, s2,
                 inc=C.ve)
    A.reset(m0)


def emit_v_proj(P, A, C, xn, wv_sb, v_aug, col_off=0):
    pi = 0
    for tk in range(NTK):
        for ob in range(2):
            bk = [0, 1][pi % 2]
            pi += 1
            use_bank(P, C, bk)
            for ci in range(NCH):
                P.mm(C.bank[bk][:, :], xn[:, ci, bass.ts(tk, 128)],
                     wv_sb[:, ci, col_off + ob * 512:col_off + (ob + 1) * 512],
                     start=(ci == 0), stop=(ci == NCH - 1),
                     inc=C.pe if ci == NCH - 1 else None)
            P.wait("vector", C.pe)
            P.op("vector", "tensor_copy",
                 v_aug[:, tk, ob * 8:(ob + 1) * 8, 0:64],
                 C.bank[bk][:, :].rearrange("p (h d) -> p h d", h=8), inc=C.ve)
            release_bank(C, bk, C.ve)


def emit_mshift(P, A, C, q_bf, hp, mshift, kmaxs, qsq_scr, msrow, inv_scale):
    P.wait("scalar", C.pe)
    P.op("scalar", "activation", qsq_scr, q_bf[:, hp, :], AF.Square, inc=C.se)
    for side, base in ((0, 0), (1, 64)):
        h = 2 * hp + side
        for tb in range(4):
            use_bank(P, C, tb)
        P.wait("tensor", C.se)
        for tb in range(4):
            P.mm(C.psum[0:1, bass.ts(tb, 512)], C.ones_bf_col[base:base + 64, :],
                 qsq_scr[base:base + 64, bass.ts(tb, 512)],
                 inc=C.pe if tb == 3 else None)
        P.wait("vector", C.pe)
        P.op("vector", "tensor_copy", msrow, C.psum[0:1, 0:2048], inc=C.ve)
        for tb in range(4):
            release_bank(C, tb, C.ve)
        P.wait("scalar", C.ve)
        P.op("scalar", "activation", msrow, msrow, AF.Ln)
        P.op("scalar", "activation", msrow, msrow, AF.Exp, scale=0.5, inc=C.se)
        P.wait("vector", C.se)
        P.op("vector", "tensor_scalar", msrow, msrow,
             kmaxs[0:1, h:h + 1], -64.0, ALU.mult, ALU.add)
        P.op("vector", "tensor_scalar", msrow, msrow, 0.0, -float(inv_scale),
             ALU.max, ALU.mult)
        P.op("vector", "tensor_copy", mshift[base:base + 1, :], msrow, inc=C.ve)


def emit_kmaxs(P, A, C, k_bf, kmaxs, scr, row, inv_scale):
    m0 = A.mark()
    one = A.take([1, 1], F32, "km_one")
    for hp in range(NCH):
        P.wait("scalar", C.pe)
        P.wait("scalar", C.ve)
        P.op("scalar", "activation", scr, k_bf[:, hp, :], AF.Square, inc=C.se)
        for side, base in ((0, 0), (1, 64)):
            h = 2 * hp + side
            bq = 6 + side
            use_bank(P, C, bq)
            P.wait("tensor", C.se)
            for tb in range(S // 512):
                P.mm(C.bank[bq][0:1, :], C.ones_bf_col[base:base + 64, :],
                     scr[base:base + 64, bass.ts(tb, 512)], inc=C.pe)
                P.wait("vector", C.pe)
                P.op("vector", "tensor_copy", row[0:1, bass.ts(tb, 512)],
                     C.bank[bq][0:1, :], inc=C.ve)
                P.wait("tensor", C.ve)
            release_bank(C, bq, C.ve)
            P.op("vector", "reduce_max", one, row, mybir.AxisListType.X, inc=C.ve)
            P.wait("scalar", C.ve)
            P.op("scalar", "activation", one, one, AF.Ln)
            P.op("scalar", "activation", one, one, AF.Exp, scale=0.5, inc=C.se)
            P.wait("vector", C.se)
            P.op("vector", "tensor_scalar", kmaxs[0:1, h:h + 1], one,
                 1.0 / float(inv_scale), None, ALU.mult, inc=C.ve)
    A.reset(m0)


def emit_attention(P, A, C, q_bf, k_bf, v_aug, scale, consume, shifted=False):
    m0 = A.mark()
    recips = A.take([1, 2, 512], F32, "recips")
    rb = A.take([128, 512], F32, "rb")
    s_blk = A.take([128, 512], F32, "s_blk")
    if shifted:
        inv_scale = 1.0 / scale
        mshift = A.take([65, S], BF16, "mshift")
        kmaxs = A.take([1, H], F32, "kmaxs")
        qsq_scr = A.take([128, S], BF16, "qsq")
        msrow = A.take([1, S], F32, "msrow")
        emit_kmaxs(P, A, C, k_bf, kmaxs, qsq_scr, msrow, inv_scale)
    rtmp = A.take([1, 2, 512], F32, "rtmp")
    ptab = [A.take([128, 1024], BF16, "ptab0"),
            A.take([128, 1024], BF16, "ptab1")]
    pv_pairs = [(4, 5), (6, 7)]

    def emit_scores(hp, tq, tk):
        p = tk % 2
        bA, bB = 2 * p, 2 * p + 1
        use_bank(P, C, bA)
        use_bank(P, C, bB)
        if shifted:
            P.mm(C.bank[bA], C.ones2_bf[0:1, :], mshift[0:1, tq],
                 start=True, stop=False)
            P.mm(C.bank[bB], C.ones2_bf[64:65, :], mshift[64:65, tq],
                 start=True, stop=False)
            P.mm(C.bank[bA], k_bf[0:64, hp, bass.ts(tk, 128)],
                 q_bf[0:64, hp, tq], start=False, stop=True, tp=(0, 0))
            P.mm(C.bank[bB], k_bf[64:128, hp, bass.ts(tk, 128)],
                 q_bf[64:128, hp, tq], start=False, stop=True, tp=(64, 0),
                 inc=C.pe)
        else:
            P.mm(C.bank[bA], k_bf[0:64, hp, bass.ts(tk, 128)],
                 q_bf[0:64, hp, tq], start=True, stop=True, tp=(0, 0))
            P.mm(C.bank[bB], k_bf[64:128, hp, bass.ts(tk, 128)],
                 q_bf[64:128, hp, tq], start=True, stop=True, tp=(64, 0),
                 inc=C.pe)

    def emit_exp(tk):
        p = tk % 2
        P.wait("scalar", C.pe)
        P.op("scalar", "activation", ptab[p],
             C.psum[:, 1024 * p:1024 * (p + 1)], AF.Exp, scale=scale,
             inc=C.se)
        release_bank(C, 2 * p, C.se)
        release_bank(C, 2 * p + 1, C.se)
        return C.se.n

    def emit_pv(hp, tk, se_need, bOA, bOB):
        p = tk % 2
        P.wait("tensor", C.se, se_need)
        P.mm(C.bank[bOA][0:65, :], v_aug[:, tk, 2 * hp, :], ptab[p][:, 0:512],
             start=(tk == 0), stop=(tk == NTK - 1))
        P.mm(C.bank[bOB][0:65, :], v_aug[:, tk, 2 * hp + 1, :],
             ptab[p][:, 512:1024],
             start=(tk == 0), stop=(tk == NTK - 1), inc=C.pe)

    pending = [None]

    def norm_part2(hp, tqb, bOA, bOB):
        use_bank(P, C, bOA)
        use_bank(P, C, bOB)
        P.wait("tensor", C.ve)
        P.mm(C.bank[bOA][0:64, :], C.ones_f_row[0:1, 0:64], recips[0:1, 0, :])
        P.mm(C.bank[bOB][0:64, :], C.ones_f_row[0:1, 0:64], recips[0:1, 1, :],
             inc=C.pe)
        P.wait("vector", C.pe)
        P.op("vector", "tensor_copy", rb[0:64, :], C.bank[bOA][0:64, :])
        P.op("vector", "tensor_copy", rb[64:128, :], C.bank[bOB][0:64, :],
             inc=C.ve)
        release_bank(C, bOA, C.ve)
        release_bank(C, bOB, C.ve)
        P.op("vector", "tensor_mul", s_blk[0:64, :], s_blk[0:64, :],
             rb[0:64, :])
        P.op("vector", "tensor_mul", s_blk[64:128, :], s_blk[64:128, :],
             rb[64:128, :], inc=C.ve)
        consume(hp, tqb, s_blk)

    for hp in range(NCH):
        if shifted:
            emit_mshift(P, A, C, q_bf, hp, mshift, kmaxs, qsq_scr, msrow,
                        inv_scale)
            P.wait("tensor", C.ve)
        for tqb in range(S // 512):
            tq = bass.ts(tqb, 512)
            it = hp * 4 + tqb
            bOA, bOB = pv_pairs[it % 2]
            se_at = {}
            for tk in range(NTK):
                emit_scores(hp, tq, tk)
                if tk == 0:
                    use_bank(P, C, bOA)
                    use_bank(P, C, bOB)
                se_at[tk] = emit_exp(tk)
                if tk > 0:
                    emit_pv(hp, tk - 1, se_at[tk - 1], bOA, bOB)
                if tk == 10 and pending[0] is not None:
                    norm_part2(*pending[0])
                    pending[0] = None
            emit_pv(hp, NTK - 1, se_at[NTK - 1], bOA, bOB)
            P.wait("vector", C.pe)
            P.op("vector", "tensor_scalar", rtmp[0:1, 0, :],
                 C.bank[bOA][64:65, :], 1e-36, None, ALU.add)
            P.op("vector", "tensor_scalar", rtmp[0:1, 1, :],
                 C.bank[bOB][64:65, :], 1e-36, None, ALU.add)
            P.op("vector", "tensor_copy", s_blk[0:64, :], C.bank[bOA][0:64, :])
            P.op("vector", "tensor_copy", s_blk[64:128, :],
                 C.bank[bOB][0:64, :], inc=C.ve)
            release_bank(C, bOA, C.ve)
            release_bank(C, bOB, C.ve)
            P.op("vector", "reciprocal", recips[0:1, 0, :], rtmp[0:1, 0, :])
            P.op("vector", "reciprocal", recips[0:1, 1, :], rtmp[0:1, 1, :],
                 inc=C.ve)
            pending[0] = (hp, tqb, bOA, bOB)
    norm_part2(*pending[0])
    pending[0] = None
    A.reset(m0)


def emit_ternarize(P, A, C, w_hbm, n_in, n_out, weff_sb, tag):
    m0 = A.mark()
    nic = n_in // 128
    nob = n_out // 512
    chunk_f = [A.take([128, n_out], F32, f"tn_cf0_{tag}"),
               A.take([128, n_out], F32, f"tn_cf1_{tag}")]
    chunk_abs = [A.take([128, n_out], BF16, f"tn_ca0_{tag}"),
                 A.take([128, n_out], BF16, f"tn_ca1_{tag}")]
    srow = A.take([1, n_out], F32, f"tn_sr_{tag}")
    bh = A.take([128, n_out], F32, f"tn_bh_{tag}")
    bhn = A.take([128, n_out], F32, f"tn_bhn_{tag}")
    sbc = A.take([128, n_out], BF16, f"tn_sbc_{tag}")
    mneg = chunk_abs[0]
    w_r = w_hbm.rearrange("(c p) o -> p c o", p=128)
    cf_free = [C.se.n, C.se.n]
    ca_free = [C.pe.n, C.pe.n]
    for c in range(nic):
        p = c % 2
        P.wait("sync", C.se, cf_free[p])
        P.dma("sync", chunk_f[p], w_r[:, c, :], C.dx)
        P.wait("scalar", C.dx)
        P.wait("scalar", C.pe, ca_free[p])
        P.op("scalar", "activation", chunk_abs[p], chunk_f[p], AF.Abs, inc=C.se)
        cf_free[p] = C.se.n
        P.wait("tensor", C.se)
        for ob in range(nob):
            if c == 0:
                use_bank(P, C, ob)
            P.mm(C.bank[ob][0:1, :], C.ones_bf_col,
                 chunk_abs[p][:, bass.ts(ob, 512)],
                 start=(c == 0), stop=(c == nic - 1),
                 inc=C.pe if ob == nob - 1 else None)
        ca_free[p] = C.pe.n
    P.wait("vector", C.pe)
    for ob in range(nob):
        P.op("vector", "tensor_scalar", srow[0:1, bass.ts(ob, 512)],
             C.bank[ob][0:1, :], 1.0 / n_in, 1e-5, ALU.mult, ALU.max,
             inc=C.ve if ob == nob - 1 else None)
    for ob in range(nob):
        release_bank(C, ob, C.ve)
    for ob in range(nob):
        for row_const, dst in ((C.half_f_row, bh), (C.nhalf_f_row, bhn),
                               (C.ones_f_row, None)):
            bk = ob % 2
            use_bank(P, C, bk)
            P.wait("tensor", C.ve)
            P.mm(C.bank[bk], row_const, srow[0:1, bass.ts(ob, 512)], inc=C.pe)
            P.wait("vector", C.pe)
            out_ap = (sbc if dst is None else dst)[:, bass.ts(ob, 512)]
            P.op("vector", "tensor_copy", out_ap, C.bank[bk], inc=C.ve)
            release_bank(C, bk, C.ve)
    cf2_free = [(C.ve.n, C.ge.n), (C.ve.n, C.ge.n)]
    P.wait("gpsimd", C.ve)
    for c in range(nic):
        p = c % 2
        ve_n, ge_n = cf2_free[p]
        P.wait("sync", C.ve, ve_n)
        P.wait("sync", C.ge, ge_n)
        P.dma("sync", chunk_f[p], w_r[:, c, :], C.dx)
        wch = weff_sb[:, c, :]
        P.wait("vector", C.dx)
        P.op("vector", "tensor_tensor", wch, chunk_f[p], bh, ALU.is_ge)
        P.op("vector", "tensor_tensor", mneg, chunk_f[p], bhn, ALU.is_le)
        P.op("vector", "tensor_sub", wch, wch, mneg)
        P.op("vector", "tensor_mul", wch, wch, sbc, inc=C.ve)
        cf2_free[p] = (C.ve.n, C.ge.n)
    A.reset(m0)


def emit_lowrank_fold(P, A, C, io, weff_fc):
    m0 = A.mark()
    u_sb = A.take([16, INTER], BF16, "lr_u")
    v_sb = A.take([16, D], BF16, "lr_v")
    fb_sb = A.take([16, RANK], F32, "lr_fb")
    gate_c = A.take([16, 1], F32, "lr_gate")
    gfb = A.take([16, RANK], BF16, "lr_gfb")
    at = A.take([16, D], BF16, "lr_at")
    P.dma("gpsimd", u_sb, io.w["uT"], C.dw)
    P.dma("gpsimd", v_sb, io.w["vmat"], C.dw)
    P.dma("gpsimd", fb_sb, io.w["fbT"], C.dw)
    P.dma("gpsimd", gate_c, io.w["gatec"], C.dw)
    P.wait("vector", C.dw)
    P.op("vector", "tensor_scalar", gfb, fb_sb, gate_c, None, ALU.mult, inc=C.ve)
    use_bank(P, C, 0)
    use_bank(P, C, 1)
    P.wait("tensor", C.ve)
    P.mm(C.bank[0][0:16, :], gfb, v_sb[:, 0:512], start=True, stop=True)
    P.mm(C.bank[1][0:16, :], gfb, v_sb[:, 512:1024], start=True, stop=True,
         inc=C.pe)
    P.wait("vector", C.pe)
    P.op("vector", "tensor_copy", at[:, 0:512], C.bank[0][0:16, :])
    P.op("vector", "tensor_copy", at[:, 512:1024], C.bank[1][0:16, :], inc=C.ve)
    release_bank(C, 0, C.ve)
    release_bank(C, 1, C.ve)
    for ci in range(NCH):
        for ob in range(INTER // 512):
            bk = ob % 2
            use_bank(P, C, bk)
            P.wait("tensor", C.ve)
            P.mm(C.bank[bk][:, :], at[:, bass.ts(ci, 128)],
                 u_sb[:, bass.ts(ob, 512)], inc=C.pe)
            P.wait("vector", C.pe)
            P.op("vector", "tensor_add", weff_fc[:, ci, bass.ts(ob, 512)],
                 weff_fc[:, ci, bass.ts(ob, 512)], C.bank[bk][:, :], inc=C.ve)
            release_bank(C, bk, C.ve)
    A.reset(m0)



def emit_layer0(P, A, C, io, x_src, x_dst):
    m_layer = A.mark()
    xn = A.take([128, NCH, S], BF16, "xn0")
    x_src_r = x_src.rearrange("(c p) t -> p c t", p=128)
    x_dst_r = x_dst.rearrange("(c p) t -> c p t", p=128)
    g_hbm = io.g0

    m0 = A.mark()
    x_sb = A.take([128, NCH, S], F32, "x0")
    P.dma("sync", x_sb, x_src_r, C.dx)
    P.wait("vector", C.dx)
    P.wait("vector", C.ge)
    P.wait("tensor", C.ge)
    emit_ln(P, A, C, x_sb, xn)
    A.reset(m0)
    P.barrier(C.all_sems)

    m1 = A.mark()
    w1_sb = A.take([128, NCH, INTER], BF16, "w1")
    w2_sb = A.take([128, NI, D], BF16, "w2")
    b1c = A.take([128, NI], F32, "b1c")
    b2c = A.take([128, NCH], F32, "b2c")
    P.dma("gpsimd", w1_sb, io.w["w1"].rearrange("(c p) o -> p c o", p=128), C.dw)
    P.dma("gpsimd", w2_sb, io.w["w2"].rearrange("(c p) o -> p c o", p=128), C.dw)
    P.dma("sync", b1c, io.w["b1c"], C.dx)
    P.dma("sync", b2c, io.w["b2c"], C.dx)
    P.wait("tensor", C.dw)
    P.wait("vector", C.dx)
    P.wait("scalar", C.dx)
    emit_ffn(P, A, C, xn, w1_sb, w2_sb, g_hbm, 0, b1c, b2c)
    A.reset(m1)
    P.barrier(C.all_sems)

    q_bf = A.take([128, NCH, S], BF16, "q0")
    k_bf = A.take([128, NCH, S], BF16, "k0")
    v_aug = A.take([128, NTK, H, 65], BF16, "v0")
    m2 = A.mark()
    wslot = [A.take([128, NCH, D], BF16, "ws0"),
             A.take([128, NCH, D], BF16, "ws1")]
    P.op("gpsimd", "memset", v_aug[:, :, :, 64:65], 1.0, inc=C.ge)
    P.dma("gpsimd", wslot[0], io.w["wq"].rearrange("(c p) o -> p c o", p=128), C.dw)
    P.wait("tensor", C.dw)
    emit_qk_proj_l0(P, A, C, xn, wslot[0], q_bf)
    P.dma("gpsimd", wslot[1], io.w["wk"].rearrange("(c p) o -> p c o", p=128), C.dw)
    P.wait("tensor", C.dw)
    emit_qk_proj_l0(P, A, C, xn, wslot[1], k_bf)
    P.wait("gpsimd", C.pe)
    P.dma("gpsimd", wslot[0], io.w["wv"].rearrange("(c p) o -> p c o", p=128), C.dw)
    P.wait("tensor", C.dw)
    P.wait("tensor", C.ge)
    emit_v_proj(P, A, C, xn, wslot[0], v_aug)
    A.reset(m2)
    P.barrier(C.all_sems)
    if io.dbg:
        P.dma("sync", io.d_xn, xn, C.dx)
        P.dma("sync", io.d_q, q_bf, C.dx)
        P.dma("sync", io.d_k, k_bf, C.dx)
        P.dma("sync", io.d_v, v_aug, C.dx)

    xst = A.take([128, 2, 512], F32, "xstage")
    gst = A.take([128, 2, 512], BF16, "gstage")
    stage_free = [(None, 0), (None, 0)]
    g_hbm_r = g_hbm
    x_src_cr = x_src.rearrange("(c p) t -> c p t", p=128)

    def consume0(hp, tqb, s_blk):
        sl = (hp * 4 + tqb) % 2
        if io.dbg:
            P.wait("sync", C.ve)
            P.dma("sync", io.d_s[hp, :, bass.ts(tqb, 512)], s_blk, C.dx)
            P.wait("vector", C.dx)
        P.wait("vector", C.dx, stage_issue[(hp, tqb)])
        P.op("vector", "tensor_mul", s_blk, s_blk, gst[:, sl, :])
        P.op("vector", "tensor_add", xst[:, sl, :], xst[:, sl, :], s_blk,
             inc=C.ve)
        P.wait("sync", C.ve)
        P.dma("sync", x_dst_r[hp, :, bass.ts(tqb, 512)], xst[:, sl, :], C.dx)
        stage_free[sl] = (C.dx, C.dx.n)

    stage_issue = {}

    def consume_wrap(hp, tqb, s_blk):
        consume0(hp, tqb, s_blk)
        nxt = hp * 4 + tqb + 1
        if nxt < 32:
            issue_stage(nxt // 4, nxt % 4)

    def issue_stage(hp, tqb):
        sl = (hp * 4 + tqb) % 2
        s, n = stage_free[sl]
        if s is not None:
            P.wait("sync", s, n)
        P.dma("sync", xst[:, sl, :], x_src_cr[hp, :, bass.ts(tqb, 512)], C.dx)
        P.dma("sync", gst[:, sl, :], g_hbm_r[hp, :, bass.ts(tqb, 512)], C.dx)
        stage_issue[(hp, tqb)] = C.dx.n

    issue_stage(0, 0)
    issue_stage(0, 1)
    emit_attention(P, A, C, q_bf, k_bf, v_aug, 1.0 / 32.0, consume_wrap)
    A.reset(m_layer)
    P.barrier(C.all_sems)


def emit_layer1(P, A, C, io, x_src, x_dst):
    m_layer = A.mark()
    xn = A.take([128, NCH, S], BF16, "xn1")
    x_src_r = x_src.rearrange("(c p) t -> p c t", p=128)
    x_src_cr = x_src.rearrange("(c p) t -> c p t", p=128)
    x_dst_r = x_dst.rearrange("(c p) t -> c p t", p=128)
    g_hbm = io.g1

    m0 = A.mark()
    x_sb = A.take([128, NCH, S], F32, "x1")
    P.dma("sync", x_sb, x_src_r, C.dx)
    P.wait("vector", C.dx)
    emit_ln(P, A, C, x_sb, xn)
    A.reset(m0)
    P.barrier(C.all_sems)

    weff_fc = A.take([128, NCH, INTER], BF16, "weff_fc")
    emit_ternarize(P, A, C, io.w["wfc"], D, INTER, weff_fc, "fc")
    P.barrier(C.all_sems)
    emit_lowrank_fold(P, A, C, io, weff_fc)
    P.barrier(C.all_sems)
    weff_pj = A.take([128, NI, D], BF16, "weff_pj")
    emit_ternarize(P, A, C, io.w["wproj"], INTER, D, weff_pj, "pj")
    P.barrier(C.all_sems)
    emit_ffn(P, A, C, xn, weff_fc, weff_pj, g_hbm, 1)
    A.reset(m_layer)
    xn = A.take([128, NCH, S], BF16, "xn1_alias")
    P.barrier(C.all_sems)

    t1 = A.take([128, S], BF16, "t1")
    t2 = A.take([128, S], BF16, "t2")
    pswap = A.take([128, 128], BF16, "pswap")
    attnout = A.take([128, NCH, S], BF16, "attnout")
    q_bf = A.take([128, NCH, S], BF16, "q1")
    k_bf = A.take([128, NCH, S], BF16, "k1")
    v_aug = A.take([128, NTK, H, 65], BF16, "v1")
    m2 = A.mark()
    wslot = A.take([128, NCH, D], BF16, "w1s0")
    P.dma("gpsimd", t1, io.w["t1"], C.dw)
    P.dma("gpsimd", t2, io.w["t2"], C.dw)
    P.dma("gpsimd", pswap, io.w["pswap"], C.dw)
    P.op("gpsimd", "memset", v_aug[:, :, :, 64:65], 1.0, inc=C.ge)
    P.dma("gpsimd", wslot, io.w["wq1"].rearrange("(c p) o -> p c o", p=128),
          C.dw)
    P.wait("tensor", C.dw)
    P.wait("vector", C.dw)
    emit_qk_proj_rope(P, A, C, xn, wslot, q_bf, t1, t2, pswap)
    P.wait("gpsimd", C.pe)
    P.dma("gpsimd", wslot, io.w["wk1"].rearrange("(c p) o -> p c o", p=128),
          C.dw)
    P.wait("tensor", C.dw)
    emit_qk_proj_rope(P, A, C, xn, wslot, k_bf, t1, t2, pswap)
    P.wait("gpsimd", C.pe)
    P.dma("gpsimd", wslot, io.w["wv1"].rearrange("(c p) o -> p c o", p=128),
          C.dw)
    P.wait("tensor", C.dw)
    P.wait("tensor", C.ge)
    emit_v_proj(P, A, C, xn, wslot, v_aug)
    A.reset(m2)
    P.barrier(C.all_sems)

    def consume1(hp, tqb, s_blk):
        P.op("vector", "tensor_copy", attnout[:, hp, bass.ts(tqb, 512)], s_blk,
             inc=C.ve)

    emit_attention(P, A, C, q_bf, k_bf, v_aug, 1.0 / 8.0, consume1, shifted=True)
    P.barrier(C.all_sems)

    m3 = A.mark()
    wout = A.take([128, NCH, D], BF16, "wout")
    boutc = A.take([128, NCH], F32, "boutc")
    xst = A.take([128, 2, 512], F32, "xst1")
    gst = A.take([128, 2, 512], BF16, "gst1")
    sblk = A.take([128, 512], F32, "sblk1")
    P.dma("gpsimd", wout, io.w["wout1"].rearrange("(c p) o -> p c o", p=128), C.dw)
    P.dma("sync", boutc, io.w["boutc"], C.dx)
    stage_free = [(None, 0), (None, 0)]
    stage_issue = {}

    def issue_stage(i):
        oc, tb = i // 4, i % 4
        sl = i % 2
        s, n = stage_free[sl]
        if s is not None:
            P.wait("sync", s, n)
        P.dma("sync", xst[:, sl, :], x_src_cr[oc, :, bass.ts(tb, 512)], C.dx)
        P.dma("sync", gst[:, sl, :], g_hbm[oc, :, bass.ts(tb, 512)], C.dx)
        stage_issue[i] = C.dx.n

    issue_stage(0)
    issue_stage(1)
    P.wait("tensor", C.dw)
    pi = 0
    for oc in range(NCH):
        for tb in range(S // 512):
            i = oc * 4 + tb
            sl = i % 2
            bk = [0, 1][pi % 2]
            pi += 1
            use_bank(P, C, bk)
            for ci in range(NCH):
                P.mm(C.bank[bk][:, :], wout[:, ci, bass.ts(oc, 128)],
                     attnout[:, ci, bass.ts(tb, 512)],
                     start=(ci == 0), stop=(ci == NCH - 1),
                     inc=C.pe if ci == NCH - 1 else None)
            P.wait("vector", C.pe)
            P.wait("vector", C.dx, stage_issue[i])
            P.op("vector", "tensor_scalar", sblk, C.bank[bk][:, :],
                 boutc[:, oc:oc + 1], None, ALU.add, inc=C.ve)
            release_bank(C, bk, C.ve)
            P.op("vector", "tensor_mul", sblk, sblk, gst[:, sl, :])
            P.op("vector", "tensor_add", xst[:, sl, :], xst[:, sl, :], sblk,
                 inc=C.ve)
            P.wait("sync", C.ve)
            P.dma("sync", x_dst_r[oc, :, bass.ts(tb, 512)], xst[:, sl, :], C.dx)
            stage_free[sl] = (C.dx, C.dx.n)
            if i + 2 < 32:
                issue_stage(i + 2)
    A.reset(m_layer)



W_L0 = ["wq", "wk", "wv", "w1", "w2", "b1c", "b2c"]
W_L1 = ["wq1", "wk1", "wv1", "wout1", "boutc", "wfc", "wproj", "uT", "vmat",
        "fbT", "gatec", "t1", "t2", "pswap"]


def build_program(layers=(0, 1), dbg=False):
    from contextlib import ExitStack
    nc = bass.Bass("TRN2", target_bir_lowering=False)
    io = Ctx()
    io.dbg = dbg
    io.x_in = nc.dram_tensor("x_in", [D, S], F32, kind="ExternalInput").ap()
    io.x_out = nc.dram_tensor("x_out", [D, S], F32, kind="ExternalOutput").ap()
    if layers == (0, 1):
        io.x_mid = nc.dram_tensor("x_mid", [D, S], F32, kind="Internal").ap()
    io.w = {}
    names = (W_L0 if 0 in layers else []) + (W_L1 if 1 in layers else [])
    shapes = dict(wq=(D, D), wk=(D, D), wv=(D, D), w1=(D, INTER), w2=(INTER, D),
                  b1c=(128, NI), b2c=(128, NCH), wq1=(D, D), wk1=(D, D),
                  wv1=(D, D), wout1=(D, D), boutc=(128, NCH), wfc=(D, INTER),
                  wproj=(INTER, D), uT=(RANK, INTER), vmat=(RANK, D),
                  fbT=(RANK, RANK), gatec=(16, 1), t1=(128, S), t2=(128, S),
                  pswap=(128, 128))
    for n in names:
        io.w[n] = nc.dram_tensor(n, list(shapes[n]), F32,
                                 kind="ExternalInput").ap()
    gkind = "ExternalOutput" if dbg else "Internal"
    if 0 in layers:
        io.g0 = nc.dram_tensor("g0", [NCH, 128, S], BF16, kind=gkind).ap()
    if 1 in layers:
        io.g1 = nc.dram_tensor("g1", [NCH, 128, S], BF16, kind=gkind).ap()
    if dbg:
        io.d_xn = nc.dram_tensor("d_xn", [128, NCH, S], BF16,
                                 kind="ExternalOutput").ap()
        io.d_q = nc.dram_tensor("d_q", [128, NCH, S], BF16,
                                kind="ExternalOutput").ap()
        io.d_k = nc.dram_tensor("d_k", [128, NCH, S], BF16,
                                kind="ExternalOutput").ap()
        io.d_v = nc.dram_tensor("d_v", [128, NTK, H, 65], BF16,
                                kind="ExternalOutput").ap()
        io.d_s = nc.dram_tensor("d_s", [NCH, 128, S], F32,
                                kind="ExternalOutput").ap()

    es = ExitStack()
    P = Prog(nc)
    A = Arena(nc)
    with nc.Block() as block:
        C = setup_common(P, A, es)
        if layers == (0, 1):
            emit_layer0(P, A, C, io, io.x_in, io.x_mid)
            emit_layer1(P, A, C, io, io.x_mid, io.x_out)
        elif layers == (0,):
            emit_layer0(P, A, C, io, io.x_in, io.x_out)
        elif layers == (1,):
            emit_layer1(P, A, C, io, io.x_in, io.x_out)
        P.wait("sync", C.dx)
        P.wait("sync", C.dw)
        P.op("sync", "nop")
        P.simulate()
        P.replay(block)
    es.close()
    return nc


def kernel(**inputs):
    g = host_prep(inputs)
    x = np.asarray(inputs["L0_attn_x"], dtype=np.float32)
    nc = build_program((0, 1))
    in_maps = []
    for b in range(8):
        m = dict(g)
        m["x_in"] = np.ascontiguousarray(x[b].T)
        in_maps.append(m)
    res = run_bass_kernel_spmd(nc, in_maps, core_ids=list(range(8)))
    out = np.stack([res.results[b]["x_out"].T for b in range(8)], axis=0)
    return out.astype(np.float32)


# version 14
# speedup vs baseline: 1.2589x; 1.0551x over previous
import numpy as np
import concourse.bass as bass
import concourse.mybir as mybir
from concourse.bass_utils import run_bass_kernel_spmd

F32 = mybir.dt.float32
BF16 = mybir.dt.bfloat16
AF = mybir.ActivationFunctionType
ALU = mybir.AluOpType

S = 2048
D = 1024
NCH = D // 128
H = 16
HD = 64
INTER = 4096
NI = INTER // 128
NTK = S // 128
RANK = 16
TQB = 256
NFB = S // TQB

ENGINES = ("tensor", "vector", "scalar", "gpsimd", "sync")


class Sem:
    __slots__ = ("h", "n", "name")

    def __init__(self, h, name):
        self.h = h
        self.n = 0
        self.name = name


class Prog:
    def __init__(self, nc):
        self.nc = nc
        self.recs = {e: [] for e in ENGINES}
        self._pend = {e: [] for e in ENGINES}
        self.sems = []

    def sem(self, name, es):
        s = Sem(es.enter_context(self.nc.semaphore(name)), name)
        self.sems.append(s)
        return s

    def wait(self, eng, sem, val=None):
        v = sem.n if val is None else val
        if v > 0:
            self._pend[eng].append((sem, v))

    def op(self, eng, meth, *args, inc=None, inc16=None, **kw):
        waits = self._pend[eng]
        self._pend[eng] = []
        incs = []
        if inc is not None:
            for s in (inc if isinstance(inc, (list, tuple)) else [inc]):
                s.n += 1
                incs.append((s, 1))
        if inc16 is not None:
            inc16.n += 16
            incs.append((inc16, 16))
        self.recs[eng].append((waits, meth, args, kw, incs))

    def dma(self, eng, out, in_, sem, **kw):
        self.op(eng, "dma_start", inc16=sem, out=out, in_=in_, **kw)

    def mm(self, out, lhsT, rhs, start=True, stop=True, inc=None, tp=None):
        self.op("tensor", "matmul", out, lhsT, rhs, start=start, stop=stop,
                skip_group_check=True, tile_position=tp, inc=inc)

    def barrier(self, sems):
        for e in ENGINES:
            for s in sems:
                self.wait(e, s)

    def simulate(self):
        pos = {e: 0 for e in ENGINES}
        val = {s.name: 0 for s in self.sems}
        total = sum(len(r) for r in self.recs.values())
        done = 0
        while done < total:
            progressed = False
            for e in ENGINES:
                while pos[e] < len(self.recs[e]):
                    waits, meth, args, kw, incs = self.recs[e][pos[e]]
                    if any(val[s.name] < v for s, v in waits):
                        break
                    for s, k in incs:
                        val[s.name] += k
                    pos[e] += 1
                    done += 1
                    progressed = True
            if not progressed:
                lines = []
                for e in ENGINES:
                    if pos[e] < len(self.recs[e]):
                        waits, meth, _, _, _ = self.recs[e][pos[e]]
                        ws = [(s.name, v, val[s.name]) for s, v in waits
                              if val[s.name] < v]
                        lines.append(f"{e}@{pos[e]}: {meth} blocked on {ws}")
                raise RuntimeError("DEADLOCK:\n" + "\n".join(lines))

    def replay(self, block):
        for e in ENGINES:
            recs = self.recs[e]

            def make(engine_recs):
                def emit(eng):
                    for waits, meth, args, kw, incs in engine_recs:
                        ww = {}
                        for s, v in waits:
                            if s.name in ww:
                                ww[s.name] = (s, max(v, ww[s.name][1]))
                            else:
                                ww[s.name] = (s, v)
                        for s, v in ww.values():
                            eng.wait_ge(s.h, v)
                        inst = getattr(eng, meth)(*args, **kw)
                        for s, k in incs:
                            inst = inst.then_inc(s.h, k)
                return emit
            getattr(block, e)(make(recs))


class Arena:
    def __init__(self, nc, nbytes=212608):
        self.t = nc.alloc_sbuf_tensor("arena", [128, nbytes], mybir.dt.uint8)
        self.nbytes = nbytes
        self.off = 0
        self.peak = 0

    def take(self, shape, dtype, name=""):
        esz = 4 if dtype == F32 else 2
        free = int(np.prod(shape[1:]))
        nb = free * esz
        off = (self.off + 31) // 32 * 32
        assert off + nb <= self.nbytes, \
            f"SBUF arena overflow at {name}: need {off + nb}, have {self.nbytes}"
        self.off = off + nb
        self.peak = max(self.peak, self.off)
        v = self.t.ap()[:, off:off + nb].bitcast(dtype)
        if len(shape) > 2:
            dims = " ".join(f"d{i}" for i in range(1, len(shape)))
            v = v.rearrange(f"p ({dims}) -> p {dims}",
                            **{f"d{i}": int(shape[i]) for i in range(1, len(shape))})
        if shape[0] != 128:
            v = v[0:shape[0]]
        return v

    def mark(self):
        return self.off

    def reset(self, mark):
        self.off = mark



def _evens_odds_perm():
    perm = np.zeros(D, dtype=np.int64)
    for h in range(H):
        perm[h * 64:h * 64 + 32] = h * 64 + np.arange(0, 64, 2)
        perm[h * 64 + 32:h * 64 + 64] = h * 64 + np.arange(1, 64, 2)
    return perm


def host_prep(inputs):
    g = {}
    T = lambda a: np.ascontiguousarray(np.asarray(a, dtype=np.float32).T)
    C = lambda a: np.ascontiguousarray(np.asarray(a, dtype=np.float32))
    g["wq"] = T(inputs["L0_attn_Wq"])
    g["wk"] = T(inputs["L0_attn_Wk"])
    g["wv"] = T(inputs["L0_attn_Wv"])
    g["w1"] = C(inputs["L0_ffn_W1"])
    g["w2"] = C(inputs["L0_ffn_W2"])
    g["b1c"] = np.ascontiguousarray(
        np.asarray(inputs["L0_ffn_b1"], np.float32).reshape(NI, 128).T)
    g["b2c"] = np.ascontiguousarray(
        np.asarray(inputs["L0_ffn_b2"], np.float32).reshape(NCH, 128).T)

    wqkv = T(inputs["L1_attn_w_qkv"])
    perm = _evens_odds_perm()
    g["wq1"] = np.ascontiguousarray(wqkv[:, 0:D][:, perm])
    g["wk1"] = np.ascontiguousarray(wqkv[:, D:2 * D][:, perm])
    g["wv1"] = np.ascontiguousarray(wqkv[:, 2 * D:3 * D])
    g["wout1"] = T(inputs["L1_attn_w_out"])
    g["boutc"] = np.ascontiguousarray(
        np.asarray(inputs["L1_attn_b_out"], np.float32).reshape(NCH, 128).T)
    g["wfc"] = T(inputs["L1_ffn_w_fc"])
    g["wproj"] = T(inputs["L1_ffn_w_proj"])
    g["uT"] = T(inputs["L1_ffn_u"])
    g["vmat"] = C(inputs["L1_ffn_v"])
    g["fbT"] = T(inputs["L1_ffn_fast_b"])
    g["gatec"] = np.full((16, 1), float(np.asarray(inputs["L1_ffn_gate"])),
                         dtype=np.float32)

    inv_freq = 1.0 / 10000.0 ** (np.arange(0, HD, 2, dtype=np.float32) / HD)
    t = np.arange(S, dtype=np.float32)
    fr = np.outer(t, inv_freq)
    cos = np.cos(fr).astype(np.float32).T
    sin = np.sin(fr).astype(np.float32).T
    g["t1"] = np.ascontiguousarray(np.vstack([cos, cos, cos, cos]))
    g["t2"] = np.ascontiguousarray(np.vstack([-sin, sin, -sin, sin]))
    sw = np.zeros((128, 128), dtype=np.float32)
    for blk in range(4):
        b = blk * 32
        dst = (b + 32) % 64 + (b // 64) * 64
        for i in range(32):
            sw[dst + i, b + i] = 1.0
    g["pswap"] = sw
    return g



class Ctx:
    pass


def setup_common(P, A, es):
    C = Ctx()
    nc = P.nc
    C.psum = es.enter_context(nc.psum_tensor("psum", [128, 4096], F32))
    C.bank = [C.psum[:, 512 * i:512 * (i + 1)] for i in range(8)]
    C.bank_free = [(None, 0)] * 8
    C.dw = P.sem("dw", es)
    C.dx = P.sem("dx", es)
    C.pe = P.sem("pe", es)
    C.ve = P.sem("ve", es)
    C.se = P.sem("se", es)
    C.ge = P.sem("ge", es)
    C.all_sems = [C.dw, C.dx, C.pe, C.ve, C.se, C.ge]
    C.ones_bf_col = A.take([128, 1], BF16, "ones_col")
    C.ones_f_row = A.take([1, 128], F32, "ones_f")
    C.ones_bf_row = A.take([1, 128], BF16, "ones_bfr")
    C.half_f_row = A.take([1, 128], F32, "half_f")
    C.nhalf_f_row = A.take([1, 128], F32, "nhalf_f")
    C.eps_t = A.take([1, 1], F32, "eps")
    C.ones2_bf = A.take([65, 128], BF16, "ones2")
    P.op("gpsimd", "memset", C.ones_bf_col, 1.0)
    P.op("gpsimd", "memset", C.ones_f_row, 1.0)
    P.op("gpsimd", "memset", C.ones_bf_row, 1.0)
    P.op("gpsimd", "memset", C.half_f_row, 0.5)
    P.op("gpsimd", "memset", C.eps_t, 1e-5)
    P.op("gpsimd", "memset", C.ones2_bf, 1.0)
    P.op("gpsimd", "memset", C.nhalf_f_row, -0.5, inc=C.ge)
    return C


def use_bank(P, C, i):
    s, n = C.bank_free[i]
    if s is not None:
        P.wait("tensor", s, n)


def release_bank(C, i, sem):
    C.bank_free[i] = (sem, sem.n)


def emit_ln(P, A, C, x_sb, xn_out):
    m0 = A.mark()
    xb = A.take([128, NCH, S], BF16, "ln_xb")
    sums = A.take([1, 2, S], F32, "ln_sums")
    mneg = A.take([1, S], F32, "ln_mneg")
    rstd = A.take([1, S], F32, "ln_rstd")
    mneg_b = A.take([128, S], F32, "ln_mnegb")
    rstd_b = A.take([128, S], F32, "ln_rstdb")
    scrf = A.take([128, S], F32, "ln_scr")
    bs, bb = 7, 6

    for c in range(NCH):
        P.op("vector", "tensor_copy", xb[:, c, :], x_sb[:, c, :], inc=C.ve)
    for tb in range(S // 512):
        use_bank(P, C, bs)
        P.wait("tensor", C.ve)
        for c in range(NCH):
            P.mm(C.bank[bs][0:1, :], C.ones_bf_col, xb[:, c, bass.ts(tb, 512)],
                 start=(c == 0), stop=(c == NCH - 1),
                 inc=C.pe if c == NCH - 1 else None)
        P.wait("vector", C.pe)
        P.op("vector", "tensor_copy", sums[0:1, 0, bass.ts(tb, 512)],
             C.bank[bs][0:1, :], inc=C.ve)
        release_bank(C, bs, C.ve)
    for c in range(NCH):
        P.wait("scalar", C.pe)
        P.op("scalar", "activation", xb[:, c, :], xb[:, c, :], AF.Square,
             inc=C.se)
    for tb in range(S // 512):
        use_bank(P, C, bs)
        P.wait("tensor", C.se)
        for c in range(NCH):
            P.mm(C.bank[bs][0:1, :], C.ones_bf_col, xb[:, c, bass.ts(tb, 512)],
                 start=(c == 0), stop=(c == NCH - 1),
                 inc=C.pe if c == NCH - 1 else None)
        P.wait("vector", C.pe)
        P.op("vector", "tensor_copy", sums[0:1, 1, bass.ts(tb, 512)],
             C.bank[bs][0:1, :], inc=C.ve)
        release_bank(C, bs, C.ve)
    P.op("vector", "tensor_scalar", mneg, sums[0:1, 0, :], -1.0 / D, None, ALU.mult)
    P.op("vector", "tensor_scalar", rstd, sums[0:1, 0, :], 1.0 / D, None, ALU.mult)
    P.op("vector", "tensor_mul", rstd, rstd, rstd)
    P.op("vector", "scalar_tensor_tensor", rstd, sums[0:1, 1, :], 1.0 / D, rstd,
         ALU.mult, ALU.subtract, inc=C.ve)
    P.wait("scalar", C.ve)
    P.op("scalar", "activation", rstd, rstd, AF.Sqrt, bias=C.eps_t, inc=C.se)
    P.wait("vector", C.se)
    P.op("vector", "reciprocal", rstd, rstd, inc=C.ve)
    for tb in range(S // 512):
        use_bank(P, C, bb)
        P.wait("tensor", C.ve)
        P.mm(C.bank[bb][:, :], C.ones_f_row, mneg[0:1, bass.ts(tb, 512)], inc=C.pe)
        P.wait("vector", C.pe)
        P.op("vector", "tensor_copy", mneg_b[:, bass.ts(tb, 512)], C.bank[bb][:, :],
             inc=C.ve)
        P.wait("tensor", C.ve)
        P.mm(C.bank[bb][:, :], C.ones_f_row, rstd[0:1, bass.ts(tb, 512)], inc=C.pe)
        P.wait("vector", C.pe)
        P.op("vector", "tensor_copy", rstd_b[:, bass.ts(tb, 512)], C.bank[bb][:, :],
             inc=C.ve)
        release_bank(C, bb, C.ve)
    for c in range(NCH):
        P.op("vector", "tensor_add", scrf, x_sb[:, c, :], mneg_b)
        P.op("vector", "tensor_mul", xn_out[:, c, :], scrf, rstd_b, inc=C.ve)
    A.reset(m0)


def emit_ffn(P, A, C, xn, w1_sb, w2_sb, g_hbm, layer, b1c=None, b2c=None):
    m0 = A.mark()
    rr = A.take([128, NI, TQB], BF16, "rr")
    gtile = A.take([128, NCH, TQB], BF16, "gtile")
    scrf = A.take([128, TQB], F32, "ffn_scr")
    f1b = [0, 1]
    f2b = [2, 3, 4, 5]
    gtile_free = C.dx.n
    for tb in range(NFB):
        for c2 in range(NI):
            bk = f1b[c2 % 2]
            use_bank(P, C, bk)
            for ci in range(NCH):
                P.mm(C.bank[bk][:, 0:TQB], w1_sb[:, ci, bass.ts(c2, 128)],
                     xn[:, ci, bass.ts(tb, TQB)],
                     start=(ci == 0), stop=(ci == NCH - 1),
                     inc=C.pe if ci == NCH - 1 else None)
            if layer == 0:
                P.wait("vector", C.pe)
                P.op("vector", "tensor_scalar", rr[:, c2, :], C.bank[bk][:, 0:TQB],
                     b1c[:, c2:c2 + 1], 0.0, ALU.add, ALU.max, inc=C.ve)
                release_bank(C, bk, C.ve)
            else:
                P.wait("vector", C.pe)
                P.op("vector", "tensor_scalar", scrf, C.bank[bk][:, 0:TQB],
                     0.0, None, ALU.max, inc=C.ve)
                release_bank(C, bk, C.ve)
                P.wait("scalar", C.ve)
                P.op("scalar", "activation", rr[:, c2, :], scrf, AF.Square,
                     inc=C.se)
        for oc in range(NCH):
            bk = f2b[oc % 4]
            use_bank(P, C, bk)
            if layer == 1:
                P.wait("tensor", C.se)
            else:
                P.wait("tensor", C.ve)
            for c2 in range(NI):
                P.mm(C.bank[bk][:, 0:TQB], w2_sb[:, c2, bass.ts(oc, 128)],
                     rr[:, c2, :],
                     start=(c2 == 0), stop=(c2 == NI - 1),
                     inc=C.pe if c2 == NI - 1 else None)
            P.wait("scalar", C.pe)
            P.wait("scalar", C.dx, gtile_free)
            if layer == 0:
                P.op("scalar", "activation", gtile[:, oc, :], C.bank[bk][:, 0:TQB],
                     AF.Silu, bias=b2c[:, oc:oc + 1], inc=C.se)
            else:
                P.op("scalar", "activation", gtile[:, oc, :], C.bank[bk][:, 0:TQB],
                     AF.Silu, inc=C.se)
            release_bank(C, bk, C.se)
        P.wait("sync", C.se)
        P.dma("sync", g_hbm[:, :, bass.ts(tb, TQB)].rearrange("c p t -> p c t"),
              gtile, C.dx)
        gtile_free = C.dx.n
    A.reset(m0)


def emit_qk_proj_l0(P, A, C, xn, w_sb, out_bf):
    bks = [0, 1]
    free_i = 0
    for oc in range(NCH):
        for tb in range(S // 512):
            bk = bks[free_i % 2]
            free_i += 1
            use_bank(P, C, bk)
            for ci in range(NCH):
                P.mm(C.bank[bk][:, :], w_sb[:, ci, bass.ts(oc, 128)],
                     xn[:, ci, bass.ts(tb, 512)],
                     start=(ci == 0), stop=(ci == NCH - 1),
                     inc=C.pe if ci == NCH - 1 else None)
            P.wait("vector", C.pe)
            P.op("vector", "tensor_copy", out_bf[:, oc, bass.ts(tb, 512)],
                 C.bank[bk][:, :], inc=C.ve)
            release_bank(C, bk, C.ve)


def emit_qk_proj_rope(P, A, C, xn, w_sb, out_bf, t1, t2, pswap_bf):
    m0 = A.mark()
    qpre = A.take([128, 512], BF16, "rope_qpre")
    s1 = A.take([128, 512], F32, "rope_s1")
    s2 = A.take([128, 512], F32, "rope_s2")
    pi = 0
    for oc in range(NCH):
        for tb in range(S // 512):
            bk = [0, 1][pi % 2]
            bk2 = [2, 3][pi % 2]
            pi += 1
            use_bank(P, C, bk)
            for ci in range(NCH):
                P.mm(C.bank[bk][:, :], w_sb[:, ci, bass.ts(oc, 128)],
                     xn[:, ci, bass.ts(tb, 512)],
                     start=(ci == 0), stop=(ci == NCH - 1),
                     inc=C.pe if ci == NCH - 1 else None)
            P.wait("vector", C.pe)
            P.op("vector", "tensor_copy", qpre, C.bank[bk][:, :], inc=C.ve)
            release_bank(C, bk, C.ve)
            use_bank(P, C, bk2)
            P.wait("tensor", C.ve)
            P.mm(C.bank[bk2][:, :], pswap_bf, qpre, inc=C.pe)
            P.op("vector", "tensor_mul", s1, qpre, t1[:, bass.ts(tb, 512)])
            P.wait("vector", C.pe)
            P.op("vector", "tensor_mul", s2, C.bank[bk2][:, :],
                 t2[:, bass.ts(tb, 512)], inc=C.ve)
            release_bank(C, bk2, C.ve)
            P.op("vector", "tensor_add", out_bf[:, oc, bass.ts(tb, 512)], s1, s2,
                 inc=C.ve)
    A.reset(m0)


def emit_v_proj(P, A, C, xn, wv_sb, v_aug, col_off=0):
    pi = 0
    for tk in range(NTK):
        for ob in range(2):
            bk = [0, 1][pi % 2]
            pi += 1
            use_bank(P, C, bk)
            for ci in range(NCH):
                P.mm(C.bank[bk][:, :], xn[:, ci, bass.ts(tk, 128)],
                     wv_sb[:, ci, col_off + ob * 512:col_off + (ob + 1) * 512],
                     start=(ci == 0), stop=(ci == NCH - 1),
                     inc=C.pe if ci == NCH - 1 else None)
            P.wait("vector", C.pe)
            P.op("vector", "tensor_copy",
                 v_aug[:, tk, ob * 8:(ob + 1) * 8, 0:64],
                 C.bank[bk][:, :].rearrange("p (h d) -> p h d", h=8), inc=C.ve)
            release_bank(C, bk, C.ve)


def emit_mshift(P, A, C, q_bf, hp, mshift, kmaxs, qsq_scr, msrow, inv_scale):
    P.wait("scalar", C.pe)
    P.op("scalar", "activation", qsq_scr, q_bf[:, hp, :], AF.Square, inc=C.se)
    for tb in range(4):
        use_bank(P, C, tb)
    P.wait("tensor", C.se)
    for tb in range(4):
        P.mm(C.psum[0:1, bass.ts(tb, 512)], C.ones_bf_col[0:64, :],
             qsq_scr[0:64, bass.ts(tb, 512)])
    for tb in range(4):
        P.mm(C.psum[64:65, bass.ts(tb, 512)], C.ones_bf_col[64:128, :],
             qsq_scr[64:128, bass.ts(tb, 512)],
             inc=C.pe if tb == 3 else None)
    P.wait("vector", C.pe)
    P.op("vector", "tensor_copy", msrow, C.psum[0:65, 0:2048], inc=C.ve)
    for tb in range(4):
        release_bank(C, tb, C.ve)
    P.wait("scalar", C.ve)
    P.op("scalar", "activation", msrow, msrow, AF.Ln)
    P.op("scalar", "activation", msrow, msrow, AF.Exp, scale=0.5, inc=C.se)
    P.wait("vector", C.se)
    P.op("vector", "tensor_scalar", msrow, msrow,
         kmaxs[:, hp:hp + 1], -64.0, ALU.mult, ALU.add)
    P.op("vector", "tensor_scalar", msrow, msrow, 0.0, -float(inv_scale),
         ALU.max, ALU.mult)
    P.op("vector", "tensor_copy", mshift, msrow, inc=C.ve)


def emit_kmaxs(P, A, C, k_bf, kmaxs, scr, row, inv_scale):
    m0 = A.mark()
    one = A.take([65, 1], F32, "km_one")
    for hp in range(NCH):
        P.wait("scalar", C.pe)
        P.wait("scalar", C.ve)
        P.op("scalar", "activation", scr, k_bf[:, hp, :], AF.Square, inc=C.se)
        for tb in range(4):
            use_bank(P, C, tb)
        P.wait("tensor", C.se)
        for tb in range(4):
            P.mm(C.psum[0:1, bass.ts(tb, 512)], C.ones_bf_col[0:64, :],
                 scr[0:64, bass.ts(tb, 512)])
        for tb in range(4):
            P.mm(C.psum[64:65, bass.ts(tb, 512)], C.ones_bf_col[64:128, :],
                 scr[64:128, bass.ts(tb, 512)],
                 inc=C.pe if tb == 3 else None)
        P.wait("vector", C.pe)
        P.op("vector", "tensor_copy", row, C.psum[0:65, 0:2048], inc=C.ve)
        for tb in range(4):
            release_bank(C, tb, C.ve)
        P.op("vector", "reduce_max", one, row, mybir.AxisListType.X, inc=C.ve)
        P.wait("scalar", C.ve)
        P.op("scalar", "activation", one, one, AF.Ln)
        P.op("scalar", "activation", one, one, AF.Exp, scale=0.5, inc=C.se)
        P.wait("vector", C.se)
        P.op("vector", "tensor_scalar", kmaxs[:, hp:hp + 1], one,
             1.0 / float(inv_scale), None, ALU.mult, inc=C.ve)
    A.reset(m0)


def emit_attention(P, A, C, q_bf, k_bf, v_aug, scale, consume, shifted=False):
    m0 = A.mark()
    recips = A.take([1, 2, 512], F32, "recips")
    rb = A.take([128, 512], F32, "rb")
    s_blk = A.take([128, 512], F32, "s_blk")
    if shifted:
        inv_scale = 1.0 / scale
        mshift = A.take([65, S], BF16, "mshift")
        kmaxs = A.take([65, NCH], F32, "kmaxs")
        qsq_scr = A.take([128, S], BF16, "qsq")
        msrow = A.take([65, S], F32, "msrow")
        emit_kmaxs(P, A, C, k_bf, kmaxs, qsq_scr, msrow, inv_scale)
    rtmp = A.take([1, 2, 512], F32, "rtmp")
    recips_bf = A.take([1, 2, 512], BF16, "recips_bf")
    ptab = [A.take([128, 1024], BF16, "ptab0"),
            A.take([128, 1024], BF16, "ptab1")]
    pv_pairs = [(4, 5), (6, 7)]

    def emit_scores(hp, tq, tk):
        p = tk % 2
        bA, bB = 2 * p, 2 * p + 1
        use_bank(P, C, bA)
        use_bank(P, C, bB)
        if shifted:
            P.mm(C.bank[bA], C.ones2_bf[0:1, :], mshift[0:1, tq],
                 start=True, stop=False)
            P.mm(C.bank[bB], C.ones2_bf[64:65, :], mshift[64:65, tq],
                 start=True, stop=False)
            P.mm(C.bank[bA], k_bf[0:64, hp, bass.ts(tk, 128)],
                 q_bf[0:64, hp, tq], start=False, stop=True, tp=(0, 0))
            P.mm(C.bank[bB], k_bf[64:128, hp, bass.ts(tk, 128)],
                 q_bf[64:128, hp, tq], start=False, stop=True, tp=(64, 0),
                 inc=C.pe)
        else:
            P.mm(C.bank[bA], k_bf[0:64, hp, bass.ts(tk, 128)],
                 q_bf[0:64, hp, tq], start=True, stop=True, tp=(0, 0))
            P.mm(C.bank[bB], k_bf[64:128, hp, bass.ts(tk, 128)],
                 q_bf[64:128, hp, tq], start=True, stop=True, tp=(64, 0),
                 inc=C.pe)

    def emit_exp(tk):
        p = tk % 2
        P.wait("scalar", C.pe)
        P.op("scalar", "activation", ptab[p],
             C.psum[:, 1024 * p:1024 * (p + 1)], AF.Exp, scale=scale,
             inc=C.se)
        release_bank(C, 2 * p, C.se)
        release_bank(C, 2 * p + 1, C.se)
        return C.se.n

    def emit_pv(hp, tk, se_need, bOA, bOB):
        p = tk % 2
        P.wait("tensor", C.se, se_need)
        P.mm(C.bank[bOA][0:65, :], v_aug[:, tk, 2 * hp, :], ptab[p][:, 0:512],
             start=(tk == 0), stop=(tk == NTK - 1))
        P.mm(C.bank[bOB][0:65, :], v_aug[:, tk, 2 * hp + 1, :],
             ptab[p][:, 512:1024],
             start=(tk == 0), stop=(tk == NTK - 1), inc=C.pe)

    pending = [None]

    def norm_part2(hp, tqb, bOA, bOB):
        use_bank(P, C, bOA)
        use_bank(P, C, bOB)
        P.wait("tensor", C.ve)
        P.mm(C.bank[bOA][0:64, :], C.ones_bf_row[0:1, 0:64],
             recips_bf[0:1, 0, :])
        P.mm(C.bank[bOB][0:64, :], C.ones_bf_row[0:1, 0:64],
             recips_bf[0:1, 1, :], inc=C.pe)
        P.wait("vector", C.pe)
        P.op("vector", "tensor_copy", rb[0:64, :], C.bank[bOA][0:64, :])
        P.op("vector", "tensor_copy", rb[64:128, :], C.bank[bOB][0:64, :],
             inc=C.ve)
        release_bank(C, bOA, C.ve)
        release_bank(C, bOB, C.ve)
        P.op("vector", "tensor_mul", s_blk[0:64, :], s_blk[0:64, :],
             rb[0:64, :])
        P.op("vector", "tensor_mul", s_blk[64:128, :], s_blk[64:128, :],
             rb[64:128, :], inc=C.ve)
        consume(hp, tqb, s_blk)

    for hp in range(NCH):
        if shifted:
            emit_mshift(P, A, C, q_bf, hp, mshift, kmaxs, qsq_scr, msrow,
                        inv_scale)
            P.wait("tensor", C.ve)
        for tqb in range(S // 512):
            tq = bass.ts(tqb, 512)
            it = hp * 4 + tqb
            bOA, bOB = pv_pairs[it % 2]
            se_at = {}
            for tk in range(NTK):
                emit_scores(hp, tq, tk)
                if tk == 0:
                    use_bank(P, C, bOA)
                    use_bank(P, C, bOB)
                se_at[tk] = emit_exp(tk)
                if tk > 0:
                    emit_pv(hp, tk - 1, se_at[tk - 1], bOA, bOB)
                if tk == 10 and pending[0] is not None:
                    norm_part2(*pending[0])
                    pending[0] = None
            emit_pv(hp, NTK - 1, se_at[NTK - 1], bOA, bOB)
            P.wait("vector", C.pe)
            P.op("vector", "tensor_scalar", rtmp[0:1, 0, :],
                 C.bank[bOA][64:65, :], 1e-36, None, ALU.add)
            P.op("vector", "tensor_scalar", rtmp[0:1, 1, :],
                 C.bank[bOB][64:65, :], 1e-36, None, ALU.add)
            P.op("vector", "tensor_copy", s_blk[0:64, :], C.bank[bOA][0:64, :])
            P.op("vector", "tensor_copy", s_blk[64:128, :],
                 C.bank[bOB][0:64, :], inc=C.ve)
            release_bank(C, bOA, C.ve)
            release_bank(C, bOB, C.ve)
            P.op("vector", "reciprocal", recips[0:1, 0, :], rtmp[0:1, 0, :])
            P.op("vector", "reciprocal", recips[0:1, 1, :], rtmp[0:1, 1, :])
            P.op("vector", "tensor_copy", recips_bf[0:1, 0, :],
                 recips[0:1, 0, :])
            P.op("vector", "tensor_copy", recips_bf[0:1, 1, :],
                 recips[0:1, 1, :], inc=C.ve)
            pending[0] = (hp, tqb, bOA, bOB)
    norm_part2(*pending[0])
    pending[0] = None
    A.reset(m0)


def emit_ternarize(P, A, C, w_hbm, n_in, n_out, weff_sb, tag):
    m0 = A.mark()
    nic = n_in // 128
    nob = n_out // 512
    chunk_f = [A.take([128, n_out], F32, f"tn_cf0_{tag}"),
               A.take([128, n_out], F32, f"tn_cf1_{tag}")]
    chunk_abs = [A.take([128, n_out], BF16, f"tn_ca0_{tag}"),
                 A.take([128, n_out], BF16, f"tn_ca1_{tag}")]
    srow = A.take([1, n_out], F32, f"tn_sr_{tag}")
    bh = A.take([128, n_out], F32, f"tn_bh_{tag}")
    bhn = A.take([128, n_out], F32, f"tn_bhn_{tag}")
    sbc = A.take([128, n_out], BF16, f"tn_sbc_{tag}")
    mneg = chunk_abs[0]
    w_r = w_hbm.rearrange("(c p) o -> p c o", p=128)
    cf_free = [C.se.n, C.se.n]
    ca_free = [C.pe.n, C.pe.n]
    for c in range(nic):
        p = c % 2
        P.wait("sync", C.se, cf_free[p])
        P.dma("sync", chunk_f[p], w_r[:, c, :], C.dx)
        P.wait("scalar", C.dx)
        P.wait("scalar", C.pe, ca_free[p])
        P.op("scalar", "activation", chunk_abs[p], chunk_f[p], AF.Abs, inc=C.se)
        cf_free[p] = C.se.n
        P.wait("tensor", C.se)
        for ob in range(nob):
            if c == 0:
                use_bank(P, C, ob)
            P.mm(C.bank[ob][0:1, :], C.ones_bf_col,
                 chunk_abs[p][:, bass.ts(ob, 512)],
                 start=(c == 0), stop=(c == nic - 1),
                 inc=C.pe if ob == nob - 1 else None)
        ca_free[p] = C.pe.n
    P.wait("vector", C.pe)
    for ob in range(nob):
        P.op("vector", "tensor_scalar", srow[0:1, bass.ts(ob, 512)],
             C.bank[ob][0:1, :], 1.0 / n_in, 1e-5, ALU.mult, ALU.max,
             inc=C.ve if ob == nob - 1 else None)
    for ob in range(nob):
        release_bank(C, ob, C.ve)
    for ob in range(nob):
        for row_const, dst in ((C.half_f_row, bh), (C.nhalf_f_row, bhn),
                               (C.ones_f_row, None)):
            bk = ob % 2
            use_bank(P, C, bk)
            P.wait("tensor", C.ve)
            P.mm(C.bank[bk], row_const, srow[0:1, bass.ts(ob, 512)], inc=C.pe)
            P.wait("vector", C.pe)
            out_ap = (sbc if dst is None else dst)[:, bass.ts(ob, 512)]
            P.op("vector", "tensor_copy", out_ap, C.bank[bk], inc=C.ve)
            release_bank(C, bk, C.ve)
    cf2_free = [(C.ve.n, C.ge.n), (C.ve.n, C.ge.n)]
    P.wait("gpsimd", C.ve)
    for c in range(nic):
        p = c % 2
        ve_n, ge_n = cf2_free[p]
        P.wait("sync", C.ve, ve_n)
        P.wait("sync", C.ge, ge_n)
        P.dma("sync", chunk_f[p], w_r[:, c, :], C.dx)
        wch = weff_sb[:, c, :]
        P.wait("vector", C.dx)
        P.op("vector", "tensor_tensor", wch, chunk_f[p], bh, ALU.is_ge)
        P.op("vector", "tensor_tensor", mneg, chunk_f[p], bhn, ALU.is_le)
        P.op("vector", "tensor_sub", wch, wch, mneg)
        P.op("vector", "tensor_mul", wch, wch, sbc, inc=C.ve)
        cf2_free[p] = (C.ve.n, C.ge.n)
    A.reset(m0)


def emit_lowrank_fold(P, A, C, io, weff_fc):
    m0 = A.mark()
    u_sb = A.take([16, INTER], BF16, "lr_u")
    v_sb = A.take([16, D], BF16, "lr_v")
    fb_sb = A.take([16, RANK], F32, "lr_fb")
    gate_c = A.take([16, 1], F32, "lr_gate")
    gfb = A.take([16, RANK], BF16, "lr_gfb")
    at = A.take([16, D], BF16, "lr_at")
    P.dma("gpsimd", u_sb, io.w["uT"], C.dw)
    P.dma("gpsimd", v_sb, io.w["vmat"], C.dw)
    P.dma("gpsimd", fb_sb, io.w["fbT"], C.dw)
    P.dma("gpsimd", gate_c, io.w["gatec"], C.dw)
    P.wait("vector", C.dw)
    P.op("vector", "tensor_scalar", gfb, fb_sb, gate_c, None, ALU.mult, inc=C.ve)
    use_bank(P, C, 0)
    use_bank(P, C, 1)
    P.wait("tensor", C.ve)
    P.mm(C.bank[0][0:16, :], gfb, v_sb[:, 0:512], start=True, stop=True)
    P.mm(C.bank[1][0:16, :], gfb, v_sb[:, 512:1024], start=True, stop=True,
         inc=C.pe)
    P.wait("vector", C.pe)
    P.op("vector", "tensor_copy", at[:, 0:512], C.bank[0][0:16, :])
    P.op("vector", "tensor_copy", at[:, 512:1024], C.bank[1][0:16, :], inc=C.ve)
    release_bank(C, 0, C.ve)
    release_bank(C, 1, C.ve)
    for ci in range(NCH):
        for ob in range(INTER // 512):
            bk = ob % 2
            use_bank(P, C, bk)
            P.wait("tensor", C.ve)
            P.mm(C.bank[bk][:, :], at[:, bass.ts(ci, 128)],
                 u_sb[:, bass.ts(ob, 512)], inc=C.pe)
            P.wait("vector", C.pe)
            P.op("vector", "tensor_add", weff_fc[:, ci, bass.ts(ob, 512)],
                 weff_fc[:, ci, bass.ts(ob, 512)], C.bank[bk][:, :], inc=C.ve)
            release_bank(C, bk, C.ve)
    A.reset(m0)



def emit_layer0(P, A, C, io, x_src, x_dst):
    m_layer = A.mark()
    xn = A.take([128, NCH, S], BF16, "xn0")
    x_src_r = x_src.rearrange("(c p) t -> p c t", p=128)
    x_dst_r = x_dst.rearrange("(c p) t -> c p t", p=128)
    g_hbm = io.g0

    m0 = A.mark()
    x_sb = A.take([128, NCH, S], F32, "x0")
    P.dma("sync", x_sb, x_src_r, C.dx)
    P.wait("vector", C.dx)
    P.wait("vector", C.ge)
    P.wait("tensor", C.ge)
    emit_ln(P, A, C, x_sb, xn)
    A.reset(m0)
    P.barrier(C.all_sems)

    m1 = A.mark()
    w1_sb = A.take([128, NCH, INTER], BF16, "w1")
    w2_sb = A.take([128, NI, D], BF16, "w2")
    b1c = A.take([128, NI], F32, "b1c")
    b2c = A.take([128, NCH], F32, "b2c")
    P.dma("gpsimd", w1_sb, io.w["w1"].rearrange("(c p) o -> p c o", p=128), C.dw)
    P.dma("gpsimd", w2_sb, io.w["w2"].rearrange("(c p) o -> p c o", p=128), C.dw)
    P.dma("sync", b1c, io.w["b1c"], C.dx)
    P.dma("sync", b2c, io.w["b2c"], C.dx)
    P.wait("tensor", C.dw)
    P.wait("vector", C.dx)
    P.wait("scalar", C.dx)
    emit_ffn(P, A, C, xn, w1_sb, w2_sb, g_hbm, 0, b1c, b2c)
    A.reset(m1)
    P.barrier(C.all_sems)

    q_bf = A.take([128, NCH, S], BF16, "q0")
    k_bf = A.take([128, NCH, S], BF16, "k0")
    v_aug = A.take([128, NTK, H, 65], BF16, "v0")
    m2 = A.mark()
    wslot = [A.take([128, NCH, D], BF16, "ws0"),
             A.take([128, NCH, D], BF16, "ws1")]
    P.op("gpsimd", "memset", v_aug[:, :, :, 64:65], 1.0, inc=C.ge)
    P.dma("gpsimd", wslot[0], io.w["wq"].rearrange("(c p) o -> p c o", p=128), C.dw)
    P.wait("tensor", C.dw)
    emit_qk_proj_l0(P, A, C, xn, wslot[0], q_bf)
    P.dma("gpsimd", wslot[1], io.w["wk"].rearrange("(c p) o -> p c o", p=128), C.dw)
    P.wait("tensor", C.dw)
    emit_qk_proj_l0(P, A, C, xn, wslot[1], k_bf)
    P.wait("gpsimd", C.pe)
    P.dma("gpsimd", wslot[0], io.w["wv"].rearrange("(c p) o -> p c o", p=128), C.dw)
    P.wait("tensor", C.dw)
    P.wait("tensor", C.ge)
    emit_v_proj(P, A, C, xn, wslot[0], v_aug)
    A.reset(m2)
    P.barrier(C.all_sems)
    if io.dbg:
        P.dma("sync", io.d_xn, xn, C.dx)
        P.dma("sync", io.d_q, q_bf, C.dx)
        P.dma("sync", io.d_k, k_bf, C.dx)
        P.dma("sync", io.d_v, v_aug, C.dx)

    xst = A.take([128, 2, 512], F32, "xstage")
    gst = A.take([128, 2, 512], BF16, "gstage")
    stage_free = [(None, 0), (None, 0)]
    g_hbm_r = g_hbm
    x_src_cr = x_src.rearrange("(c p) t -> c p t", p=128)

    def consume0(hp, tqb, s_blk):
        sl = (hp * 4 + tqb) % 2
        if io.dbg:
            P.wait("sync", C.ve)
            P.dma("sync", io.d_s[hp, :, bass.ts(tqb, 512)], s_blk, C.dx)
            P.wait("vector", C.dx)
        P.wait("vector", C.dx, stage_issue[(hp, tqb)])
        P.op("vector", "tensor_mul", s_blk, s_blk, gst[:, sl, :])
        P.op("vector", "tensor_add", xst[:, sl, :], xst[:, sl, :], s_blk,
             inc=C.ve)
        P.wait("sync", C.ve)
        P.dma("sync", x_dst_r[hp, :, bass.ts(tqb, 512)], xst[:, sl, :], C.dx)
        stage_free[sl] = (C.dx, C.dx.n)

    stage_issue = {}

    def consume_wrap(hp, tqb, s_blk):
        consume0(hp, tqb, s_blk)
        nxt = hp * 4 + tqb + 1
        if nxt < 32:
            issue_stage(nxt // 4, nxt % 4)

    def issue_stage(hp, tqb):
        sl = (hp * 4 + tqb) % 2
        s, n = stage_free[sl]
        if s is not None:
            P.wait("sync", s, n)
        P.dma("sync", xst[:, sl, :], x_src_cr[hp, :, bass.ts(tqb, 512)], C.dx)
        P.dma("sync", gst[:, sl, :], g_hbm_r[hp, :, bass.ts(tqb, 512)], C.dx)
        stage_issue[(hp, tqb)] = C.dx.n

    issue_stage(0, 0)
    issue_stage(0, 1)
    emit_attention(P, A, C, q_bf, k_bf, v_aug, 1.0 / 32.0, consume_wrap)
    A.reset(m_layer)
    P.barrier(C.all_sems)


def emit_layer1(P, A, C, io, x_src, x_dst):
    m_layer = A.mark()
    xn = A.take([128, NCH, S], BF16, "xn1")
    x_src_r = x_src.rearrange("(c p) t -> p c t", p=128)
    x_src_cr = x_src.rearrange("(c p) t -> c p t", p=128)
    x_dst_r = x_dst.rearrange("(c p) t -> c p t", p=128)
    g_hbm = io.g1

    m0 = A.mark()
    x_sb = A.take([128, NCH, S], F32, "x1")
    P.dma("sync", x_sb, x_src_r, C.dx)
    P.wait("vector", C.dx)
    emit_ln(P, A, C, x_sb, xn)
    A.reset(m0)
    P.barrier(C.all_sems)

    weff_fc = A.take([128, NCH, INTER], BF16, "weff_fc")
    emit_ternarize(P, A, C, io.w["wfc"], D, INTER, weff_fc, "fc")
    P.barrier(C.all_sems)
    emit_lowrank_fold(P, A, C, io, weff_fc)
    P.barrier(C.all_sems)
    weff_pj = A.take([128, NI, D], BF16, "weff_pj")
    emit_ternarize(P, A, C, io.w["wproj"], INTER, D, weff_pj, "pj")
    P.barrier(C.all_sems)
    emit_ffn(P, A, C, xn, weff_fc, weff_pj, g_hbm, 1)
    A.reset(m_layer)
    xn = A.take([128, NCH, S], BF16, "xn1_alias")
    P.barrier(C.all_sems)

    t1 = A.take([128, S], BF16, "t1")
    t2 = A.take([128, S], BF16, "t2")
    pswap = A.take([128, 128], BF16, "pswap")
    attnout = A.take([128, NCH, S], BF16, "attnout")
    q_bf = A.take([128, NCH, S], BF16, "q1")
    k_bf = A.take([128, NCH, S], BF16, "k1")
    v_aug = A.take([128, NTK, H, 65], BF16, "v1")
    m2 = A.mark()
    wslot = A.take([128, NCH, D], BF16, "w1s0")
    P.dma("gpsimd", t1, io.w["t1"], C.dw)
    P.dma("gpsimd", t2, io.w["t2"], C.dw)
    P.dma("gpsimd", pswap, io.w["pswap"], C.dw)
    P.op("gpsimd", "memset", v_aug[:, :, :, 64:65], 1.0, inc=C.ge)
    P.dma("gpsimd", wslot, io.w["wq1"].rearrange("(c p) o -> p c o", p=128),
          C.dw)
    P.wait("tensor", C.dw)
    P.wait("vector", C.dw)
    emit_qk_proj_rope(P, A, C, xn, wslot, q_bf, t1, t2, pswap)
    P.wait("gpsimd", C.pe)
    P.dma("gpsimd", wslot, io.w["wk1"].rearrange("(c p) o -> p c o", p=128),
          C.dw)
    P.wait("tensor", C.dw)
    emit_qk_proj_rope(P, A, C, xn, wslot, k_bf, t1, t2, pswap)
    P.wait("gpsimd", C.pe)
    P.dma("gpsimd", wslot, io.w["wv1"].rearrange("(c p) o -> p c o", p=128),
          C.dw)
    P.wait("tensor", C.dw)
    P.wait("tensor", C.ge)
    emit_v_proj(P, A, C, xn, wslot, v_aug)
    A.reset(m2)
    P.barrier(C.all_sems)

    def consume1(hp, tqb, s_blk):
        P.op("vector", "tensor_copy", attnout[:, hp, bass.ts(tqb, 512)], s_blk,
             inc=C.ve)

    emit_attention(P, A, C, q_bf, k_bf, v_aug, 1.0 / 8.0, consume1, shifted=True)
    P.barrier(C.all_sems)

    m3 = A.mark()
    wout = A.take([128, NCH, D], BF16, "wout")
    boutc = A.take([128, NCH], F32, "boutc")
    xst = A.take([128, 2, 512], F32, "xst1")
    gst = A.take([128, 2, 512], BF16, "gst1")
    sblk = A.take([128, 512], F32, "sblk1")
    P.dma("gpsimd", wout, io.w["wout1"].rearrange("(c p) o -> p c o", p=128), C.dw)
    P.dma("sync", boutc, io.w["boutc"], C.dx)
    stage_free = [(None, 0), (None, 0)]
    stage_issue = {}

    def issue_stage(i):
        oc, tb = i // 4, i % 4
        sl = i % 2
        s, n = stage_free[sl]
        if s is not None:
            P.wait("sync", s, n)
        P.dma("sync", xst[:, sl, :], x_src_cr[oc, :, bass.ts(tb, 512)], C.dx)
        P.dma("sync", gst[:, sl, :], g_hbm[oc, :, bass.ts(tb, 512)], C.dx)
        stage_issue[i] = C.dx.n

    issue_stage(0)
    issue_stage(1)
    P.wait("tensor", C.dw)
    pi = 0
    for oc in range(NCH):
        for tb in range(S // 512):
            i = oc * 4 + tb
            sl = i % 2
            bk = [0, 1][pi % 2]
            pi += 1
            use_bank(P, C, bk)
            for ci in range(NCH):
                P.mm(C.bank[bk][:, :], wout[:, ci, bass.ts(oc, 128)],
                     attnout[:, ci, bass.ts(tb, 512)],
                     start=(ci == 0), stop=(ci == NCH - 1),
                     inc=C.pe if ci == NCH - 1 else None)
            P.wait("vector", C.pe)
            P.wait("vector", C.dx, stage_issue[i])
            P.op("vector", "tensor_scalar", sblk, C.bank[bk][:, :],
                 boutc[:, oc:oc + 1], None, ALU.add, inc=C.ve)
            release_bank(C, bk, C.ve)
            P.op("vector", "tensor_mul", sblk, sblk, gst[:, sl, :])
            P.op("vector", "tensor_add", xst[:, sl, :], xst[:, sl, :], sblk,
                 inc=C.ve)
            P.wait("sync", C.ve)
            P.dma("sync", x_dst_r[oc, :, bass.ts(tb, 512)], xst[:, sl, :], C.dx)
            stage_free[sl] = (C.dx, C.dx.n)
            if i + 2 < 32:
                issue_stage(i + 2)
    A.reset(m_layer)



W_L0 = ["wq", "wk", "wv", "w1", "w2", "b1c", "b2c"]
W_L1 = ["wq1", "wk1", "wv1", "wout1", "boutc", "wfc", "wproj", "uT", "vmat",
        "fbT", "gatec", "t1", "t2", "pswap"]


def build_program(layers=(0, 1), dbg=False):
    from contextlib import ExitStack
    nc = bass.Bass("TRN2", target_bir_lowering=False)
    io = Ctx()
    io.dbg = dbg
    io.x_in = nc.dram_tensor("x_in", [D, S], F32, kind="ExternalInput").ap()
    io.x_out = nc.dram_tensor("x_out", [D, S], F32, kind="ExternalOutput").ap()
    if layers == (0, 1):
        io.x_mid = nc.dram_tensor("x_mid", [D, S], F32, kind="Internal").ap()
    io.w = {}
    names = (W_L0 if 0 in layers else []) + (W_L1 if 1 in layers else [])
    shapes = dict(wq=(D, D), wk=(D, D), wv=(D, D), w1=(D, INTER), w2=(INTER, D),
                  b1c=(128, NI), b2c=(128, NCH), wq1=(D, D), wk1=(D, D),
                  wv1=(D, D), wout1=(D, D), boutc=(128, NCH), wfc=(D, INTER),
                  wproj=(INTER, D), uT=(RANK, INTER), vmat=(RANK, D),
                  fbT=(RANK, RANK), gatec=(16, 1), t1=(128, S), t2=(128, S),
                  pswap=(128, 128))
    for n in names:
        io.w[n] = nc.dram_tensor(n, list(shapes[n]), F32,
                                 kind="ExternalInput").ap()
    gkind = "ExternalOutput" if dbg else "Internal"
    if 0 in layers:
        io.g0 = nc.dram_tensor("g0", [NCH, 128, S], BF16, kind=gkind).ap()
    if 1 in layers:
        io.g1 = nc.dram_tensor("g1", [NCH, 128, S], BF16, kind=gkind).ap()
    if dbg:
        io.d_xn = nc.dram_tensor("d_xn", [128, NCH, S], BF16,
                                 kind="ExternalOutput").ap()
        io.d_q = nc.dram_tensor("d_q", [128, NCH, S], BF16,
                                kind="ExternalOutput").ap()
        io.d_k = nc.dram_tensor("d_k", [128, NCH, S], BF16,
                                kind="ExternalOutput").ap()
        io.d_v = nc.dram_tensor("d_v", [128, NTK, H, 65], BF16,
                                kind="ExternalOutput").ap()
        io.d_s = nc.dram_tensor("d_s", [NCH, 128, S], F32,
                                kind="ExternalOutput").ap()

    es = ExitStack()
    P = Prog(nc)
    A = Arena(nc)
    with nc.Block() as block:
        C = setup_common(P, A, es)
        if layers == (0, 1):
            emit_layer0(P, A, C, io, io.x_in, io.x_mid)
            emit_layer1(P, A, C, io, io.x_mid, io.x_out)
        elif layers == (0,):
            emit_layer0(P, A, C, io, io.x_in, io.x_out)
        elif layers == (1,):
            emit_layer1(P, A, C, io, io.x_in, io.x_out)
        P.wait("sync", C.dx)
        P.wait("sync", C.dw)
        P.op("sync", "nop")
        P.simulate()
        P.replay(block)
    es.close()
    return nc


def kernel(**inputs):
    g = host_prep(inputs)
    x = np.asarray(inputs["L0_attn_x"], dtype=np.float32)
    nc = build_program((0, 1))
    in_maps = []
    for b in range(8):
        m = dict(g)
        m["x_in"] = np.ascontiguousarray(x[b].T)
        in_maps.append(m)
    res = run_bass_kernel_spmd(nc, in_maps, core_ids=list(range(8)))
    out = np.stack([res.results[b]["x_out"].T for b in range(8)], axis=0)
    return out.astype(np.float32)


# version 16
# speedup vs baseline: 1.3070x; 1.0382x over previous
import numpy as np
import concourse.bass as bass
import concourse.mybir as mybir
from concourse.bass_utils import run_bass_kernel_spmd

F32 = mybir.dt.float32
BF16 = mybir.dt.bfloat16
AF = mybir.ActivationFunctionType
ALU = mybir.AluOpType

S = 2048
D = 1024
NCH = D // 128
H = 16
HD = 64
INTER = 4096
NI = INTER // 128
NTK = S // 128
RANK = 16
TQB = 256
NFB = S // TQB

ENGINES = ("tensor", "vector", "scalar", "gpsimd", "sync")


class Sem:
    __slots__ = ("h", "n", "name")

    def __init__(self, h, name):
        self.h = h
        self.n = 0
        self.name = name


class Prog:
    def __init__(self, nc):
        self.nc = nc
        self.recs = {e: [] for e in ENGINES}
        self._pend = {e: [] for e in ENGINES}
        self.sems = []

    def sem(self, name, es):
        s = Sem(es.enter_context(self.nc.semaphore(name)), name)
        self.sems.append(s)
        return s

    def wait(self, eng, sem, val=None):
        v = sem.n if val is None else val
        if v > 0:
            self._pend[eng].append((sem, v))

    def op(self, eng, meth, *args, inc=None, inc16=None, **kw):
        waits = self._pend[eng]
        self._pend[eng] = []
        incs = []
        if inc is not None:
            for s in (inc if isinstance(inc, (list, tuple)) else [inc]):
                s.n += 1
                incs.append((s, 1))
        if inc16 is not None:
            inc16.n += 16
            incs.append((inc16, 16))
        self.recs[eng].append((waits, meth, args, kw, incs))

    def dma(self, eng, out, in_, sem, **kw):
        self.op(eng, "dma_start", inc16=sem, out=out, in_=in_, **kw)

    def mm(self, out, lhsT, rhs, start=True, stop=True, inc=None, tp=None):
        self.op("tensor", "matmul", out, lhsT, rhs, start=start, stop=stop,
                skip_group_check=True, tile_position=tp, inc=inc)

    def barrier(self, sems):
        for e in ENGINES:
            for s in sems:
                self.wait(e, s)

    def simulate(self):
        pos = {e: 0 for e in ENGINES}
        val = {s.name: 0 for s in self.sems}
        total = sum(len(r) for r in self.recs.values())
        done = 0
        while done < total:
            progressed = False
            for e in ENGINES:
                while pos[e] < len(self.recs[e]):
                    waits, meth, args, kw, incs = self.recs[e][pos[e]]
                    if any(val[s.name] < v for s, v in waits):
                        break
                    for s, k in incs:
                        val[s.name] += k
                    pos[e] += 1
                    done += 1
                    progressed = True
            if not progressed:
                lines = []
                for e in ENGINES:
                    if pos[e] < len(self.recs[e]):
                        waits, meth, _, _, _ = self.recs[e][pos[e]]
                        ws = [(s.name, v, val[s.name]) for s, v in waits
                              if val[s.name] < v]
                        lines.append(f"{e}@{pos[e]}: {meth} blocked on {ws}")
                raise RuntimeError("DEADLOCK:\n" + "\n".join(lines))

    def replay(self, block):
        for e in ENGINES:
            recs = self.recs[e]

            def make(engine_recs):
                def emit(eng):
                    for waits, meth, args, kw, incs in engine_recs:
                        ww = {}
                        for s, v in waits:
                            if s.name in ww:
                                ww[s.name] = (s, max(v, ww[s.name][1]))
                            else:
                                ww[s.name] = (s, v)
                        for s, v in ww.values():
                            eng.wait_ge(s.h, v)
                        inst = getattr(eng, meth)(*args, **kw)
                        for s, k in incs:
                            inst = inst.then_inc(s.h, k)
                return emit
            getattr(block, e)(make(recs))


class Arena:
    def __init__(self, nc, nbytes=212608):
        self.t = nc.alloc_sbuf_tensor("arena", [128, nbytes], mybir.dt.uint8)
        self.nbytes = nbytes
        self.off = 0
        self.peak = 0

    def take(self, shape, dtype, name=""):
        esz = 4 if dtype == F32 else 2
        free = int(np.prod(shape[1:]))
        nb = free * esz
        off = (self.off + 31) // 32 * 32
        assert off + nb <= self.nbytes, \
            f"SBUF arena overflow at {name}: need {off + nb}, have {self.nbytes}"
        self.off = off + nb
        self.peak = max(self.peak, self.off)
        v = self.t.ap()[:, off:off + nb].bitcast(dtype)
        if len(shape) > 2:
            dims = " ".join(f"d{i}" for i in range(1, len(shape)))
            v = v.rearrange(f"p ({dims}) -> p {dims}",
                            **{f"d{i}": int(shape[i]) for i in range(1, len(shape))})
        if shape[0] != 128:
            v = v[0:shape[0]]
        return v

    def mark(self):
        return self.off

    def reset(self, mark):
        self.off = mark



def _evens_odds_perm():
    perm = np.zeros(D, dtype=np.int64)
    for h in range(H):
        perm[h * 64:h * 64 + 32] = h * 64 + np.arange(0, 64, 2)
        perm[h * 64 + 32:h * 64 + 64] = h * 64 + np.arange(1, 64, 2)
    return perm


def host_prep(inputs):
    g = {}
    T = lambda a: np.ascontiguousarray(np.asarray(a, dtype=np.float32).T)
    C = lambda a: np.ascontiguousarray(np.asarray(a, dtype=np.float32))
    g["wq"] = T(inputs["L0_attn_Wq"])
    g["wk"] = T(inputs["L0_attn_Wk"])
    g["wv"] = T(inputs["L0_attn_Wv"])
    g["w1"] = C(inputs["L0_ffn_W1"])
    g["w2"] = C(inputs["L0_ffn_W2"])
    g["b1c"] = np.ascontiguousarray(
        np.asarray(inputs["L0_ffn_b1"], np.float32).reshape(NI, 128).T)
    g["b2c"] = np.ascontiguousarray(
        np.asarray(inputs["L0_ffn_b2"], np.float32).reshape(NCH, 128).T)

    wqkv = T(inputs["L1_attn_w_qkv"])
    perm = _evens_odds_perm()
    g["wq1"] = np.ascontiguousarray(wqkv[:, 0:D][:, perm])
    g["wk1"] = np.ascontiguousarray(wqkv[:, D:2 * D][:, perm])
    g["wv1"] = np.ascontiguousarray(wqkv[:, 2 * D:3 * D])
    g["wout1"] = T(inputs["L1_attn_w_out"])
    g["boutc"] = np.ascontiguousarray(
        np.asarray(inputs["L1_attn_b_out"], np.float32).reshape(NCH, 128).T)
    g["wfc"] = T(inputs["L1_ffn_w_fc"])
    g["wproj"] = T(inputs["L1_ffn_w_proj"])
    g["uT"] = T(inputs["L1_ffn_u"])
    g["vmat"] = C(inputs["L1_ffn_v"])
    g["fbT"] = T(inputs["L1_ffn_fast_b"])
    g["gatec"] = np.full((16, 1), float(np.asarray(inputs["L1_ffn_gate"])),
                         dtype=np.float32)

    inv_freq = 1.0 / 10000.0 ** (np.arange(0, HD, 2, dtype=np.float32) / HD)
    t = np.arange(S, dtype=np.float32)
    fr = np.outer(t, inv_freq)
    cos = np.cos(fr).astype(np.float32).T
    sin = np.sin(fr).astype(np.float32).T
    g["t1"] = np.ascontiguousarray(np.vstack([cos, cos, cos, cos]))
    g["t2"] = np.ascontiguousarray(np.vstack([-sin, sin, -sin, sin]))
    sw = np.zeros((128, 128), dtype=np.float32)
    for blk in range(4):
        b = blk * 32
        dst = (b + 32) % 64 + (b // 64) * 64
        for i in range(32):
            sw[dst + i, b + i] = 1.0
    g["pswap"] = sw
    return g



class Ctx:
    pass


def setup_common(P, A, es):
    C = Ctx()
    nc = P.nc
    C.psum = es.enter_context(nc.psum_tensor("psum", [128, 4096], F32))
    C.bank = [C.psum[:, 512 * i:512 * (i + 1)] for i in range(8)]
    C.bank_free = [(None, 0)] * 8
    C.dw = P.sem("dw", es)
    C.dx = P.sem("dx", es)
    C.pe = P.sem("pe", es)
    C.ve = P.sem("ve", es)
    C.se = P.sem("se", es)
    C.ge = P.sem("ge", es)
    C.all_sems = [C.dw, C.dx, C.pe, C.ve, C.se, C.ge]
    C.ones_bf_col = A.take([128, 1], BF16, "ones_col")
    C.ones_f_row = A.take([1, 128], F32, "ones_f")
    C.ones_bf_row = A.take([1, 128], BF16, "ones_bfr")
    C.half_f_row = A.take([1, 128], F32, "half_f")
    C.nhalf_f_row = A.take([1, 128], F32, "nhalf_f")
    C.eps_t = A.take([1, 1], F32, "eps")
    C.ones2_bf = A.take([65, 128], BF16, "ones2")
    P.op("gpsimd", "memset", C.ones_bf_col, 1.0)
    P.op("gpsimd", "memset", C.ones_f_row, 1.0)
    P.op("gpsimd", "memset", C.ones_bf_row, 1.0)
    P.op("gpsimd", "memset", C.half_f_row, 0.5)
    P.op("gpsimd", "memset", C.eps_t, 1e-5)
    P.op("gpsimd", "memset", C.ones2_bf, 1.0)
    P.op("gpsimd", "memset", C.nhalf_f_row, -0.5, inc=C.ge)
    return C


def use_bank(P, C, i):
    s, n = C.bank_free[i]
    if s is not None:
        P.wait("tensor", s, n)


def release_bank(C, i, sem):
    C.bank_free[i] = (sem, sem.n)


def emit_ln(P, A, C, x_sb, xn_out):
    m0 = A.mark()
    xb = A.take([128, NCH, S], BF16, "ln_xb")
    sq = A.take([128, NCH, S], BF16, "ln_sq")
    msq = A.take([1, 512], F32, "ln_msq")
    mneg = A.take([1, S], F32, "ln_mneg")
    rstd = A.take([1, S], F32, "ln_rstd")
    mneg_b = A.take([128, S], F32, "ln_mnegb")
    rstd_b = A.take([128, S], F32, "ln_rstdb")
    scrf = A.take([128, S], F32, "ln_scr")
    for c in range(NCH):
        P.op("vector", "tensor_copy", xb[:, c, :], x_sb[:, c, :], inc=C.ve)
        P.wait("scalar", C.ve)
        P.op("scalar", "activation", sq[:, c, :], xb[:, c, :], AF.Square,
             inc=C.se)
    for tb in range(S // 512):
        bk = 6 + tb % 2
        use_bank(P, C, bk)
        if tb < 2:
            P.wait("tensor", C.ve)
            P.wait("tensor", C.se)
        for c in range(NCH):
            P.mm(C.bank[bk][0:1, :], C.ones_bf_col, xb[:, c, bass.ts(tb, 512)],
                 start=(c == 0), stop=(c == NCH - 1))
        for c in range(NCH):
            P.mm(C.bank[bk][32:33, :], C.ones_bf_col, sq[:, c, bass.ts(tb, 512)],
                 start=(c == 0), stop=(c == NCH - 1),
                 inc=C.pe if c == NCH - 1 else None)
        P.wait("vector", C.pe)
        sl = bass.ts(tb, 512)
        P.op("vector", "tensor_scalar", mneg[0:1, sl], C.bank[bk][0:1, :],
             -1.0 / D, None, ALU.mult)
        P.op("vector", "tensor_scalar", rstd[0:1, sl], C.bank[bk][32:33, :],
             1.0 / D, None, ALU.mult, inc=C.ve)
        release_bank(C, bk, C.ve)
        P.op("vector", "tensor_mul", msq, mneg[0:1, sl], mneg[0:1, sl])
        P.op("vector", "tensor_sub", rstd[0:1, sl], rstd[0:1, sl], msq,
             inc=C.ve)
    P.wait("scalar", C.ve)
    P.op("scalar", "activation", rstd, rstd, AF.Sqrt, bias=C.eps_t, inc=C.se)
    P.wait("vector", C.se)
    P.op("vector", "reciprocal", rstd, rstd, inc=C.ve)
    for tb in range(S // 512):
        bk = 6 + tb % 2
        use_bank(P, C, bk)
        P.wait("tensor", C.ve)
        P.mm(C.bank[bk], C.ones_f_row, mneg[0:1, bass.ts(tb, 512)], inc=C.pe)
        P.wait("vector", C.pe)
        P.op("vector", "tensor_copy", mneg_b[:, bass.ts(tb, 512)], C.bank[bk],
             inc=C.ve)
        release_bank(C, bk, C.ve)
    for tb in range(S // 512):
        bk = 6 + tb % 2
        use_bank(P, C, bk)
        P.mm(C.bank[bk], C.ones_f_row, rstd[0:1, bass.ts(tb, 512)], inc=C.pe)
        P.wait("vector", C.pe)
        P.op("vector", "tensor_copy", rstd_b[:, bass.ts(tb, 512)], C.bank[bk],
             inc=C.ve)
        release_bank(C, bk, C.ve)
    for c in range(NCH):
        P.op("vector", "tensor_add", scrf, x_sb[:, c, :], mneg_b)
        P.op("vector", "tensor_mul", xn_out[:, c, :], scrf, rstd_b, inc=C.ve)
    A.reset(m0)


def emit_ffn(P, A, C, xn, w1_sb, w2_sb, g_hbm, layer, b1c=None, b2c=None):
    m0 = A.mark()
    rr = A.take([128, NI, TQB], BF16, "rr")
    gtile = A.take([128, NCH, TQB], BF16, "gtile")
    scrf = A.take([128, TQB], F32, "ffn_scr")
    f1b = [0, 1]
    f2b = [2, 3, 4, 5]
    gtile_free = C.dx.n
    for tb in range(NFB):
        for c2 in range(NI):
            bk = f1b[c2 % 2]
            use_bank(P, C, bk)
            for ci in range(NCH):
                P.mm(C.bank[bk][:, 0:TQB], w1_sb[:, ci, bass.ts(c2, 128)],
                     xn[:, ci, bass.ts(tb, TQB)],
                     start=(ci == 0), stop=(ci == NCH - 1),
                     inc=C.pe if ci == NCH - 1 else None)
            if layer == 0:
                P.wait("vector", C.pe)
                P.op("vector", "tensor_scalar", rr[:, c2, :], C.bank[bk][:, 0:TQB],
                     b1c[:, c2:c2 + 1], 0.0, ALU.add, ALU.max, inc=C.ve)
                release_bank(C, bk, C.ve)
            else:
                P.wait("vector", C.pe)
                P.op("vector", "tensor_scalar", scrf, C.bank[bk][:, 0:TQB],
                     0.0, None, ALU.max, inc=C.ve)
                release_bank(C, bk, C.ve)
                P.wait("scalar", C.ve)
                P.op("scalar", "activation", rr[:, c2, :], scrf, AF.Square,
                     inc=C.se)
        for oc in range(NCH):
            bk = f2b[oc % 4]
            use_bank(P, C, bk)
            if layer == 1:
                P.wait("tensor", C.se)
            else:
                P.wait("tensor", C.ve)
            for c2 in range(NI):
                P.mm(C.bank[bk][:, 0:TQB], w2_sb[:, c2, bass.ts(oc, 128)],
                     rr[:, c2, :],
                     start=(c2 == 0), stop=(c2 == NI - 1),
                     inc=C.pe if c2 == NI - 1 else None)
            P.wait("scalar", C.pe)
            P.wait("scalar", C.dx, gtile_free)
            if layer == 0:
                P.op("scalar", "activation", gtile[:, oc, :], C.bank[bk][:, 0:TQB],
                     AF.Silu, bias=b2c[:, oc:oc + 1], inc=C.se)
            else:
                P.op("scalar", "activation", gtile[:, oc, :], C.bank[bk][:, 0:TQB],
                     AF.Silu, inc=C.se)
            release_bank(C, bk, C.se)
        P.wait("sync", C.se)
        P.dma("sync", g_hbm[:, :, bass.ts(tb, TQB)].rearrange("c p t -> p c t"),
              gtile, C.dx)
        gtile_free = C.dx.n
    A.reset(m0)


def emit_qk_proj_l0(P, A, C, xn, w_sb, out_bf):
    bks = [0, 1]
    free_i = 0
    for oc in range(NCH):
        for tb in range(S // 512):
            bk = bks[free_i % 2]
            free_i += 1
            use_bank(P, C, bk)
            for ci in range(NCH):
                P.mm(C.bank[bk][:, :], w_sb[:, ci, bass.ts(oc, 128)],
                     xn[:, ci, bass.ts(tb, 512)],
                     start=(ci == 0), stop=(ci == NCH - 1),
                     inc=C.pe if ci == NCH - 1 else None)
            P.wait("vector", C.pe)
            P.op("vector", "tensor_copy", out_bf[:, oc, bass.ts(tb, 512)],
                 C.bank[bk][:, :], inc=C.ve)
            release_bank(C, bk, C.ve)


def emit_qk_proj_rope(P, A, C, xn, w_sb, out_bf, t1, t2, pswap_bf):
    m0 = A.mark()
    qpre = [A.take([128, 512], BF16, "rope_qpre0"),
            A.take([128, 512], BF16, "rope_qpre1")]
    s1 = [A.take([128, 512], F32, "rope_s10"),
          A.take([128, 512], F32, "rope_s11")]
    s2 = [A.take([128, 512], F32, "rope_s20"),
          A.take([128, 512], F32, "rope_s21")]
    qpre_free = [C.pe.n, C.pe.n]
    pi = 0
    for oc in range(NCH):
        for tb in range(S // 512):
            p = pi % 2
            bk = [0, 1][p]
            bk2 = [2, 3][p]
            pi += 1
            use_bank(P, C, bk)
            for ci in range(NCH):
                P.mm(C.bank[bk], w_sb[:, ci, bass.ts(oc, 128)],
                     xn[:, ci, bass.ts(tb, 512)],
                     start=(ci == 0), stop=(ci == NCH - 1),
                     inc=C.pe if ci == NCH - 1 else None)
            P.wait("vector", C.pe)
            P.wait("vector", C.pe, qpre_free[p])
            P.op("vector", "tensor_copy", qpre[p], C.bank[bk], inc=C.ve)
            release_bank(C, bk, C.ve)
            use_bank(P, C, bk2)
            P.wait("tensor", C.ve)
            P.mm(C.bank[bk2], pswap_bf, qpre[p], inc=C.pe)
            qpre_free[p] = C.pe.n
            P.op("vector", "tensor_mul", s1[p], qpre[p], t1[:, bass.ts(tb, 512)])
            P.wait("vector", C.pe)
            P.op("vector", "tensor_mul", s2[p], C.bank[bk2],
                 t2[:, bass.ts(tb, 512)], inc=C.ve)
            release_bank(C, bk2, C.ve)
            P.op("vector", "tensor_add", out_bf[:, oc, bass.ts(tb, 512)],
                 s1[p], s2[p], inc=C.ve)
    A.reset(m0)


def emit_v_proj(P, A, C, xn, wv_sb, v_aug, col_off=0):
    pi = 0
    for tk in range(NTK):
        for ob in range(2):
            bk = [0, 1][pi % 2]
            pi += 1
            use_bank(P, C, bk)
            for ci in range(NCH):
                P.mm(C.bank[bk][:, :], xn[:, ci, bass.ts(tk, 128)],
                     wv_sb[:, ci, col_off + ob * 512:col_off + (ob + 1) * 512],
                     start=(ci == 0), stop=(ci == NCH - 1),
                     inc=C.pe if ci == NCH - 1 else None)
            P.wait("vector", C.pe)
            P.op("vector", "tensor_copy",
                 v_aug[:, tk, ob * 8:(ob + 1) * 8, 0:64],
                 C.bank[bk][:, :].rearrange("p (h d) -> p h d", h=8), inc=C.ve)
            release_bank(C, bk, C.ve)


def emit_mshift(P, A, C, q_bf, hp, msh, kmaxs, qsq_scr, msrow, inv_scale):
    P.wait("scalar", C.pe)
    P.op("scalar", "activation", qsq_scr, q_bf[:, hp, :], AF.Square, inc=C.se)
    for tb in range(4):
        use_bank(P, C, tb)
    P.wait("tensor", C.se)
    for tb in range(4):
        P.mm(C.psum[0:1, bass.ts(tb, 512)], C.ones_bf_col[0:64, :],
             qsq_scr[0:64, bass.ts(tb, 512)])
    for tb in range(4):
        P.mm(C.psum[64:65, bass.ts(tb, 512)], C.ones_bf_col[64:128, :],
             qsq_scr[64:128, bass.ts(tb, 512)],
             inc=C.pe if tb == 3 else None)
    P.wait("vector", C.pe)
    P.op("vector", "tensor_copy", msrow, C.psum[0:65, 0:2048], inc=C.ve)
    for tb in range(4):
        release_bank(C, tb, C.ve)
    P.wait("scalar", C.ve)
    P.op("scalar", "activation", msrow, msrow, AF.Ln)
    P.op("scalar", "activation", msrow, msrow, AF.Exp, scale=0.5, inc=C.se)
    P.wait("vector", C.se)
    P.op("vector", "tensor_scalar", msrow, msrow,
         kmaxs[:, hp:hp + 1], -64.0, ALU.mult, ALU.add)
    P.op("vector", "tensor_scalar", msrow, msrow, 0.0, -float(inv_scale),
         ALU.max, ALU.mult)
    P.op("vector", "tensor_copy", msh, msrow, inc=C.ve)


def emit_kmaxs(P, A, C, k_bf, kmaxs, scr, row, inv_scale):
    m0 = A.mark()
    one = A.take([65, 1], F32, "km_one")
    for hp in range(NCH):
        P.wait("scalar", C.pe)
        P.wait("scalar", C.ve)
        P.op("scalar", "activation", scr, k_bf[:, hp, :], AF.Square, inc=C.se)
        for tb in range(4):
            use_bank(P, C, tb)
        P.wait("tensor", C.se)
        for tb in range(4):
            P.mm(C.psum[0:1, bass.ts(tb, 512)], C.ones_bf_col[0:64, :],
                 scr[0:64, bass.ts(tb, 512)])
        for tb in range(4):
            P.mm(C.psum[64:65, bass.ts(tb, 512)], C.ones_bf_col[64:128, :],
                 scr[64:128, bass.ts(tb, 512)],
                 inc=C.pe if tb == 3 else None)
        P.wait("vector", C.pe)
        P.op("vector", "tensor_copy", row, C.psum[0:65, 0:2048], inc=C.ve)
        for tb in range(4):
            release_bank(C, tb, C.ve)
        P.op("vector", "reduce_max", one, row, mybir.AxisListType.X, inc=C.ve)
        P.wait("scalar", C.ve)
        P.op("scalar", "activation", one, one, AF.Ln)
        P.op("scalar", "activation", one, one, AF.Exp, scale=0.5, inc=C.se)
        P.wait("vector", C.se)
        P.op("vector", "tensor_scalar", kmaxs[:, hp:hp + 1], one,
             1.0 / float(inv_scale), None, ALU.mult, inc=C.ve)
    A.reset(m0)


def emit_attention(P, A, C, q_bf, k_bf, v_aug, scale, consume, shifted=False):
    m0 = A.mark()
    recips = A.take([1, 2, 512], F32, "recips")
    rb = A.take([128, 512], F32, "rb")
    s_blk = A.take([128, 512], F32, "s_blk")
    if shifted:
        inv_scale = 1.0 / scale
        mshift = [A.take([65, S], BF16, "mshift0"),
                  A.take([65, S], BF16, "mshift1")]
        kmaxs = A.take([65, NCH], F32, "kmaxs")
        qsq_scr = A.take([128, S], BF16, "qsq")
        msrow = A.take([65, S], F32, "msrow")
        emit_kmaxs(P, A, C, k_bf, kmaxs, qsq_scr, msrow, inv_scale)
    rtmp = A.take([1, 2, 512], F32, "rtmp")
    recips_bf = A.take([1, 2, 512], BF16, "recips_bf")
    ptab = [A.take([128, 1024], BF16, "ptab0"),
            A.take([128, 1024], BF16, "ptab1")]
    pv_pairs = [(4, 5), (6, 7)]

    def emit_scores(hp, tq, tk):
        p = tk % 2
        bA, bB = 2 * p, 2 * p + 1
        use_bank(P, C, bA)
        use_bank(P, C, bB)
        if shifted:
            msh = mshift[hp % 2]
            P.mm(C.bank[bA], C.ones2_bf[0:1, :], msh[0:1, tq],
                 start=True, stop=False)
            P.mm(C.bank[bB], C.ones2_bf[64:65, :], msh[64:65, tq],
                 start=True, stop=False)
            P.mm(C.bank[bA], k_bf[0:64, hp, bass.ts(tk, 128)],
                 q_bf[0:64, hp, tq], start=False, stop=True, tp=(0, 0))
            P.mm(C.bank[bB], k_bf[64:128, hp, bass.ts(tk, 128)],
                 q_bf[64:128, hp, tq], start=False, stop=True, tp=(64, 0),
                 inc=C.pe)
        else:
            P.mm(C.bank[bA], k_bf[0:64, hp, bass.ts(tk, 128)],
                 q_bf[0:64, hp, tq], start=True, stop=True, tp=(0, 0))
            P.mm(C.bank[bB], k_bf[64:128, hp, bass.ts(tk, 128)],
                 q_bf[64:128, hp, tq], start=True, stop=True, tp=(64, 0),
                 inc=C.pe)

    def emit_exp(tk):
        p = tk % 2
        P.wait("scalar", C.pe)
        P.op("scalar", "activation", ptab[p],
             C.psum[:, 1024 * p:1024 * (p + 1)], AF.Exp, scale=scale,
             inc=C.se)
        release_bank(C, 2 * p, C.se)
        release_bank(C, 2 * p + 1, C.se)
        return C.se.n

    def emit_pv(hp, tk, se_need, bOA, bOB):
        p = tk % 2
        P.wait("tensor", C.se, se_need)
        P.mm(C.bank[bOA][0:65, :], v_aug[:, tk, 2 * hp, :], ptab[p][:, 0:512],
             start=(tk == 0), stop=(tk == NTK - 1))
        P.mm(C.bank[bOB][0:65, :], v_aug[:, tk, 2 * hp + 1, :],
             ptab[p][:, 512:1024],
             start=(tk == 0), stop=(tk == NTK - 1), inc=C.pe)

    pending = [None]

    def norm_part2(hp, tqb, bOA, bOB):
        use_bank(P, C, bOA)
        use_bank(P, C, bOB)
        P.wait("tensor", C.ve)
        P.mm(C.bank[bOA][0:64, :], C.ones_bf_row[0:1, 0:64],
             recips_bf[0:1, 0, :])
        P.mm(C.bank[bOB][0:64, :], C.ones_bf_row[0:1, 0:64],
             recips_bf[0:1, 1, :], inc=C.pe)
        P.wait("vector", C.pe)
        P.op("vector", "tensor_copy", rb[0:64, :], C.bank[bOA][0:64, :])
        P.op("vector", "tensor_copy", rb[64:128, :], C.bank[bOB][0:64, :],
             inc=C.ve)
        release_bank(C, bOA, C.ve)
        release_bank(C, bOB, C.ve)
        P.op("vector", "tensor_mul", s_blk[0:64, :], s_blk[0:64, :],
             rb[0:64, :])
        P.op("vector", "tensor_mul", s_blk[64:128, :], s_blk[64:128, :],
             rb[64:128, :], inc=C.ve)
        consume(hp, tqb, s_blk)

    msh_ready = {}
    if shifted:
        emit_mshift(P, A, C, q_bf, 0, mshift[0], kmaxs, qsq_scr, msrow,
                    inv_scale)
        msh_ready[0] = C.ve.n
    for hp in range(NCH):
        if shifted:
            P.wait("tensor", C.ve, msh_ready[hp])
        for tqb in range(S // 512):
            tq = bass.ts(tqb, 512)
            it = hp * 4 + tqb
            bOA, bOB = pv_pairs[it % 2]
            se_at = {}
            for tk in range(NTK):
                emit_scores(hp, tq, tk)
                if tk == 0:
                    use_bank(P, C, bOA)
                    use_bank(P, C, bOB)
                se_at[tk] = emit_exp(tk)
                if tk > 0:
                    emit_pv(hp, tk - 1, se_at[tk - 1], bOA, bOB)
                if (tk == 3 and shifted and tqb == 3 and hp + 1 < NCH):
                    emit_mshift(P, A, C, q_bf, hp + 1, mshift[(hp + 1) % 2],
                                kmaxs, qsq_scr, msrow, inv_scale)
                    msh_ready[hp + 1] = C.ve.n
                if tk == 10 and pending[0] is not None:
                    norm_part2(*pending[0])
                    pending[0] = None
            emit_pv(hp, NTK - 1, se_at[NTK - 1], bOA, bOB)
            P.wait("vector", C.pe)
            P.op("vector", "tensor_scalar", rtmp[0:1, 0, :],
                 C.bank[bOA][64:65, :], 1e-36, None, ALU.add)
            P.op("vector", "tensor_scalar", rtmp[0:1, 1, :],
                 C.bank[bOB][64:65, :], 1e-36, None, ALU.add)
            P.op("vector", "tensor_copy", s_blk[0:64, :], C.bank[bOA][0:64, :])
            P.op("vector", "tensor_copy", s_blk[64:128, :],
                 C.bank[bOB][0:64, :], inc=C.ve)
            release_bank(C, bOA, C.ve)
            release_bank(C, bOB, C.ve)
            P.op("vector", "reciprocal", recips[0:1, 0, :], rtmp[0:1, 0, :])
            P.op("vector", "reciprocal", recips[0:1, 1, :], rtmp[0:1, 1, :])
            P.op("vector", "tensor_copy", recips_bf[0:1, 0, :],
                 recips[0:1, 0, :])
            P.op("vector", "tensor_copy", recips_bf[0:1, 1, :],
                 recips[0:1, 1, :], inc=C.ve)
            pending[0] = (hp, tqb, bOA, bOB)
    norm_part2(*pending[0])
    pending[0] = None
    A.reset(m0)


def emit_ternarize(P, A, C, w_hbm, n_in, n_out, weff_sb, tag):
    m0 = A.mark()
    nic = n_in // 128
    nob = n_out // 512
    chunk_f = [A.take([128, n_out], F32, f"tn_cf0_{tag}"),
               A.take([128, n_out], F32, f"tn_cf1_{tag}")]
    chunk_abs = [A.take([128, n_out], BF16, f"tn_ca0_{tag}"),
                 A.take([128, n_out], BF16, f"tn_ca1_{tag}")]
    srow = A.take([1, n_out], F32, f"tn_sr_{tag}")
    bh = A.take([128, n_out], F32, f"tn_bh_{tag}")
    bhn = A.take([128, n_out], F32, f"tn_bhn_{tag}")
    sbc = A.take([128, n_out], BF16, f"tn_sbc_{tag}")
    mneg = chunk_abs[0]
    w_r = w_hbm.rearrange("(c p) o -> p c o", p=128)
    cf_free = [C.se.n, C.se.n]
    ca_free = [C.pe.n, C.pe.n]
    for c in range(nic):
        p = c % 2
        P.wait("sync", C.se, cf_free[p])
        P.dma("sync", chunk_f[p], w_r[:, c, :], C.dx)
        P.wait("scalar", C.dx)
        P.wait("scalar", C.pe, ca_free[p])
        P.op("scalar", "activation", chunk_abs[p], chunk_f[p], AF.Abs, inc=C.se)
        cf_free[p] = C.se.n
        P.wait("tensor", C.se)
        for ob in range(nob):
            if c == 0:
                use_bank(P, C, ob)
            P.mm(C.bank[ob][0:1, :], C.ones_bf_col,
                 chunk_abs[p][:, bass.ts(ob, 512)],
                 start=(c == 0), stop=(c == nic - 1),
                 inc=C.pe if ob == nob - 1 else None)
        ca_free[p] = C.pe.n
    P.wait("vector", C.pe)
    for ob in range(nob):
        P.op("vector", "tensor_scalar", srow[0:1, bass.ts(ob, 512)],
             C.bank[ob][0:1, :], 1.0 / n_in, 1e-5, ALU.mult, ALU.max,
             inc=C.ve if ob == nob - 1 else None)
    for ob in range(nob):
        release_bank(C, ob, C.ve)
    for ob in range(nob):
        for row_const, dst in ((C.half_f_row, bh), (C.nhalf_f_row, bhn),
                               (C.ones_f_row, None)):
            bk = ob % 2
            use_bank(P, C, bk)
            P.wait("tensor", C.ve)
            P.mm(C.bank[bk], row_const, srow[0:1, bass.ts(ob, 512)], inc=C.pe)
            P.wait("vector", C.pe)
            out_ap = (sbc if dst is None else dst)[:, bass.ts(ob, 512)]
            P.op("vector", "tensor_copy", out_ap, C.bank[bk], inc=C.ve)
            release_bank(C, bk, C.ve)
    cf2_free = [(C.ve.n, C.ge.n), (C.ve.n, C.ge.n)]
    P.wait("gpsimd", C.ve)
    for c in range(nic):
        p = c % 2
        ve_n, ge_n = cf2_free[p]
        P.wait("sync", C.ve, ve_n)
        P.wait("sync", C.ge, ge_n)
        P.dma("sync", chunk_f[p], w_r[:, c, :], C.dx)
        wch = weff_sb[:, c, :]
        P.wait("vector", C.dx)
        P.op("vector", "tensor_tensor", wch, chunk_f[p], bh, ALU.is_ge)
        P.op("vector", "tensor_tensor", mneg, chunk_f[p], bhn, ALU.is_le)
        P.op("vector", "tensor_sub", wch, wch, mneg)
        P.op("vector", "tensor_mul", wch, wch, sbc, inc=C.ve)
        cf2_free[p] = (C.ve.n, C.ge.n)
    A.reset(m0)


def emit_lowrank_fold(P, A, C, io, weff_fc):
    m0 = A.mark()
    u_sb = A.take([16, INTER], BF16, "lr_u")
    v_sb = A.take([16, D], BF16, "lr_v")
    fb_sb = A.take([16, RANK], F32, "lr_fb")
    gate_c = A.take([16, 1], F32, "lr_gate")
    gfb = A.take([16, RANK], BF16, "lr_gfb")
    at = A.take([16, D], BF16, "lr_at")
    P.dma("gpsimd", u_sb, io.w["uT"], C.dw)
    P.dma("gpsimd", v_sb, io.w["vmat"], C.dw)
    P.dma("gpsimd", fb_sb, io.w["fbT"], C.dw)
    P.dma("gpsimd", gate_c, io.w["gatec"], C.dw)
    P.wait("vector", C.dw)
    P.op("vector", "tensor_scalar", gfb, fb_sb, gate_c, None, ALU.mult, inc=C.ve)
    use_bank(P, C, 0)
    use_bank(P, C, 1)
    P.wait("tensor", C.ve)
    P.mm(C.bank[0][0:16, :], gfb, v_sb[:, 0:512], start=True, stop=True)
    P.mm(C.bank[1][0:16, :], gfb, v_sb[:, 512:1024], start=True, stop=True,
         inc=C.pe)
    P.wait("vector", C.pe)
    P.op("vector", "tensor_copy", at[:, 0:512], C.bank[0][0:16, :])
    P.op("vector", "tensor_copy", at[:, 512:1024], C.bank[1][0:16, :], inc=C.ve)
    release_bank(C, 0, C.ve)
    release_bank(C, 1, C.ve)
    for ci in range(NCH):
        for ob in range(INTER // 512):
            bk = ob % 2
            use_bank(P, C, bk)
            P.wait("tensor", C.ve)
            P.mm(C.bank[bk][:, :], at[:, bass.ts(ci, 128)],
                 u_sb[:, bass.ts(ob, 512)], inc=C.pe)
            P.wait("vector", C.pe)
            P.op("vector", "tensor_add", weff_fc[:, ci, bass.ts(ob, 512)],
                 weff_fc[:, ci, bass.ts(ob, 512)], C.bank[bk][:, :], inc=C.ve)
            release_bank(C, bk, C.ve)
    A.reset(m0)



def emit_layer0(P, A, C, io, x_src, x_dst):
    m_layer = A.mark()
    xn = A.take([128, NCH, S], BF16, "xn0")
    x_src_r = x_src.rearrange("(c p) t -> p c t", p=128)
    x_dst_r = x_dst.rearrange("(c p) t -> c p t", p=128)
    g_hbm = io.g0

    m0 = A.mark()
    x_sb = A.take([128, NCH, S], F32, "x0")
    P.dma("sync", x_sb, x_src_r, C.dx)
    P.wait("vector", C.dx)
    P.wait("vector", C.ge)
    P.wait("tensor", C.ge)
    emit_ln(P, A, C, x_sb, xn)
    A.reset(m0)
    P.barrier(C.all_sems)

    m1 = A.mark()
    w1_sb = A.take([128, NCH, INTER], BF16, "w1")
    w2_sb = A.take([128, NI, D], BF16, "w2")
    b1c = A.take([128, NI], F32, "b1c")
    b2c = A.take([128, NCH], F32, "b2c")
    P.dma("gpsimd", w1_sb, io.w["w1"].rearrange("(c p) o -> p c o", p=128), C.dw)
    P.dma("gpsimd", w2_sb, io.w["w2"].rearrange("(c p) o -> p c o", p=128), C.dw)
    P.dma("sync", b1c, io.w["b1c"], C.dx)
    P.dma("sync", b2c, io.w["b2c"], C.dx)
    P.wait("tensor", C.dw)
    P.wait("vector", C.dx)
    P.wait("scalar", C.dx)
    emit_ffn(P, A, C, xn, w1_sb, w2_sb, g_hbm, 0, b1c, b2c)
    A.reset(m1)
    P.barrier(C.all_sems)

    q_bf = A.take([128, NCH, S], BF16, "q0")
    k_bf = A.take([128, NCH, S], BF16, "k0")
    v_aug = A.take([128, NTK, H, 65], BF16, "v0")
    m2 = A.mark()
    wslot = [A.take([128, NCH, D], BF16, "ws0"),
             A.take([128, NCH, D], BF16, "ws1")]
    P.op("gpsimd", "memset", v_aug[:, :, :, 64:65], 1.0, inc=C.ge)
    P.dma("gpsimd", wslot[0], io.w["wq"].rearrange("(c p) o -> p c o", p=128), C.dw)
    P.wait("tensor", C.dw)
    emit_qk_proj_l0(P, A, C, xn, wslot[0], q_bf)
    P.dma("gpsimd", wslot[1], io.w["wk"].rearrange("(c p) o -> p c o", p=128), C.dw)
    P.wait("tensor", C.dw)
    emit_qk_proj_l0(P, A, C, xn, wslot[1], k_bf)
    P.wait("gpsimd", C.pe)
    P.dma("gpsimd", wslot[0], io.w["wv"].rearrange("(c p) o -> p c o", p=128), C.dw)
    P.wait("tensor", C.dw)
    P.wait("tensor", C.ge)
    emit_v_proj(P, A, C, xn, wslot[0], v_aug)
    A.reset(m2)
    P.barrier(C.all_sems)
    if io.dbg:
        P.dma("sync", io.d_xn, xn, C.dx)
        P.dma("sync", io.d_q, q_bf, C.dx)
        P.dma("sync", io.d_k, k_bf, C.dx)
        P.dma("sync", io.d_v, v_aug, C.dx)

    xst = A.take([128, 2, 512], F32, "xstage")
    gst = A.take([128, 2, 512], BF16, "gstage")
    stage_free = [(None, 0), (None, 0)]
    g_hbm_r = g_hbm
    x_src_cr = x_src.rearrange("(c p) t -> c p t", p=128)

    def consume0(hp, tqb, s_blk):
        sl = (hp * 4 + tqb) % 2
        if io.dbg:
            P.wait("sync", C.ve)
            P.dma("sync", io.d_s[hp, :, bass.ts(tqb, 512)], s_blk, C.dx)
            P.wait("vector", C.dx)
        P.wait("vector", C.dx, stage_issue[(hp, tqb)])
        P.op("vector", "tensor_mul", s_blk, s_blk, gst[:, sl, :])
        P.op("vector", "tensor_add", xst[:, sl, :], xst[:, sl, :], s_blk,
             inc=C.ve)
        P.wait("sync", C.ve)
        P.dma("sync", x_dst_r[hp, :, bass.ts(tqb, 512)], xst[:, sl, :], C.dx)
        stage_free[sl] = (C.dx, C.dx.n)

    stage_issue = {}

    def consume_wrap(hp, tqb, s_blk):
        consume0(hp, tqb, s_blk)
        nxt = hp * 4 + tqb + 1
        if nxt < 32:
            issue_stage(nxt // 4, nxt % 4)

    def issue_stage(hp, tqb):
        sl = (hp * 4 + tqb) % 2
        s, n = stage_free[sl]
        if s is not None:
            P.wait("sync", s, n)
        P.dma("sync", xst[:, sl, :], x_src_cr[hp, :, bass.ts(tqb, 512)], C.dx)
        P.dma("sync", gst[:, sl, :], g_hbm_r[hp, :, bass.ts(tqb, 512)], C.dx)
        stage_issue[(hp, tqb)] = C.dx.n

    issue_stage(0, 0)
    issue_stage(0, 1)
    emit_attention(P, A, C, q_bf, k_bf, v_aug, 1.0 / 32.0, consume_wrap)
    A.reset(m_layer)
    P.barrier(C.all_sems)


def emit_layer1(P, A, C, io, x_src, x_dst):
    m_layer = A.mark()
    xn = A.take([128, NCH, S], BF16, "xn1")
    x_src_r = x_src.rearrange("(c p) t -> p c t", p=128)
    x_src_cr = x_src.rearrange("(c p) t -> c p t", p=128)
    x_dst_r = x_dst.rearrange("(c p) t -> c p t", p=128)
    g_hbm = io.g1

    m0 = A.mark()
    x_sb = A.take([128, NCH, S], F32, "x1")
    P.dma("sync", x_sb, x_src_r, C.dx)
    P.wait("vector", C.dx)
    emit_ln(P, A, C, x_sb, xn)
    A.reset(m0)
    P.barrier(C.all_sems)

    weff_fc = A.take([128, NCH, INTER], BF16, "weff_fc")
    emit_ternarize(P, A, C, io.w["wfc"], D, INTER, weff_fc, "fc")
    P.barrier(C.all_sems)
    emit_lowrank_fold(P, A, C, io, weff_fc)
    P.barrier(C.all_sems)
    weff_pj = A.take([128, NI, D], BF16, "weff_pj")
    emit_ternarize(P, A, C, io.w["wproj"], INTER, D, weff_pj, "pj")
    P.barrier(C.all_sems)
    emit_ffn(P, A, C, xn, weff_fc, weff_pj, g_hbm, 1)
    A.reset(m_layer)
    xn = A.take([128, NCH, S], BF16, "xn1_alias")
    P.barrier(C.all_sems)

    attnout = A.take([128, NCH, S], BF16, "attnout")
    q_bf = A.take([128, NCH, S], BF16, "q1")
    k_bf = A.take([128, NCH, S], BF16, "k1")
    v_aug = A.take([128, NTK, H, 65], BF16, "v1")
    m2 = A.mark()
    t1 = A.take([128, S], BF16, "t1")
    t2 = A.take([128, S], BF16, "t2")
    pswap = A.take([128, 128], BF16, "pswap")
    wslot = A.take([128, NCH, D], BF16, "w1s0")
    P.dma("gpsimd", t1, io.w["t1"], C.dw)
    P.dma("gpsimd", t2, io.w["t2"], C.dw)
    P.dma("gpsimd", pswap, io.w["pswap"], C.dw)
    P.op("gpsimd", "memset", v_aug[:, :, :, 64:65], 1.0, inc=C.ge)
    P.dma("gpsimd", wslot, io.w["wq1"].rearrange("(c p) o -> p c o", p=128),
          C.dw)
    P.wait("tensor", C.dw)
    P.wait("vector", C.dw)
    emit_qk_proj_rope(P, A, C, xn, wslot, q_bf, t1, t2, pswap)
    P.wait("gpsimd", C.pe)
    P.dma("gpsimd", wslot, io.w["wk1"].rearrange("(c p) o -> p c o", p=128),
          C.dw)
    P.wait("tensor", C.dw)
    emit_qk_proj_rope(P, A, C, xn, wslot, k_bf, t1, t2, pswap)
    P.wait("gpsimd", C.pe)
    P.dma("gpsimd", wslot, io.w["wv1"].rearrange("(c p) o -> p c o", p=128),
          C.dw)
    P.wait("tensor", C.dw)
    P.wait("tensor", C.ge)
    emit_v_proj(P, A, C, xn, wslot, v_aug)
    A.reset(m2)
    P.barrier(C.all_sems)

    def consume1(hp, tqb, s_blk):
        P.op("vector", "tensor_copy", attnout[:, hp, bass.ts(tqb, 512)], s_blk,
             inc=C.ve)

    emit_attention(P, A, C, q_bf, k_bf, v_aug, 1.0 / 8.0, consume1, shifted=True)
    P.barrier(C.all_sems)

    m3 = A.mark()
    wout = A.take([128, NCH, D], BF16, "wout")
    boutc = A.take([128, NCH], F32, "boutc")
    xst = A.take([128, 2, 512], F32, "xst1")
    gst = A.take([128, 2, 512], BF16, "gst1")
    sblk = A.take([128, 512], F32, "sblk1")
    P.dma("gpsimd", wout, io.w["wout1"].rearrange("(c p) o -> p c o", p=128), C.dw)
    P.dma("sync", boutc, io.w["boutc"], C.dx)
    stage_free = [(None, 0), (None, 0)]
    stage_issue = {}

    def issue_stage(i):
        oc, tb = i // 4, i % 4
        sl = i % 2
        s, n = stage_free[sl]
        if s is not None:
            P.wait("sync", s, n)
        P.dma("sync", xst[:, sl, :], x_src_cr[oc, :, bass.ts(tb, 512)], C.dx)
        P.dma("sync", gst[:, sl, :], g_hbm[oc, :, bass.ts(tb, 512)], C.dx)
        stage_issue[i] = C.dx.n

    issue_stage(0)
    issue_stage(1)
    P.wait("tensor", C.dw)
    pi = 0
    for oc in range(NCH):
        for tb in range(S // 512):
            i = oc * 4 + tb
            sl = i % 2
            bk = [0, 1][pi % 2]
            pi += 1
            use_bank(P, C, bk)
            for ci in range(NCH):
                P.mm(C.bank[bk][:, :], wout[:, ci, bass.ts(oc, 128)],
                     attnout[:, ci, bass.ts(tb, 512)],
                     start=(ci == 0), stop=(ci == NCH - 1),
                     inc=C.pe if ci == NCH - 1 else None)
            P.wait("vector", C.pe)
            P.wait("vector", C.dx, stage_issue[i])
            P.op("vector", "tensor_scalar", sblk, C.bank[bk][:, :],
                 boutc[:, oc:oc + 1], None, ALU.add, inc=C.ve)
            release_bank(C, bk, C.ve)
            P.op("vector", "tensor_mul", sblk, sblk, gst[:, sl, :])
            P.op("vector", "tensor_add", xst[:, sl, :], xst[:, sl, :], sblk,
                 inc=C.ve)
            P.wait("sync", C.ve)
            P.dma("sync", x_dst_r[oc, :, bass.ts(tb, 512)], xst[:, sl, :], C.dx)
            stage_free[sl] = (C.dx, C.dx.n)
            if i + 2 < 32:
                issue_stage(i + 2)
    A.reset(m_layer)



W_L0 = ["wq", "wk", "wv", "w1", "w2", "b1c", "b2c"]
W_L1 = ["wq1", "wk1", "wv1", "wout1", "boutc", "wfc", "wproj", "uT", "vmat",
        "fbT", "gatec", "t1", "t2", "pswap"]


def build_program(layers=(0, 1), dbg=False):
    from contextlib import ExitStack
    nc = bass.Bass("TRN2", target_bir_lowering=False)
    io = Ctx()
    io.dbg = dbg
    io.x_in = nc.dram_tensor("x_in", [D, S], F32, kind="ExternalInput").ap()
    io.x_out = nc.dram_tensor("x_out", [D, S], F32, kind="ExternalOutput").ap()
    if layers == (0, 1):
        io.x_mid = nc.dram_tensor("x_mid", [D, S], F32, kind="Internal").ap()
    io.w = {}
    names = (W_L0 if 0 in layers else []) + (W_L1 if 1 in layers else [])
    shapes = dict(wq=(D, D), wk=(D, D), wv=(D, D), w1=(D, INTER), w2=(INTER, D),
                  b1c=(128, NI), b2c=(128, NCH), wq1=(D, D), wk1=(D, D),
                  wv1=(D, D), wout1=(D, D), boutc=(128, NCH), wfc=(D, INTER),
                  wproj=(INTER, D), uT=(RANK, INTER), vmat=(RANK, D),
                  fbT=(RANK, RANK), gatec=(16, 1), t1=(128, S), t2=(128, S),
                  pswap=(128, 128))
    for n in names:
        io.w[n] = nc.dram_tensor(n, list(shapes[n]), F32,
                                 kind="ExternalInput").ap()
    gkind = "ExternalOutput" if dbg else "Internal"
    if 0 in layers:
        io.g0 = nc.dram_tensor("g0", [NCH, 128, S], BF16, kind=gkind).ap()
    if 1 in layers:
        io.g1 = nc.dram_tensor("g1", [NCH, 128, S], BF16, kind=gkind).ap()
    if dbg:
        io.d_xn = nc.dram_tensor("d_xn", [128, NCH, S], BF16,
                                 kind="ExternalOutput").ap()
        io.d_q = nc.dram_tensor("d_q", [128, NCH, S], BF16,
                                kind="ExternalOutput").ap()
        io.d_k = nc.dram_tensor("d_k", [128, NCH, S], BF16,
                                kind="ExternalOutput").ap()
        io.d_v = nc.dram_tensor("d_v", [128, NTK, H, 65], BF16,
                                kind="ExternalOutput").ap()
        io.d_s = nc.dram_tensor("d_s", [NCH, 128, S], F32,
                                kind="ExternalOutput").ap()

    es = ExitStack()
    P = Prog(nc)
    A = Arena(nc)
    with nc.Block() as block:
        C = setup_common(P, A, es)
        if layers == (0, 1):
            emit_layer0(P, A, C, io, io.x_in, io.x_mid)
            emit_layer1(P, A, C, io, io.x_mid, io.x_out)
        elif layers == (0,):
            emit_layer0(P, A, C, io, io.x_in, io.x_out)
        elif layers == (1,):
            emit_layer1(P, A, C, io, io.x_in, io.x_out)
        P.wait("sync", C.dx)
        P.wait("sync", C.dw)
        P.op("sync", "nop")
        P.simulate()
        P.replay(block)
    es.close()
    return nc


def kernel(**inputs):
    g = host_prep(inputs)
    x = np.asarray(inputs["L0_attn_x"], dtype=np.float32)
    nc = build_program((0, 1))
    in_maps = []
    for b in range(8):
        m = dict(g)
        m["x_in"] = np.ascontiguousarray(x[b].T)
        in_maps.append(m)
    res = run_bass_kernel_spmd(nc, in_maps, core_ids=list(range(8)))
    out = np.stack([res.results[b]["x_out"].T for b in range(8)], axis=0)
    return out.astype(np.float32)
